# version 08
# baseline (speedup 1.0000x reference)
import numpy as np
import ml_dtypes
from contextlib import ExitStack

import concourse.bass as bass
import concourse.mybir as mybir
from concourse.bass_utils import run_bass_kernel_spmd

F32 = mybir.dt.float32
BF16 = mybir.dt.bfloat16
AF = mybir.ActivationFunctionType
ALU = mybir.AluOpType
AX = mybir.AxisListType

NCORES = 8
T = 2048
D = 1024
NDC = 8
HEADS = 16
E = 64
ST = 512
NST = T // ST

AY_W = 18496
AX_W = 16384
AW_W = 12288
AS_W = 4096

STOP_AFTER = None
MTAP = False
LAST_DBG = None
LAST_BUILD = None

ENGINES = ("sync", "gpsimd", "tensor", "vector", "scalar")


class Build:
    def __init__(self, nc):
        self.nc = nc
        self.steps = []
        self.phase = "init"
        self.sem = {}
        self.cnt = {}
        self.observed = {e: {} for e in ENGINES}
        self.owner = {}

    def add_sem(self, name, handle):
        self.sem[name] = handle
        self.cnt[name] = 0

    def inc(self, name, n=1):
        self.cnt[name] += n
        return self.cnt[name]

    def on(self, engine, fn, nreal=1):
        self.steps.append((engine, fn, nreal, self.phase))

    def wait(self, engine, sem_name, value=None):
        if value is None:
            value = self.cnt[sem_name]
        if value <= 0:
            return
        ob = self.observed[engine]
        if ob.get(sem_name, 0) >= value:
            return
        ob[sem_name] = value
        h = self.sem[sem_name]
        self.on(engine, lambda eng, h=h, value=value: eng.wait_ge(h, value))

    def use(self, engine, key):
        for s, c in self.owner.get(key, ()):
            self.wait(engine, s, c)

    def consumed(self, key, sem_name, cnt=None):
        if cnt is None:
            cnt = self.cnt[sem_name]
        self.owner[key] = [(sem_name, cnt)]

    def also_consumed(self, key, sem_name, cnt=None):
        if cnt is None:
            cnt = self.cnt[sem_name]
        self.owner.setdefault(key, []).append((sem_name, cnt))


def tap(b, ap):
    i = b.dbg_i
    b.dbg_i += 1
    eng_q0 = "sync" if str(ap.dtype) == "dt.float32" else "gpsimd"
    for s in ("pe", "dve", "act"):
        b.wait(eng_q0, s, b.cnt[s])
    eng_q = "sync" if str(ap.dtype) == "dt.float32" else "gpsimd"
    c = emit(b, eng_q, "dtap", 16)(
        lambda eng, ap=ap, i=i: eng.dma_start(
            out=b.dbg_t.ap()[i * 128:i * 128 + ap.shape[0], 0:ap.shape[1]],
            in_=ap))
    for e in ("tensor", "vector", "scalar"):
        b.wait(e, "dtap", c)


def emit(b, engine, sem_name=None, n=1):
    def deco(fn):
        if sem_name is None:
            b.on(engine, fn)
            return None
        h = b.sem[sem_name]
        val = b.inc(sem_name, n)

        def wrapped(eng, fn=fn, h=h, n=n):
            fn(eng).then_inc(h, n)
        b.on(engine, wrapped)
        return val
    return deco



def build_program():
    nc = bass.Bass("TRN2", target_bir_lowering=False, enable_partition_id=True)

    ti = {}

    def ein(name, shape, dtype=F32):
        ti[name] = nc.dram_tensor(name, shape, dtype, kind="ExternalInput")

    ein("x_own", [T, D])
    ein("aw_qkvT", [D, 3 * D], BF16)
    ein("ident_f", [128, 128])
    ein("ident_b", [128, 128], BF16)
    ein("ones_col", [128, 1])
    ein("ones_row", [1, 128])
    ein("blk2", [128, 2])
    ein("blk2T", [2, 128])
    ein("cvec", [128, 16])
    ein("f_wqkv", [6 * 1024, 3 * 1024], BF16)
    ein("f_wo", [6 * 1024, 1024], BF16)
    ein("f_w1t", [128, 6 * 22 * 2048], BF16)
    ein("f_w2t", [128, 6 * 8 * 2816], BF16)
    ein("rope", [4 * 128, 2312], BF16)
    ein("masks", [4 * 128, 512], BF16)
    ein("rsign", [128, 128], BF16)
    ein("qs2", [2, 6 * 128])
    ein("ks2", [2, 6 * 128])
    ein("b1c", [128, 6 * 44])
    ein("b2c", [128, 6 * 8])
    ein("lngc", [128, 6 * 8])
    ein("lnbc", [128, 6 * 8])
    ein("hvalid", [128, 1])
    ein("ones_col_bf", [128, 1], BF16)
    ein("m_win", [1024, 4096], BF16)
    ein("m_wxp", [2048, 96], BF16)
    ein("m_wdt", [64, 2048], BF16)
    ein("m_wout", [2048, 1024], BF16)
    ein("m_alog", [128, 32])
    ein("m_dparam", [128, 16])
    ein("m_lng", [128, 8])
    ein("m_lnb", [128, 8])
    ein("m_convw", [128, 64])
    ein("m_convb", [128, 16])
    ein("m_bdt", [128, 16])
    ein("m_sel", [16, 2048], BF16)

    out_t = nc.dram_tensor("out", [T, D], F32, kind="ExternalOutput")
    dbg_t = nc.dram_tensor("dbg", [16 * 128, 512], F32, kind="ExternalOutput")

    dram = {
        "ar_in": nc.dram_tensor("ar_in", [128, 520], F32),
        "ar_out": nc.dram_tensor("ar_out", [128, 520], F32),
        "hx_in": nc.dram_tensor("hx_in", [1024, 257], F32),
        "hx_out": nc.dram_tensor("hx_out", [8192, 257], F32),
        "hxs_in": nc.dram_tensor("hxs_in", [1024, 8], F32),
        "hxs_out": nc.dram_tensor("hxs_out", [8192, 8], F32),
        "x_spill": nc.dram_tensor("x_spill", [1024, 2048], F32),
        "a2a_dl_in": nc.dram_tensor("a2a_dl_in", [2048, 2048], F32),
        "a2a_dl_out": nc.dram_tensor("a2a_dl_out", [2048, 2048], F32),
        "a2a_du_in": nc.dram_tensor("a2a_du_in", [2048, 2048], BF16),
        "a2a_du_out": nc.dram_tensor("a2a_du_out", [2048, 2048], BF16),
        "bc_in": nc.dram_tensor("bc_in", [32, 2048], F32),
        "bc_out": nc.dram_tensor("bc_out", [256, 2048], F32),
        "a2a_y_in": nc.dram_tensor("a2a_y_in", [2048, 2048], F32),
        "u_spill": nc.dram_tensor("u_spill", [2048, 2048], BF16),
        "a2a_y_out": nc.dram_tensor("a2a_y_out", [2048, 2048], F32),
    }

    global LAST_BUILD
    b = Build(nc)
    LAST_BUILD = b
    with ExitStack() as ctx:
        sb = {}
        for name, shape, dt in (
            ("AY", [128, AY_W], F32), ("AX", [128, AX_W], F32),
            ("AW", [128, AW_W], F32), ("AS", [128, AS_W], F32),
            ("c_identf", [128, 128], F32), ("c_identb", [128, 128], BF16),
            ("c_ones_col", [128, 1], F32), ("c_ones_row", [1, 128], F32),
            ("c_blk2", [128, 2], F32), ("c_blk2T", [2, 128], F32),
            ("c_cvec", [128, 16], F32),
            ("c_rsign", [128, 128], BF16),
            ("c_qs2", [2, 128], F32), ("c_ks2", [2, 128], F32),
            ("c_b1", [128, 44], F32), ("c_b2", [128, 8], F32),
            ("c_lng", [128, 8], F32), ("c_lnb", [128, 8], F32),
            ("c_masks", [128, 2048], BF16),
            ("c_hv", [128, 1], F32),
            ("c_ones_col_bf", [128, 1], BF16),
        ):
            sb[name] = ctx.enter_context(nc.sbuf_tensor(name, shape, dt)).ap()
        PS = [ctx.enter_context(nc.psum_tensor(f"ps{i}", [128, 512], F32)).ap()
              for i in range(8)]
        for s in ("dc0", "dxa", "dxb", "dwa", "dwb", "dsa", "dsb",
                  "arx", "ary", "dtap", "hxa", "hxb", "rta", "rtb",
                  "w1a", "w1b", "w2a", "w2b", "w1c", "dwc", "dwd", "spl",
                  "lca", "lcb", "lcc", "lcd", "msa", "msb", "mla", "mlb",
                  "mwa", "mwb", "mlc", "mua", "mub",
                  "pe", "dve", "act", "cc", "pool"):
            b.add_sem(s, ctx.enter_context(nc.semaphore(s)))

        b.dbg_t = dbg_t
        b.dbg_i = 0
        build_steps(b, ti, out_t, dram, sb, PS)

        with nc.Block() as block:
            for ename in ENGINES:
                def mk(ename):
                    def body(eng):
                        for e, fn, _n, ph in b.steps:
                            if e == ename:
                                with nc.named_scope(ph):
                                    fn(eng)
                    return body
                getattr(block, ename)(mk(ename))
    return nc


PHASE_MARKS = []


def build_steps(b, ti, out_t, dram, sb, PS):
    nc = b.nc
    PHASE_MARKS.clear()
    def mark(name):
        PHASE_MARKS.append((name, dict(b.cnt)))
        b.phase = name
    AY, AXa, AW, AS = sb["AY"], sb["AX"], sb["AW"], sb["AS"]
    AYb, AWb, ASb = AY.bitcast(BF16), AW.bitcast(BF16), AS.bitcast(BF16)

    def xr(dc):
        return AXa[:, dc * T:(dc + 1) * T]

    for cname, iname in (("c_identf", "ident_f"), ("c_identb", "ident_b"),
                         ("c_ones_col", "ones_col"), ("c_ones_row", "ones_row"),
                         ("c_blk2", "blk2"), ("c_blk2T", "blk2T"),
                         ("c_cvec", "cvec"), ("c_rsign", "rsign"),
                         ("c_hv", "hvalid"),
                         ("c_ones_col_bf", "ones_col_bf")):
        emit(b, "sync", "dc0", 16)(
            lambda eng, cname=cname, iname=iname: eng.dma_start(
                out=sb[cname], in_=ti[iname].ap()))
    for t in range(4):
        emit(b, "sync", "dc0", 16)(
            lambda eng, t=t: eng.dma_start(
                out=sb["c_masks"][:, t * 512:(t + 1) * 512],
                in_=ti["masks"].ap()[t * 128:(t + 1) * 128, :]))
    consts_loaded = b.cnt["dc0"]
    b.wait("tensor", "dc0", consts_loaded)
    b.wait("vector", "dc0", consts_loaded)
    b.wait("scalar", "dc0", consts_loaded)

    regs = {}

    def _(eng):
        pid = eng.partition_id()
        prev = (pid + 7) % 8
        regs["r257"] = prev * (1024 * 257)
        regs["r8"] = prev * (1024 * 8)
    b.on("sync", _, nreal=0)
    b.regs = regs

    xin = ti["x_own"].ap()
    for tc in range(T // 128):
        buf = AS[:, (tc % 2) * 1024:(tc % 2) * 1024 + 1024]
        sem = "dxa" if tc % 2 == 0 else "dxb"
        b.use("sync", ("xin", tc % 2))
        ld = emit(b, "sync", sem, 16)(
            lambda eng, buf=buf, tc=tc: eng.dma_start(
                out=buf, in_=xin[tc * 128:(tc + 1) * 128, :]))
        b.wait("tensor", sem, ld)
        for grp in range(2):
            bk = grp
            b.use("tensor", ("bank", bk))
            for j in range(4):
                dc = grp * 4 + j
                emit(b, "tensor", "pe")(
                    lambda eng, bk=bk, j=j, buf=buf, dc=dc: eng.matmul(
                        PS[bk][:, j * 128:(j + 1) * 128],
                        buf[:, dc * 128:(dc + 1) * 128], sb["c_identf"],
                        start=(j == 0), stop=(j == 3), is_transpose=True))
            pc = b.cnt["pe"]
            b.wait("scalar", "pe", pc)
            for j in range(4):
                dc = grp * 4 + j
                emit(b, "scalar", "act")(
                    lambda eng, bk=bk, j=j, dc=dc, tc=tc: eng.copy(
                        out=xr(dc)[:, tc * 128:(tc + 1) * 128],
                        in_=PS[bk][:, j * 128:(j + 1) * 128]))
            b.consumed(("bank", bk), "act")
        b.consumed(("xin", tc % 2), "pe")
    x_dep = ("act", b.cnt["act"])

    pipeA = AS[:, 1536:2048]
    pipeB = AS[:, 2048:2560]
    sqA = AS[:, 2560:3072]
    sqB = AS[:, 3072:3584]
    mu_t = AS[0:1, 3584:4096]
    var_t = AS[0:1, 1536:2048]
    sd_t = sqA[0:1, :]

    def emit_ln(x_tiles, t0, n, out_cb, x_dep, stats=None, bf_in=False):
        if stats is None:
            l_mu, l_var, l_sd = mu_t, var_t, sd_t
        else:
            l_mu, l_var, l_sd = stats
        ones_l = sb["c_ones_col_bf"] if bf_in else sb["c_ones_col"]
        ps_sum = PS[2]
        ps_sq = PS[3]
        b.use("tensor", ("bank", 2))
        b.use("tensor", ("bank", 3))
        b.wait("tensor", *x_dep)
        sq_mm_at = {}
        for dc in range(NDC):
            xap = x_tiles(dc)[:, t0:t0 + n]
            emit(b, "tensor", "pe")(
                lambda eng, xap=xap, dc=dc: eng.matmul(
                    ps_sum[0:1, 0:n], ones_l, xap,
                    start=(dc == 0), stop=(dc == NDC - 1)))
            sq = sqA if dc % 2 == 0 else sqB
            if dc >= 2:
                b.wait("scalar", "pe", sq_mm_at[dc - 2])
            b.use("scalar", ("sq", dc % 2))
            b.wait("scalar", *x_dep)
            ac = emit(b, "scalar", "act")(
                lambda eng, xap=xap, sq=sq: eng.activation(
                    out=sq[:, 0:n], in_=xap, func=AF.Square))
            b.wait("tensor", "act", ac)
            emit(b, "tensor", "pe")(
                lambda eng, sq=sq, dc=dc: eng.matmul(
                    ps_sq[0:1, 0:n], sb["c_ones_col"], sq[:, 0:n],
                    start=(dc == 0), stop=(dc == NDC - 1)))
            sq_mm_at[dc] = b.cnt["pe"]
        if n < 128:
            emit(b, "tensor", "pe")(
                lambda eng: eng.matmul(PS[4][0:1, 0:512], sb["c_ones_col"],
                                       sqA[:, 0:512], start=True, stop=True))
        mm_done = b.cnt["pe"]
        b.wait("vector", "pe", mm_done)
        b.use("vector", "ln_stats")
        emit(b, "vector", "dve")(
            lambda eng: eng.tensor_scalar(
                out=l_mu[:, 0:n], in0=ps_sum[0:1, 0:n], scalar1=1.0 / D,
                scalar2=None, op0=ALU.mult))
        dve_mu = b.cnt["dve"]
        b.wait("scalar", "dve", dve_mu)
        b.use("scalar", "pipeA")
        ac = emit(b, "scalar", "act")(
            lambda eng: eng.activation(out=l_var[:, 0:n], in_=l_mu[:, 0:n],
                                       func=AF.Square))
        b.wait("vector", "act", ac)
        emit(b, "vector", "dve")(
            lambda eng: eng.tensor_scalar(
                out=l_var[:, 0:n], in0=l_var[:, 0:n], scalar1=-1.0,
                scalar2=None, op0=ALU.mult))
        emit(b, "vector", "dve")(
            lambda eng: eng.scalar_tensor_tensor(
                out=var_t[:, 0:n], in0=ps_sq[0:1, 0:n], scalar=1.0 / D,
                in1=l_var[:, 0:n], op0=ALU.mult, op1=ALU.add))
        b.consumed(("bank", 2), "dve")
        b.consumed(("bank", 3), "dve")
        b.wait("scalar", "dve", b.cnt["dve"])
        ac = emit(b, "scalar", "act")(
            lambda eng: eng.activation(out=l_sd[:, 0:n], in_=l_var[:, 0:n],
                                       func=AF.Sqrt, bias=sb["c_cvec"][0:1, 1:2]))
        b.wait("vector", "act", ac)
        dve_r = emit(b, "vector", "dve")(
            lambda eng: eng.reciprocal(out=l_var[:, 0:n], in_=l_sd[:, 0:n]))
        b.use("tensor", ("bank", 4))
        b.use("tensor", ("bank", 5))
        b.wait("tensor", "dve", dve_r)
        nw = max(n, 512)
        emit(b, "tensor", "pe")(
            lambda eng: eng.matmul(PS[4][:, 0:nw], sb["c_ones_row"],
                                   l_mu[:, 0:nw], start=True, stop=True))
        rep_done = emit(b, "tensor", "pe")(
            lambda eng: eng.matmul(PS[5][:, 0:nw], sb["c_ones_row"],
                                   l_var[:, 0:nw], start=True, stop=True))
        b.consumed("ln_stats", "pe", rep_done)
        b.consumed("pipeA", "pe", rep_done)
        npd = n if n >= 128 else 64
        if n < 128:
            mu_rep = AS[:, 3712:3712 + npd]
            sd_rep = AS[:, 3776:3776 + npd]
            b.wait("scalar", "pe", rep_done)
            emit(b, "scalar", "act")(
                lambda eng: eng.copy(out=mu_rep, in_=PS[4][:, 0:npd]))
            ev_c = emit(b, "scalar", "act")(
                lambda eng: eng.copy(out=sd_rep, in_=PS[5][:, 0:npd]))
            b.wait("vector", "act", ev_c)
            rep4, rep5 = mu_rep, sd_rep
        else:
            rep4, rep5 = PS[4][:, 0:n], PS[5][:, 0:n]
        b.wait("vector", "pe", rep_done)
        b.wait("vector", *x_dep)
        for dc in range(NDC):
            xap = x_tiles(dc)[:, t0:t0 + npd]
            sub = sqA if dc % 2 == 0 else sqB
            b.wait("vector", "pe", mm_done)
            emit(b, "vector", "dve")(
                lambda eng, xap=xap, sub=sub: eng.tensor_tensor(
                    out=sub[:, 0:npd], in0=xap, in1=rep4,
                    op=ALU.subtract))
            emit(b, "vector", "dve")(
                lambda eng, sub=sub: eng.tensor_tensor(
                    out=sub[:, 0:npd], in0=sub[:, 0:npd], in1=rep5,
                    op=ALU.mult))
            out_cb(dc, sub[:, 0:npd])
            b.consumed(("sq", dc % 2), "dve")
        b.consumed(("bank", 4), "dve")
        b.consumed(("bank", 5), "dve")

    def y0bf(dc):
        return AYb[:, dc * T:(dc + 1) * T]

    if STOP_AFTER == "x0":
        store_x(b, sb, xr, out_t, AS, PS)
        return

    y0_done = []
    for st in range(NST):
        def cb(dc, sub, st=st):
            emit(b, "vector", "dve")(
                lambda eng, dc=dc, sub=sub: eng.tensor_copy(
                    out=y0bf(dc)[:, st * ST:(st + 1) * ST], in_=sub))
        emit_ln(xr, st * ST, ST, cb, x_dep)
        y0_done.append(b.cnt["dve"])

    if STOP_AFTER == "y0":
        for dc in range(NDC):
            emit(b, "vector", "dve")(
                lambda eng, dc=dc: eng.tensor_copy(out=xr(dc), in_=y0bf(dc)))
        store_x(b, sb, xr, out_t, AS, PS)
        return

    mark("p0_ln")
    attn_a(b, ti, sb, PS, AYb, AS, ASb, AWb, y0bf, xr, dram, y0_done, out_t)
    mark("attn_a")
    if STOP_AFTER == "kv":
        return

    if STOP_AFTER == "x1":
        store_x(b, sb, xr, out_t, AS, PS)
        return

    ffn6_call(b, ti, sb, PS, AY, AXa, AW, AS, dram, xr, emit_ln, out_t)
    mark("ffn0")
    if STOP_AFTER is not None and (STOP_AFTER in ("x2", "fg0", "fg1")
                                   or STOP_AFTER.startswith("ysub")):
        store_x(b, sb, xr, out_t, AS, PS)
        return

    def exchange_x(xh_ap):
        hxs_in, hxs_out = dram["hxs_in"].ap(), dram["hxs_out"].ap()
        b.use("sync", ("hxm",))
        b.wait("sync", "dve", b.cnt["dve"])
        for dc in range(NDC):
            emit(b, "sync", "hxa", 16)(
                lambda eng, dc=dc: eng.dma_start(
                    out=hxs_in[dc * 128:(dc + 1) * 128, :],
                    in_=xr(dc)[:, T - 8:T]))
        b.wait("gpsimd", "hxa", b.cnt["hxa"])
        b.wait("gpsimd", "cc", b.cnt["cc"])
        cc_c = emit(b, "gpsimd", "cc")(
            lambda eng: eng.collective_compute(
                "AllGather", ALU.bypass, replica_groups=[list(range(NCORES))],
                ins=[hxs_in.opt()], outs=[hxs_out.opt()]))
        b.wait("sync", "cc", cc_c)
        for dc in range(NDC):
            def f(eng, dc=dc):
                src_ap = bass.AP(hxs_out.tensor, b.regs["r8"] + dc * 128 * 8,
                                 [[8, 128], [1, 8]])
                return eng.dma_start(out=xh_ap[:, dc * 8:(dc + 1) * 8],
                                     in_=src_ap)
            emit(b, "sync", "hxa", 16)(f)
        rb = b.cnt["hxa"]
        b.wait("vector", "hxa", rb)
        for dc in range(NDC):
            emit(b, "vector", "dve")(
                lambda eng, dc=dc: eng.tensor_scalar(
                    out=xh_ap[:, dc * 8:(dc + 1) * 8],
                    in0=xh_ap[:, dc * 8:(dc + 1) * 8],
                    scalar1=sb["c_hv"][:, 0:1], scalar2=None, op0=ALU.mult))
        b.consumed(("hxm",), "dve", b.cnt["dve"])
        return b.cnt["dve"]

    mamba_block(b, ti, sb, PS, AY, AXa, AW, AS, dram, xr, emit_ln, exchange_x)
    mark("mamba")
    if STOP_AFTER == "x3":
        store_x(b, sb, xr, out_t, AS, PS)
        return

    ffn6_call(b, ti, sb, PS, AY, AXa, AW, AS, dram, xr, emit_ln, out_t)
    mark("ffn1")

    store_x(b, sb, xr, out_t, AS, PS)
    mark("store")


def store_x(b, sb, xr, out_t, AS, PS):
    oap = out_t.ap()
    xw = ("dve", b.cnt["dve"])
    for tc in range(T // 128):
        buf = AS[:, (tc % 2) * 1024:(tc % 2) * 1024 + 1024]
        b.use("scalar", ("xout", tc % 2))
        b.wait("tensor", *xw)
        for grp in range(2):
            bk = 6 + grp
            b.use("tensor", ("bank", bk))
            for j in range(4):
                dc = grp * 4 + j
                emit(b, "tensor", "pe")(
                    lambda eng, bk=bk, j=j, dc=dc, tc=tc: eng.matmul(
                        PS[bk][:, j * 128:(j + 1) * 128],
                        xr(dc)[:, tc * 128:(tc + 1) * 128], sb["c_identf"],
                        start=(j == 0), stop=(j == 3), is_transpose=True))
            pc = b.cnt["pe"]
            b.wait("scalar", "pe", pc)
            emit(b, "scalar", "act")(
                lambda eng, bk=bk, buf=buf, grp=grp: eng.copy(
                    out=buf[:, grp * 512:(grp + 1) * 512], in_=PS[bk][:, 0:512]))
            b.consumed(("bank", bk), "act")
        b.wait("sync", "act", b.cnt["act"])
        sem = "dsa" if tc % 2 == 0 else "dsb"
        sc = emit(b, "sync", sem, 16)(
            lambda eng, buf=buf, tc=tc: eng.dma_start(
                out=oap[tc * 128:(tc + 1) * 128, :], in_=buf))
        b.consumed(("xout", tc % 2), sem, sc)
    b.wait("sync", "dsa", b.cnt["dsa"])
    b.wait("sync", "dsb", b.cnt["dsb"])


def attn_a(b, ti, sb, PS, AYb, AS, ASb, AWb, y0bf, xr, dram, y0_done,
           *out_t_global):
    nc = b.nc
    wsrc = ti["aw_qkvT"].ap()

    def wslot(slot, kc):
        return AWb[:, slot * 8192 + kc * 1024: slot * 8192 + (kc + 1) * 1024]

    def load_third(slot, which):
        sem = "dwa" if slot == 0 else "dwb"
        b.use("gpsimd", ("wslot", slot))
        for kc in range(NDC):
            emit(b, "gpsimd", sem, 16)(
                lambda eng, slot=slot, kc=kc, which=which: eng.dma_start(
                    out=wslot(slot, kc),
                    in_=wsrc[kc * 128:(kc + 1) * 128,
                             which * 1024:(which + 1) * 1024]))
        return sem, b.cnt[sem]

    k_loaded = load_third(0, 1)
    v_loaded = load_third(1, 2)

    kst = lambda oc: AYb[:, 16384 + oc * 512: 16384 + (oc + 1) * 512]
    vst = lambda oc: AYb[:, 20480 + oc * 512: 20480 + (oc + 1) * 512]
    ktm = lambda tcl: AYb[:, 24576 + tcl * 1024: 24576 + (tcl + 1) * 1024]
    vtm = lambda tcl: AYb[:, 28672 + tcl * 1024: 28672 + (tcl + 1) * 1024]

    kvbd = lambda pr: ASb[:, pr * 128:(pr + 1) * 128]
    q2s = lambda i: ASb[:, 1024 + i * 512: 1024 + (i + 1) * 512]
    ks2_bf = ASb[:, 2048:2064]
    pipeA = AS[:, 1536:2048]
    pipeB = AS[:, 2048:2560]
    kv_acc = AS[:, 1024:1536]
    ksum_st = AS[:, 3584:3616]
    ksum_acc = AS[:, 3616:3624]

    ks2_bf = ASb[:, 7248:7280]

    pipe_calls = [0]

    def qk_pipeline(ps_in, n, out_bf):
        dbgme = (STOP_AFTER == "tm" and pipe_calls[0] == 0)
        pipe_calls[0] += 1
        b.use("scalar", "pipeA")
        b.use("scalar", "pipeB")
        a1 = emit(b, "scalar", "act")(
            lambda eng: eng.activation(out=pipeA[:, 0:n], in_=ps_in,
                                       func=AF.Square))
        a2 = emit(b, "scalar", "act")(
            lambda eng: eng.copy(out=pipeB[:, 0:n], in_=ps_in))
        if dbgme:
            tap(b, pipeB[:, 0:n])
            tap(b, pipeA[:, 0:n])
        b.wait("tensor", "act", a1)
        b.use("tensor", ("bank", 2))
        p1 = emit(b, "tensor", "pe")(
            lambda eng: eng.matmul(PS[2][0:2, 0:n], sb["c_blk2"],
                                   pipeA[:, 0:n], start=True, stop=True))
        b.wait("scalar", "pe", p1)
        a3 = emit(b, "scalar", "act")(
            lambda eng: eng.activation(out=pipeA[0:2, 0:n], in_=PS[2][0:2, 0:n],
                                       func=AF.Sqrt))
        b.consumed(("bank", 2), "act", a3)
        b.wait("vector", "act", a3)
        emit(b, "vector", "dve")(
            lambda eng: eng.tensor_scalar(
                out=pipeA[0:2, 0:n], in0=pipeA[0:2, 0:n], scalar1=1e-12,
                scalar2=None, op0=ALU.max))
        d2 = emit(b, "vector", "dve")(
            lambda eng: eng.reciprocal(out=pipeA[0:2, 0:n],
                                       in_=pipeA[0:2, 0:n]))
        if dbgme:
            tap(b, pipeA[0:2, 0:n])
        b.wait("tensor", "dve", d2)
        b.use("tensor", ("bank", 3))
        p2 = emit(b, "tensor", "pe")(
            lambda eng: eng.matmul(PS[3][:, 0:n], sb["c_blk2T"],
                                   pipeA[0:2, 0:n], start=True, stop=True))
        b.wait("vector", "pe", p2)
        d3 = emit(b, "vector", "dve")(
            lambda eng: eng.tensor_tensor(out=pipeB[:, 0:n], in0=pipeB[:, 0:n],
                                          in1=PS[3][:, 0:n], op=ALU.mult))
        b.consumed(("bank", 3), "dve", d3)
        if dbgme:
            tap(b, pipeB[:, 0:n])
        b.wait("scalar", "dve", d3)
        a4 = emit(b, "scalar", "act")(
            lambda eng: eng.activation(out=pipeA[:, 0:n], in_=pipeB[:, 0:n],
                                       func=AF.Abs))
        emit(b, "scalar", "act")(
            lambda eng: eng.activation(out=pipeA[:, 0:n], in_=pipeA[:, 0:n],
                                       func=AF.Exp, scale=-1.0))
        a6 = emit(b, "scalar", "act")(
            lambda eng: eng.activation(out=pipeA[:, 0:n], in_=pipeA[:, 0:n],
                                       func=AF.Ln, bias=sb["c_cvec"][:, 0:1]))
        b.wait("vector", "act", a6)
        d5 = emit(b, "vector", "dve")(
            lambda eng: eng.scalar_tensor_tensor(
                out=out_bf, in0=pipeB[:, 0:n], scalar=0.0, in1=pipeA[:, 0:n],
                op0=ALU.max, op1=ALU.add))
        b.consumed("pipeA", "dve", d5)
        b.consumed("pipeB", "dve", d5)
        return d5

    b.wait("tensor", *k_loaded)
    emit(b, "vector", "dve")(
        lambda eng: eng.memset(kv_acc, 0.0))

    k_mms_done = 0
    for st in range(NST):
        b.wait("tensor", "dve", y0_done[st])
        for half in (0, 1):
            if half == 1:
                b.wait("tensor", *v_loaded)
            for oc in range(NDC):
                bank = PS[oc % 2]
                b.use("tensor", ("bank", oc % 2))
                for kc in range(NDC):
                    emit(b, "tensor", "pe")(
                        lambda eng, bank=bank, half=half, kc=kc, oc=oc, st=st:
                        eng.matmul(
                            bank[:, 0:ST],
                            wslot(half, kc)[:, oc * 128:(oc + 1) * 128],
                            y0bf(kc)[:, st * ST:(st + 1) * ST],
                            start=(kc == 0), stop=(kc == NDC - 1)))
                mm_done = b.cnt["pe"]
                if half == 0:
                    k_mms_done = mm_done
                b.wait("scalar", "pe", mm_done)
                if half == 0:
                    b.use("vector", ("kst", oc))
                    qk_pipeline(bank[:, 0:ST], ST, kst(oc))
                    b.consumed(("bank", oc % 2), "act")
                    b.also_consumed(("bank", oc % 2), "dve")
                    emit(b, "vector", "dve")(
                        lambda eng, oc=oc, st=st: eng.reduce_sum(
                            out=ksum_st[:, oc * 4 + st:oc * 4 + st + 1],
                            in_=kst(oc), axis=AX.X))
                else:
                    b.use("scalar", ("vst", oc))
                    emit(b, "scalar", "act")(
                        lambda eng, bank=bank, oc=oc: eng.copy(
                            out=vst(oc), in_=bank[:, 0:ST]))
                    b.consumed(("bank", oc % 2), "act")
        stage_done_d = b.cnt["dve"]
        stage_done_a = b.cnt["act"]
        b.wait("tensor", "dve", stage_done_d)
        b.wait("tensor", "act", stage_done_a)
        for tcl in range(ST // 128):
            for src, dstf in ((kst, ktm), (vst, vtm)):
                for grp in range(2):
                    bk = 4 + grp
                    trv = PS[bk].bitcast(BF16)
                    b.use("tensor", ("bank", bk))
                    for j in range(4):
                        oc = grp * 4 + j
                        emit(b, "tensor", "pe")(
                            lambda eng, trv=trv, j=j, src=src, oc=oc, tcl=tcl:
                            eng.matmul(
                                trv[:, j * 128:(j + 1) * 128],
                                src(oc)[:, tcl * 128:(tcl + 1) * 128],
                                sb["c_identb"],
                                start=(j == 0), stop=(j == 3),
                                is_transpose=True))
                    pc = b.cnt["pe"]
                    b.wait("scalar", "pe", pc)
                    b.use("scalar", ("tm", dstf is vtm, tcl))
                    emit(b, "scalar", "act")(
                        lambda eng, trv=trv, dstf=dstf, tcl=tcl, grp=grp:
                        eng.copy(
                            out=dstf(tcl)[:, grp * 512:(grp + 1) * 512],
                            in_=trv[:, 0:512]))
                    b.consumed(("bank", bk), "act")
        tm_done = b.cnt["act"]
        if STOP_AFTER == "tm" and st == 0:
            oap = out_t_global[0].ap()
            b.wait("sync", "act", tm_done)
            b.wait("sync", "dve", b.cnt["dve"])
            for oc in range(4):
                emit(b, "vector", "dve")(
                    lambda eng, oc=oc: eng.tensor_copy(
                        out=AS[:, 1536 + oc * 512:1536 + (oc + 1) * 512][:, 0:512],
                        in_=kst(oc)))
            b.wait("sync", "dve", b.cnt["dve"])
            emit(b, "sync", "dsa", 16)(
                lambda eng: eng.dma_start(out=oap[0:128, 0:1024],
                                          in_=AS[:, 1536:2560]))
            for i, tmf in ((0, ktm), (1, vtm)):
                emit(b, "vector", "dve")(
                    lambda eng, tmf=tmf: eng.tensor_copy(
                        out=AS[:, 1536:2560], in_=tmf(0)))
                b.wait("sync", "dve", b.cnt["dve"])
                emit(b, "sync", "dsa", 16)(
                    lambda eng, i=i: eng.dma_start(
                        out=oap[128 * (i + 1):128 * (i + 2), 0:1024],
                        in_=AS[:, 1536:2560]))
            b.wait("sync", "dsa", b.cnt["dsa"])
            return
        b.wait("tensor", "act", tm_done)
        for h in range(HEADS):
            pr, off = h // 2, (h % 2) * 64
            bk = 6 + (h % 2)
            b.use("tensor", ("bank", bk))
            for tcl in range(ST // 128):
                emit(b, "tensor", "pe")(
                    lambda eng, h=h, tcl=tcl, bk=bk: eng.matmul(
                        PS[bk][0:64, 0:64],
                        ktm(tcl)[:, h * 64:(h + 1) * 64],
                        vtm(tcl)[:, h * 64:(h + 1) * 64],
                        start=(tcl == 0), stop=(tcl == 3)))
            hp = b.cnt["pe"]
            b.wait("vector", "pe", hp)
            emit(b, "vector", "dve")(
                lambda eng, pr=pr, off=off, bk=bk, st=st: eng.tensor_tensor(
                    out=kv_acc[off:off + 64, pr * 64:(pr + 1) * 64],
                    in0=kv_acc[off:off + 64, pr * 64:(pr + 1) * 64],
                    in1=PS[bk][0:64, 0:64], op=ALU.add))
            b.consumed(("bank", bk), "dve")
        for oc in range(NDC):
            b.consumed(("kst", oc), "pe", b.cnt["pe"])
            b.consumed(("vst", oc), "pe", b.cnt["pe"])
        for tcl in range(ST // 128):
            b.consumed(("tm", False, tcl), "pe", b.cnt["pe"])
            b.consumed(("tm", True, tcl), "pe", b.cnt["pe"])

    if STOP_AFTER == "kv":
        tap(b, ksum_st)

    for oc in range(NDC):
        emit(b, "vector", "dve")(
            lambda eng, oc=oc: eng.tensor_tensor(
                out=ksum_st[:, oc * 4:oc * 4 + 1],
                in0=ksum_st[:, oc * 4:oc * 4 + 1],
                in1=ksum_st[:, oc * 4 + 1:oc * 4 + 2], op=ALU.add))
        emit(b, "vector", "dve")(
            lambda eng, oc=oc: eng.tensor_tensor(
                out=ksum_st[:, oc * 4 + 2:oc * 4 + 3],
                in0=ksum_st[:, oc * 4 + 2:oc * 4 + 3],
                in1=ksum_st[:, oc * 4 + 3:oc * 4 + 4], op=ALU.add))
        emit(b, "vector", "dve")(
            lambda eng, oc=oc: eng.tensor_tensor(
                out=ksum_acc[:, oc:oc + 1],
                in0=ksum_st[:, oc * 4:oc * 4 + 1],
                in1=ksum_st[:, oc * 4 + 2:oc * 4 + 3], op=ALU.add))
    ks_done = b.cnt["dve"]

    if STOP_AFTER == "kv":
        tap(b, kv_acc)
        tap(b, ksum_acc)

    ar_in, ar_out = dram["ar_in"], dram["ar_out"]
    b.wait("sync", "dve", ks_done)
    emit(b, "sync", "arx", 16)(
        lambda eng: eng.dma_start(out=ar_in.ap()[:, 0:512], in_=kv_acc))
    d2 = emit(b, "sync", "arx", 16)(
        lambda eng: eng.dma_start(out=ar_in.ap()[:, 512:520], in_=ksum_acc))
    b.wait("gpsimd", "arx", d2)
    cc_c = emit(b, "gpsimd", "cc")(
        lambda eng: eng.collective_compute(
            "AllReduce", ALU.add, replica_groups=[list(range(NCORES))],
            ins=[ar_in.ap().opt()], outs=[ar_out.ap().opt()]))
    b.wait("sync", "cc", cc_c)
    d3 = emit(b, "sync", "ary", 16)(
        lambda eng: eng.dma_start(out=kv_acc, in_=ar_out.ap()[:, 0:512]))
    d4 = emit(b, "sync", "ary", 16)(
        lambda eng: eng.dma_start(out=ksum_acc, in_=ar_out.ap()[:, 512:520]))
    b.wait("vector", "ary", d4)
    emit(b, "vector", "dve")(
        lambda eng: eng.tensor_scalar(
            out=ksum_acc, in0=ksum_acc, scalar1=1e-6, scalar2=None,
            op0=ALU.add))
    emit(b, "vector", "dve")(
        lambda eng: eng.tensor_scalar(
            out=kv_acc, in0=kv_acc, scalar1=0.125, scalar2=None,
            op0=ALU.mult))
    emit(b, "vector", "dve")(
        lambda eng: eng.memset(ASb[:, 0:1024], 0.0))
    for pr in range(8):
        emit(b, "vector", "dve")(
            lambda eng, pr=pr: eng.tensor_copy(
                out=kvbd(pr)[0:64, 0:64],
                in_=kv_acc[0:64, pr * 64:(pr + 1) * 64]))
        emit(b, "vector", "dve")(
            lambda eng, pr=pr: eng.tensor_copy(
                out=kvbd(pr)[64:128, 64:128],
                in_=kv_acc[64:128, pr * 64:(pr + 1) * 64]))
    kv_ready = b.cnt["dve"]

    if STOP_AFTER == "kv":
        oap = out_t_global[0]
        b.wait("sync", "dve", kv_ready)
        emit(b, "sync", "dsa", 16)(
            lambda eng: eng.dma_start(out=oap.ap()[0:128, 0:512], in_=kv_acc))
        emit(b, "sync", "dsa", 16)(
            lambda eng: eng.dma_start(out=oap.ap()[0:128, 512:520], in_=ksum_acc))
        b.wait("sync", "dsa", b.cnt["dsa"])
        return

    b.consumed(("wslot", 0), "pe", k_mms_done)
    b.consumed(("wslot", 1), "pe", b.cnt["pe"])
    q_loaded = load_third(0, 0)
    b.wait("tensor", *q_loaded)
    b.wait("tensor", "dve", kv_ready)

    for st in range(NST):
        for oc in range(NDC):
            bank = PS[oc % 2]
            b.use("tensor", ("bank", oc % 2))
            for kc in range(NDC):
                emit(b, "tensor", "pe")(
                    lambda eng, bank=bank, kc=kc, oc=oc, st=st: eng.matmul(
                        bank[:, 0:ST],
                        wslot(0, kc)[:, oc * 128:(oc + 1) * 128],
                        y0bf(kc)[:, st * ST:(st + 1) * ST],
                        start=(kc == 0), stop=(kc == NDC - 1)))
            mm_done = b.cnt["pe"]
            b.wait("scalar", "pe", mm_done)
            q2 = q2s(oc % 2)
            b.use("vector", ("q2", oc % 2))
            d5 = qk_pipeline(bank[:, 0:ST], ST, q2)
            b.consumed(("bank", oc % 2), "act")
            b.also_consumed(("bank", oc % 2), "dve")
            b.use("vector", ("ks2", oc))
            emit(b, "vector", "dve")(
                lambda eng, oc=oc: eng.tensor_tensor(
                    out=ks2_bf[:, 2 * oc:2 * oc + 1],
                    in0=ksum_acc[:, oc:oc + 1], in1=sb["c_blk2"][:, 0:1],
                    op=ALU.mult))
            ks_c = emit(b, "vector", "dve")(
                lambda eng, oc=oc: eng.tensor_tensor(
                    out=ks2_bf[:, 2 * oc + 1:2 * oc + 2],
                    in0=ksum_acc[:, oc:oc + 1], in1=sb["c_blk2"][:, 1:2],
                    op=ALU.mult))
            b.use("tensor", ("bank", 2))
            b.wait("tensor", "dve", ks_c)
            p1 = emit(b, "tensor", "pe")(
                lambda eng, oc=oc, q2=q2: eng.matmul(
                    PS[2][0:2, 0:ST], ks2_bf[:, 2 * oc:2 * oc + 2], q2,
                    start=True, stop=True))
            b.consumed(("ks2", oc), "pe", p1)
            b.wait("vector", "pe", p1)
            b.use("vector", "pipeA")
            d6 = emit(b, "vector", "dve")(
                lambda eng: eng.reciprocal(out=pipeA[0:2, 0:ST],
                                           in_=PS[2][0:2, 0:ST]))
            b.consumed(("bank", 2), "dve", d6)
            b.wait("tensor", "dve", d6)
            b.use("tensor", ("bank", 3))
            p2 = emit(b, "tensor", "pe")(
                lambda eng: eng.matmul(PS[3][:, 0:ST], sb["c_blk2T"],
                                       pipeA[0:2, 0:ST], start=True, stop=True))
            b.consumed("pipeA", "pe", p2)
            b.wait("vector", "pe", p2)
            d7 = emit(b, "vector", "dve")(
                lambda eng, q2=q2: eng.tensor_tensor(
                    out=q2, in0=q2, in1=PS[3][:, 0:ST], op=ALU.mult))
            b.consumed(("bank", 3), "dve", d7)
            b.wait("tensor", "dve", d7)
            b.use("tensor", ("bank", 5))
            po = emit(b, "tensor", "pe")(
                lambda eng, oc=oc, q2=q2: eng.matmul(
                    PS[5][:, 0:ST], kvbd(oc), q2, start=True, stop=True))
            b.consumed(("q2", oc % 2), "pe", po)
            b.wait("vector", "pe", po)
            d8 = emit(b, "vector", "dve")(
                lambda eng, oc=oc, st=st: eng.tensor_tensor(
                    out=xr(oc)[:, st * ST:(st + 1) * ST],
                    in0=xr(oc)[:, st * ST:(st + 1) * ST],
                    in1=PS[5][:, 0:ST], op=ALU.add))
            b.consumed(("bank", 5), "dve", d8)





HALO = 264
YW = HALO + T


def ffn6_call(b, ti, sb, PS, AY, AXa, AW, AS, dram, xr, emit_ln, out_t):
    nc = b.nc
    AXb = AXa.bitcast(BF16)
    AWb = AW.bitcast(BF16)
    ASb = AS.bitcast(BF16)

    def yt(dc):
        return AY[:, dc * YW:(dc + 1) * YW]

    def ys(dc, c0, n):
        off = -1 if dc >= 4 else 0
        return yt(dc)[:, c0 + off:c0 + off + n]

    wo_bf = lambda kc: AXb[:, kc * 1024:(kc + 1) * 1024]
    kroll = lambda oc: AXb[:, 8192 + oc * 768: 8192 + (oc + 1) * 768]
    vchunk = lambda r: AXb[:, 14336 + r * 1024: 14336 + (r + 1) * 1024]
    tb_cs = AXb[:, 20480:20992]
    tb_ss = AXb[:, 20992:21504]
    tb_ci = AXb[:, 21504:22272]
    tb_si = AXb[:, 22272:23040]
    qhg = lambda oc: AXb[:, 23040 + oc * 512: 23040 + (oc + 1) * 512]
    xreg = lambda i: AXb[:, 27136 + i * 512: 27136 + (i + 1) * 512]

    xn_bf = lambda kc: AXb[:, kc * 1024:(kc + 1) * 1024]
    h_bf = lambda j: AXb[:, 8192 + j * 1024: 8192 + (j + 1) * 1024]

    wreg = lambda r, kc: AWb[:, r * 8192 + kc * 1024: r * 8192 + (kc + 1) * 1024]
    w1buf = lambda par, kc: AWb[:, par * 2048 + kc * 256:
                                par * 2048 + (kc + 1) * 256]
    w2buf = lambda par, j: AWb[:, 4096 + par * 2816 + j * 128:
                               4096 + par * 2816 + (j + 1) * 128]

    pstage = lambda i: ASb[:, i * 512:(i + 1) * 512]
    pf32 = lambda i: AS[:, i * 512:(i + 1) * 512]
    pbf = lambda i: ASb[:, 2048 + i * 512:2048 + (i + 1) * 512]
    ptbf = lambda i: ASb[:, 3072 + i * 512:3072 + (i + 1) * 512]
    pipeA = AS[:, 2048:2560]
    pipeAb = ASb[:, 4096:4608]
    pipeB = AS[:, 2560:3072]
    stat = lambda i: AS[:, 3584 + i:3584 + i + 1]

    mask_t = lambda v, chunk: sb["c_masks"][:, (2 * v + chunk) * 512:
                                            (2 * v + chunk + 1) * 512]

    hx_in, hx_out = dram["hx_in"].ap(), dram["hx_out"].ap()
    hxs_in, hxs_out = dram["hxs_in"].ap(), dram["hxs_out"].ap()
    rope = ti["rope"].ap()

    exch_i = [0]

    def exchange(wide):
        sem = "hxa" if exch_i[0] % 2 == 0 else "hxb"
        exch_i[0] += 1
        if wide == 257:
            bin_, bout, roff = hx_in, hx_out, "r257"
        else:
            bin_, bout, roff = hxs_in, hxs_out, "r8"
        b.use("sync", ("hx", sem))
        b.wait("sync", "dve", b.cnt["dve"])
        b.wait("sync", "act", b.cnt["act"])
        b.wait("sync", "pe", b.cnt["pe"])
        for dc in range(NDC):
            emit(b, "sync", sem, 16)(
                lambda eng, dc=dc: eng.dma_start(
                    out=bin_[dc * 128:(dc + 1) * 128, :],
                    in_=yt(dc)[:, YW - wide:YW]))
        b.wait("gpsimd", sem, b.cnt[sem])
        cc_c = emit(b, "gpsimd", "cc")(
            lambda eng: eng.collective_compute(
                "AllGather", ALU.bypass, replica_groups=[list(range(NCORES))],
                ins=[bin_.opt()], outs=[bout.opt()]))
        b.wait("sync", "cc", cc_c)
        for dc in range(NDC):
            def f(eng, dc=dc, wide=wide, bout=bout, roff=roff):
                src = bass.AP(bout.tensor, b.regs[roff] + dc * 128 * wide,
                              [[wide, 128], [1, wide]])
                return eng.dma_start(out=yt(dc)[:, HALO - wide:HALO], in_=src)
            emit(b, "sync", sem, 16)(f)
        rb = b.cnt[sem]
        b.wait("vector", sem, rb)
        for dc in range(NDC):
            emit(b, "vector", "dve")(
                lambda eng, dc=dc, wide=wide: eng.tensor_scalar(
                    out=yt(dc)[:, HALO - wide:HALO],
                    in0=yt(dc)[:, HALO - wide:HALO],
                    scalar1=sb["c_hv"][:, 0:1], scalar2=None, op0=ALU.mult))
        b.consumed(("hx", sem), "dve")
        return b.cnt["dve"]

    def phase_barrier(tag):
        for s in ("pe", "dve", "act"):
            b.consumed((tag, s), s, b.cnt[s])

    def phase_wait(tag, engines=("vector", "scalar", "gpsimd", "tensor")):
        for e in engines:
            for s in ("pe", "dve", "act"):
                b.use(e, (tag, s))

    ci = ffn6_call.ci = getattr(ffn6_call, "ci", -1) + 1

    def attn_sub(li):
        b.phase = f"f{ci}L{li}a"
        halo_ready = exchange(257)
        if STOP_AFTER == "fg0" and li == 0:
            for dcq in range(2):
                tap(b, yt(dcq)[:, 0:512])
        phase_wait("phase")
        lsem = "lca" if li % 2 == 0 else "lcb"
        b.use("sync", ("lc_attn", lsem))
        for cn, iname in (("c_qs2", "qs2"), ("c_ks2", "ks2")):
            emit(b, "sync", lsem, 16)(
                lambda eng, cn=cn, iname=iname, li=li: eng.dma_start(
                    out=sb[cn], in_=ti[iname].ap()[:, li * 128:(li + 1) * 128]))
        lc_ok = (lsem, b.cnt[lsem])
        b.wait("tensor", *lc_ok)
        b.use("gpsimd", "aw_regions")
        b.use("sync", "aw_regions_s")
        wq = ti["f_wqkv"].ap()
        for r in range(2):
            for kc in range(NDC):
                emit(b, "gpsimd", "dwa", 16)(
                    lambda eng, r=r, kc=kc: eng.dma_start(
                        out=wreg(r, kc),
                        in_=wq[li * 1024 + kc * 128: li * 1024 + (kc + 1) * 128,
                               r * 1024:(r + 1) * 1024]))
        for kc in range(NDC):
            emit(b, "sync", "dwc", 16)(
                lambda eng, kc=kc: eng.dma_start(
                    out=wreg(2, kc),
                    in_=wq[li * 1024 + kc * 128: li * 1024 + (kc + 1) * 128,
                           2 * 1024:3 * 1024]))
        for kc in range(NDC):
            emit(b, "sync", "dwc", 16)(
                lambda eng, kc=kc: eng.dma_start(
                    out=wo_bf(kc),
                    in_=ti["f_wo"].ap()[li * 1024 + kc * 128:
                                        li * 1024 + (kc + 1) * 128, :]))
        w_ok = b.cnt["dwa"]
        b.wait("tensor", "dwa", w_ok)
        b.wait("tensor", "dwc", b.cnt["dwc"])

        shst = lambda dc, g: AS[:, 3600 + (dc - 4) * 3 + (g - 1):
                                3601 + (dc - 4) * 3 + (g - 1)]
        b.use("vector", ("shst",))
        b.wait("vector", "dve", halo_ready)
        for dc in range(4, 8):
            for gg in range(1, 4):
                emit(b, "vector", "dve")(
                    lambda eng, dc=dc, gg=gg: eng.tensor_copy(
                        out=shst(dc, gg),
                        in_=yt(dc)[:, HALO + gg * 512 - 1:HALO + gg * 512]))
        stash_ok = b.cnt["dve"]

        def qk_pipe(bank, n, out_bf, li, kind, tcol0):
            cs_t, ss_t, sc2 = ((tb_cs, tb_ss, "c_qs2") if kind == "q"
                               else (tb_ci, tb_si, "c_ks2"))
            b.use("scalar", "pipeA")
            b.use("scalar", "pipeB")
            a1 = emit(b, "scalar", "act")(
                lambda eng: eng.activation(out=pipeA[:, 0:n], in_=bank[:, 0:n],
                                           func=AF.Square))
            emit(b, "scalar", "act")(
                lambda eng: eng.copy(out=pipeB[:, 0:n], in_=bank[:, 0:n]))
            b.wait("tensor", "act", a1)
            b.use("tensor", ("bank", 2))
            p1 = emit(b, "tensor", "pe")(
                lambda eng: eng.matmul(PS[2][0:2, 0:n], sb["c_blk2"],
                                       pipeA[:, 0:n], start=True, stop=True))
            b.wait("scalar", "pe", p1)
            a3 = emit(b, "scalar", "act")(
                lambda eng: eng.activation(out=pipeA[0:2, 0:n],
                                           in_=PS[2][0:2, 0:n], func=AF.Sqrt))
            b.consumed(("bank", 2), "act", a3)
            b.wait("vector", "act", a3)
            emit(b, "vector", "dve")(
                lambda eng: eng.tensor_scalar(
                    out=pipeA[0:2, 0:n], in0=pipeA[0:2, 0:n], scalar1=1e-12,
                    scalar2=None, op0=ALU.max))
            d2 = emit(b, "vector", "dve")(
                lambda eng: eng.reciprocal(out=pipeA[0:2, 0:n],
                                           in_=pipeA[0:2, 0:n]))
            b.wait("tensor", "dve", d2)
            b.use("tensor", ("bank", 3))
            p2 = emit(b, "tensor", "pe")(
                lambda eng: eng.matmul(
                    PS[3][:, 0:n], sb[sc2],
                    pipeA[0:2, 0:n], start=True, stop=True))
            b.wait("vector", "pe", p2)
            emit(b, "vector", "dve")(
                lambda eng: eng.tensor_tensor(out=pipeB[:, 0:n],
                                              in0=pipeB[:, 0:n],
                                              in1=PS[3][:, 0:n], op=ALU.mult))
            d4 = emit(b, "vector", "dve")(
                lambda eng: eng.tensor_copy(out=pipeAb[:, 0:n],
                                            in_=pipeB[:, 0:n]))
            b.consumed(("bank", 3), "dve", d4)
            b.wait("tensor", "dve", d4)
            b.use("tensor", ("bank", 4))
            p3 = emit(b, "tensor", "pe")(
                lambda eng: eng.matmul(PS[4][:, 0:n], sb["c_rsign"],
                                       pipeAb[:, 0:n], start=True, stop=True))
            emit(b, "vector", "dve")(
                lambda eng: eng.tensor_tensor(
                    out=pipeB[:, 0:n], in0=pipeAb[:, 0:n],
                    in1=cs_t[:, tcol0:tcol0 + n], op=ALU.mult))
            b.wait("vector", "pe", p3)
            emit(b, "vector", "dve")(
                lambda eng: eng.tensor_tensor(
                    out=pipeA[:, 0:n], in0=PS[4][:, 0:n],
                    in1=ss_t[:, tcol0:tcol0 + n], op=ALU.mult))
            d7 = emit(b, "vector", "dve")(
                lambda eng: eng.tensor_tensor(
                    out=out_bf, in0=pipeB[:, 0:n], in1=pipeA[:, 0:n],
                    op=ALU.add))
            b.consumed(("bank", 4), "dve", d7)
            b.consumed("pipeA", "dve", d7)
            b.consumed("pipeB", "dve", d7)
            return d7

        for g in range(4):
            b.phase = f"f{ci}L{li}set"
            c0 = HALO + g * 512
            tsem = "rta" if g % 2 == 0 else "rtb"
            b.use("sync", ("tbl",))
            for t, dst, tc0, tw in ((0, tb_cs, c0, 512), (1, tb_ss, c0, 512),
                                    (2, tb_ci, 8 + g * 512, 768),
                                    (3, tb_si, 8 + g * 512, 768)):
                emit(b, "sync", tsem, 16)(
                    lambda eng, t=t, dst=dst, tc0=tc0, tw=tw: eng.dma_start(
                        out=dst[:, 0:tw],
                        in_=rope[t * 128:(t + 1) * 128, tc0:tc0 + tw]))
            tbl_ok = b.cnt[tsem]

            if g > 0:
                b.wait("vector", "pe", b.cnt["pe"])
                b.wait("vector", "act", b.cnt["act"])
                for oc in range(NDC):
                    emit(b, "vector", "dve")(
                        lambda eng, oc=oc: eng.tensor_copy(
                            out=kroll(oc)[:, 0:256], in_=kroll(oc)[:, 512:768]))
                for r in range(2):
                    emit(b, "vector", "dve")(
                        lambda eng, r=r: eng.tensor_copy(
                            out=vchunk(r), in_=vchunk(4 + r)))
            roll_ok = b.cnt["dve"]

            passes = []
            if g == 0:
                passes.append(("halo", 8, 256, 0))
            passes.append(("own", c0, 512, 256))

            for pname, ycol0, n, koff in passes:
                b.use("vector", ("xreg",))
                b.wait("vector", "dve", halo_ready)
                for kc in range(NDC):
                    if kc >= 4 and pname == "own" and g > 0:
                        emit(b, "vector", "dve")(
                            lambda eng, kc=kc, g=g: eng.tensor_copy(
                                out=xreg(kc)[:, 0:1], in_=shst(kc, g)))
                        emit(b, "vector", "dve")(
                            lambda eng, kc=kc, ycol0=ycol0, n=n: eng.tensor_copy(
                                out=xreg(kc)[:, 1:n],
                                in_=yt(kc)[:, ycol0:ycol0 + n - 1]))
                    else:
                        emit(b, "vector", "dve")(
                            lambda eng, kc=kc, ycol0=ycol0, n=n: eng.tensor_copy(
                                out=xreg(kc)[:, 0:n], in_=ys(kc, ycol0, n)))
                cast_ok = b.cnt["dve"]
                b.wait("tensor", "dve", cast_ok)
                b.wait("tensor", tsem, tbl_ok)
                b.wait("tensor", "dve", roll_ok)
                kinds = [("k", 1), ("v", 2)] if pname == "halo" else \
                        [("q", 0), ("k", 1), ("v", 2)]
                for kind, r in kinds:
                    for oc in range(NDC):
                        bank = PS[oc % 2]
                        b.use("tensor", ("bank", oc % 2))
                        for kc in range(NDC):
                            emit(b, "tensor", "pe")(
                                lambda eng, bank=bank, r=r, kc=kc, oc=oc, n=n:
                                eng.matmul(
                                    bank[:, 0:n],
                                    wreg(r, kc)[:, oc * 128:(oc + 1) * 128],
                                    xreg(kc)[:, 0:n],
                                    start=(kc == 0), stop=(kc == NDC - 1)))
                        mm_done = b.cnt["pe"]
                        b.wait("scalar", "pe", mm_done)
                        if kind == "q":
                            b.use("vector", ("qhg", oc))
                            qk_pipe(bank, n, qhg(oc), li, "q", 0)
                            b.consumed(("bank", oc % 2), "act")
                            b.also_consumed(("bank", oc % 2), "dve")
                        elif kind == "k":
                            b.use("vector", ("kroll", oc))
                            tcol0 = 0 if pname == "halo" else 256
                            qk_pipe(bank, n,
                                    kroll(oc)[:, koff:koff + n], li, "k", tcol0)
                            b.consumed(("bank", oc % 2), "act")
                            b.also_consumed(("bank", oc % 2), "dve")
                        else:
                            b.use("scalar", "pipeA")
                            a_c = emit(b, "scalar", "act")(
                                lambda eng, bank=bank, n=n: eng.copy(
                                    out=pipeAb[:, 0:n], in_=bank[:, 0:n]))
                            b.consumed(("bank", oc % 2), "act", a_c)
                            b.wait("tensor", "act", a_c)
                            ntc = n // 128
                            b.use("tensor", ("bank", 5))
                            trv = PS[5].bitcast(BF16)
                            for tcl in range(ntc):
                                emit(b, "tensor", "pe")(
                                    lambda eng, trv=trv, tcl=tcl, ntc=ntc:
                                    eng.matmul(
                                        trv[:, tcl * 128:(tcl + 1) * 128],
                                        pipeAb[:, tcl * 128:(tcl + 1) * 128],
                                        sb["c_identb"],
                                        start=(tcl == 0), stop=(tcl == ntc - 1),
                                        is_transpose=True))
                            pc = b.cnt["pe"]
                            b.wait("scalar", "pe", pc)
                            b.use("scalar", ("vchunks",))
                            for tcl in range(ntc):
                                r0 = koff // 128 + tcl
                                emit(b, "scalar", "act")(
                                    lambda eng, trv=trv, tcl=tcl, r0=r0, oc=oc:
                                    eng.copy(
                                        out=vchunk(r0)[:, oc * 128:(oc + 1) * 128],
                                        in_=trv[:, tcl * 128:(tcl + 1) * 128]))
                            b.consumed(("bank", 5), "act")
                            b.consumed("pipeA", "pe", pc)
                qv_done_d, qv_done_a = b.cnt["dve"], b.cnt["act"]
                b.consumed(("xreg",), "pe", b.cnt["pe"])
            b.consumed(("xreg",), "pe", b.cnt["pe"])
            b.consumed(("tbl",), "pe", b.cnt["pe"])
            b.also_consumed(("tbl",), "dve")

            b.phase = f"f{ci}L{li}att"
            b.wait("tensor", "dve", qv_done_d)
            b.wait("tensor", "act", qv_done_a)
            its = [(h, qc) for h in range(HEADS) for qc in range(4)]
            S = {}

            def p_sc(i):
                h, qc = its[i]
                oc, hrow = h // 2, (h % 2) * 64
                w = qc // 2
                first_win = (g == 0 and w == 0)
                sbank = PS[i % 3]
                b.use("tensor", ("bank", i % 3))
                mt = mask_t(0 if first_win else 1, qc % 2)
                emit(b, "tensor", "pe")(
                    lambda eng, sbank=sbank, mt=mt: eng.matmul(
                        sbank[:, 0:512], sb["c_identb"], mt,
                        start=True, stop=False))
                S[i] = {"sbank": sbank}
                S[i]["sc"] = emit(b, "tensor", "pe")(
                    lambda eng, sbank=sbank, h=h, oc=oc, hrow=hrow,
                    qc=qc, w=w: eng.matmul(
                        sbank[:, 0:512],
                        qhg(oc)[hrow:hrow + 64, qc * 128:(qc + 1) * 128],
                        kroll(oc)[hrow:hrow + 64, w * 256:w * 256 + 512],
                        start=False, stop=True))

            def p_mx(i):
                sA = 4 * (i % 3)
                b.wait("vector", "pe", S[i]["sc"])
                b.use("vector", ("stat", i % 3))
                S[i]["mx"] = emit(b, "vector", "dve")(
                    lambda eng, sbank=S[i]["sbank"], sA=sA: eng.reduce_max(
                        out=stat(sA), in_=sbank[:, 0:512], axis=AX.X,
                        negate=True))

            def p_ex(i):
                sA = 4 * (i % 3)
                pfs = pf32(i % 2)
                S[i]["pf"] = pfs
                b.wait("scalar", "dve", S[i]["mx"])
                b.use("scalar", ("pf", i % 2))
                S[i]["ex"] = emit(b, "scalar", "act")(
                    lambda eng, sbank=S[i]["sbank"], pfs=pfs, sA=sA:
                    eng.activation(
                        out=pfs, in_=sbank[:, 0:512], func=AF.Exp,
                        bias=stat(sA), scale=1.0, accum_out=stat(sA + 2)))
                b.consumed(("bank", i % 3), "act", S[i]["ex"])

            def p_rn(i):
                sA = 4 * (i % 3)
                pslot = pbf(i % 2)
                S[i]["pb"] = pslot
                b.wait("vector", "act", S[i]["ex"])
                drc = emit(b, "vector", "dve")(
                    lambda eng, sA=sA: eng.reciprocal(out=stat(sA + 3),
                                                      in_=stat(sA + 2)))
                b.wait("scalar", "dve", drc)
                b.use("scalar", ("pbf", i % 2))
                S[i]["rn"] = emit(b, "scalar", "act")(
                    lambda eng, pslot=pslot, pfs=S[i]["pf"], sA=sA:
                    eng.activation(
                        out=pslot, in_=pfs, func=AF.Copy,
                        scale=stat(sA + 3)))
                b.consumed(("pf", i % 2), "act", S[i]["rn"])
                b.consumed(("stat", i % 3), "act", S[i]["rn"])

            def p_pt(i):
                b.wait("tensor", "act", S[i]["rn"])
                b.use("tensor", ("bank", 4 + i % 2))
                trv = PS[4 + i % 2].bitcast(BF16)
                S[i]["trv"] = trv
                pslot = S[i]["pb"]
                for kcB in range(4):
                    emit(b, "tensor", "pe")(
                        lambda eng, trv=trv, kcB=kcB, pslot=pslot:
                        eng.matmul(
                            trv[:, kcB * 128:(kcB + 1) * 128],
                            pslot[:, kcB * 128:(kcB + 1) * 128],
                            sb["c_identb"],
                            start=(kcB == 0), stop=(kcB == 3),
                            is_transpose=True))
                ptc = b.cnt["pe"]
                b.wait("scalar", "pe", ptc)
                ptslot = ptbf(i % 2)
                S[i]["pt"] = ptslot
                b.use("scalar", ("pt", i % 2))
                S[i]["apt"] = emit(b, "scalar", "act")(
                    lambda eng, trv=trv, ptslot=ptslot: eng.copy(
                        out=ptslot, in_=trv[:, 0:512]))
                b.consumed(("bank", 4 + i % 2), "act", S[i]["apt"])
                b.consumed(("pbf", i % 2), "pe", ptc)

            def p_av(i):
                h, qc = its[i]
                oc, hrow = h // 2, (h % 2) * 64
                w = qc // 2
                avbk = 6 + (h % 2)
                ptslot = S[i]["pt"]
                b.use("tensor", ("bank", avbk))
                b.wait("tensor", "act", S[i]["apt"])
                for kcB in range(4):
                    r = 2 * w + kcB
                    tp = (0, hrow)
                    emit(b, "tensor", "pe")(
                        lambda eng, avbk=avbk, hrow=hrow, r=r, h=h,
                        kcB=kcB, ptslot=ptslot, tp=tp: eng.matmul(
                            PS[avbk][hrow:hrow + 64, 0:128],
                            vchunk(r)[:, h * 64:(h + 1) * 64],
                            ptslot[:, kcB * 128:(kcB + 1) * 128],
                            start=(kcB == 0), stop=(kcB == 3),
                            tile_position=tp))
                avc = b.cnt["pe"]
                b.wait("scalar", "pe", avc)
                b.use("scalar", ("xreg",))
                aav = emit(b, "scalar", "act")(
                    lambda eng, avbk=avbk, hrow=hrow, oc=oc, qc=qc:
                    eng.copy(
                        out=xreg(oc)[hrow:hrow + 64,
                                     qc * 128:(qc + 1) * 128],
                        in_=PS[avbk][hrow:hrow + 64, 0:128]))
                b.consumed(("bank", avbk), "act", aav)
                b.consumed(("pt", i % 2), "pe", avc)
                del S[i]

            stages = (p_sc, p_mx, p_ex, p_rn, p_pt, p_av)
            for r in range(len(its) + len(stages) - 1):
                for lag, fn in enumerate(stages):
                    j = r - lag
                    if 0 <= j < len(its):
                        fn(j)
            if STOP_AFTER == "fg1" and li == 0 and g == 0:
                for hh2 in range(2):
                    tap(b, xreg(hh2 // 2)[:, 0:512])
            if STOP_AFTER == "fg0" and li == 0 and g == 0:
                for ap in (qhg(0), qhg(4), kroll(0)[:, 0:512],
                           kroll(0)[:, 256:768], vchunk(2)[:, 0:512],
                           vchunk(2)[:, 512:1024], vchunk(3)[:, 0:512],
                           vchunk(3)[:, 512:1024]):
                    tap(b, ap)
            av_ok = b.cnt["act"]
            b.phase = f"f{ci}L{li}wo"
            b.wait("tensor", "act", av_ok)
            for oco in range(NDC):
                obk = PS[oco % 2]
                b.use("tensor", ("bank", oco % 2))
                for pair in range(NDC):
                    emit(b, "tensor", "pe")(
                        lambda eng, obk=obk, pair=pair, oco=oco: eng.matmul(
                            obk[:, 0:512],
                            wo_bf(pair)[:, oco * 128:(oco + 1) * 128],
                            xreg(pair)[:, 0:512],
                            start=(pair == 0), stop=(pair == NDC - 1)))
                oc_mm = b.cnt["pe"]
                b.wait("vector", "pe", oc_mm)
                dres = emit(b, "vector", "dve")(
                    lambda eng, obk=obk, oco=oco, c0=c0: eng.tensor_tensor(
                        out=yt(oco)[:, c0:c0 + 512],
                        in0=yt(oco)[:, c0:c0 + 512],
                        in1=obk[:, 0:512], op=ALU.add))
                b.consumed(("bank", oco % 2), "dve", dres)
            b.consumed(("xreg",), "pe", b.cnt["pe"])
        b.consumed("aw_regions", "pe", b.cnt["pe"])
        b.consumed(("lc_attn", lsem), "pe", b.cnt["pe"])
        b.consumed(("shst",), "dve", b.cnt["dve"])

    def ffn_sub(li):
        b.phase = f"f{ci}L{li}f"
        qkv_pe = b.cnt["pe"]
        exchange(8)
        phase_wait("phase")
        lsem = "lcc" if li % 2 == 0 else "lcd"
        b.use("sync", ("lc_ffn", lsem))
        for cn, iname, w in (("c_b1", "b1c", 44), ("c_b2", "b2c", 8),
                             ("c_lng", "lngc", 8), ("c_lnb", "lnbc", 8)):
            emit(b, "sync", lsem, 16)(
                lambda eng, cn=cn, iname=iname, w=w, li=li: eng.dma_start(
                    out=sb[cn], in_=ti[iname].ap()[:, li * w:(li + 1) * w]))
        lcf = (lsem, b.cnt[lsem])
        b.wait("vector", *lcf)
        b.wait("scalar", *lcf)
        w1 = ti["f_w1t"].ap()
        w2 = ti["f_w2t"].ap()
        for half in (1, 0):
            hc0 = HALO + half * 1024
            for sst in range(2):
                t0 = half * 1024 + sst * 512

                def cb(dc, sub, sst=sst):
                    emit(b, "vector", "dve")(
                        lambda eng, dc=dc, sub=sub, sst=sst: eng.tensor_scalar(
                            out=xn_bf(dc)[:, sst * 512:(sst + 1) * 512],
                            in0=sub,
                            scalar1=sb["c_lng"][:, dc:dc + 1],
                            scalar2=sb["c_lnb"][:, dc:dc + 1],
                            op0=ALU.mult, op1=ALU.add))
                b.use("vector", ("xn",))
                emit_ln(lambda dc: ys(dc, HALO, T), t0, 512, cb,
                        ("dve", b.cnt["dve"]))
            xn_ok = b.cnt["dve"]
            for j in range(22):
                par = j % 3
                w1off = (0, 2048, 9728)[par]
                w1q = ("gpsimd", "sync", "scalar")[par]
                w1sem = ("w1a", "w1b", "w1c")[par]
                b.use(w1q, ("w1buf", par))
                if par == 2:
                    b.wait(w1q, "pe", qkv_pe)
                emit(b, w1q, w1sem, 16)(
                    lambda eng, w1off=w1off, j=j, li=li: eng.dma_start(
                        out=AWb[:, w1off:w1off + 2048],
                        in_=w1[:, (li * 22 + j) * 2048:
                               (li * 22 + j + 1) * 2048]))
                w1_ok = b.cnt[w1sem]
                b.wait("tensor", w1sem, w1_ok)
                b.wait("tensor", "dve", xn_ok)
                for sst in range(2):
                    bka = PS[2 + (sst % 2) * 2]
                    bkg = PS[3 + (sst % 2) * 2]
                    b.use("tensor", ("bank", 2 + (sst % 2) * 2))
                    b.use("tensor", ("bank", 3 + (sst % 2) * 2))
                    for which, bk in ((0, bka), (1, bkg)):
                        for kc in range(NDC):
                            emit(b, "tensor", "pe")(
                                lambda eng, bk=bk, w1off=w1off, kc=kc,
                                which=which, sst=sst: eng.matmul(
                                    bk[:, 0:512],
                                    AWb[:, w1off + kc * 256 + which * 128:
                                        w1off + kc * 256 + which * 128 + 128],
                                    xn_bf(kc)[:, sst * 512:(sst + 1) * 512],
                                    start=(kc == 0), stop=(kc == NDC - 1)))
                    hg_mm = b.cnt["pe"]
                    b.wait("scalar", "pe", hg_mm)
                    b.use("scalar", "pipeA")
                    ag = emit(b, "scalar", "act")(
                        lambda eng, bkg=bkg, li=li, j=j: eng.activation(
                            out=pipeAb[:, 0:512], in_=bkg[:, 0:512],
                            func=AF.Gelu,
                            bias=sb["c_b1"][:, 2 * j + 1: 2 * j + 2]))
                    b.consumed(("bank", 3 + (sst % 2) * 2), "act", ag)
                    b.wait("vector", "act", ag)
                    b.use("vector", ("h", j))
                    dh = emit(b, "vector", "dve")(
                        lambda eng, bka=bka, li=li, j=j, sst=sst:
                        eng.scalar_tensor_tensor(
                            out=h_bf(j)[:, sst * 512:(sst + 1) * 512],
                            in0=bka[:, 0:512],
                            scalar=sb["c_b1"][:, 2 * j: 2 * j + 1],
                            in1=pipeAb[:, 0:512], op0=ALU.add, op1=ALU.mult))
                    b.consumed(("bank", 2 + (sst % 2) * 2), "dve", dh)
                    b.consumed("pipeA", "dve", dh)
                b.consumed(("w1buf", par), "pe", b.cnt["pe"])
            h_ok_d = b.cnt["dve"]
            b.phase = f"f{ci}L{li}w2"
            for oco in range(NDC):
                par = oco % 2
                w2sem = "w2a" if par == 0 else "w2b"
                w2q = "sync" if par == 0 else "gpsimd"
                b.use(w2q, ("w2buf", par))
                emit(b, w2q, w2sem, 16)(
                    lambda eng, par=par, oco=oco, li=li: eng.dma_start(
                        out=AWb[:, 4096 + par * 2816:4096 + par * 2816 + 2816],
                        in_=w2[:, (li * 8 + oco) * 2816:
                               (li * 8 + oco + 1) * 2816]))
                w2_ok = b.cnt[w2sem]
                b.wait("tensor", w2sem, w2_ok)
                b.wait("tensor", "dve", h_ok_d)
                for sst in range(2):
                    obk = PS[6 + (sst % 2)]
                    b.use("tensor", ("bank", 6 + (sst % 2)))
                    for j in range(22):
                        emit(b, "tensor", "pe")(
                            lambda eng, obk=obk, par=par, j=j, sst=sst:
                            eng.matmul(
                                obk[:, 0:512], w2buf(par, j),
                                h_bf(j)[:, sst * 512:(sst + 1) * 512],
                                start=(j == 0), stop=(j == 21)))
                    o_mm = b.cnt["pe"]
                    b.wait("vector", "pe", o_mm)
                    dres = emit(b, "vector", "dve")(
                        lambda eng, obk=obk, oco=oco, li=li, hc0=hc0, sst=sst:
                        eng.scalar_tensor_tensor(
                            out=yt(oco)[:, hc0 + sst * 512:hc0 + (sst + 1) * 512],
                            in0=obk[:, 0:512],
                            scalar=sb["c_b2"][:, oco:oco + 1],
                            in1=yt(oco)[:, hc0 + sst * 512:hc0 + (sst + 1) * 512],
                            op0=ALU.add, op1=ALU.add))
                    b.consumed(("bank", 6 + (sst % 2)), "dve", dres)
                b.consumed(("w2buf", par), "pe", b.cnt["pe"])
            b.consumed(("xn",), "pe", b.cnt["pe"])
            for j in range(22):
                b.consumed(("h", j), "pe", b.cnt["pe"])
        b.consumed(("lc_ffn", lsem), "pe", b.cnt["pe"])
        b.also_consumed(("lc_ffn", lsem), "dve")
        b.also_consumed(("lc_ffn", lsem), "act")

    xs_ap = dram["x_spill"].ap()
    x_rdy = b.cnt["dve"]
    b.wait("sync", "dve", x_rdy)
    for dc in range(NDC):
        emit(b, "sync", "spl", 16)(
            lambda eng, dc=dc: eng.dma_start(
                out=xs_ap[dc * 128:(dc + 1) * 128, :], in_=xr(dc)))
    spill_ok = b.cnt["spl"]

    for dc in range(NDC):
        emit(b, "vector", "dve")(
            lambda eng, dc=dc: eng.memset(yt(dc)[:, 0:HALO], 0.0))
    for st in range(NST):
        def cb(dc, sub, st=st):
            emit(b, "vector", "dve")(
                lambda eng, dc=dc, sub=sub, st=st: eng.tensor_copy(
                    out=yt(dc)[:, HALO + st * ST:HALO + (st + 1) * ST],
                    in_=sub))
        emit_ln(xr, st * ST, ST, cb, ("dve", x_rdy))
    b.wait("tensor", "spl", spill_ok)
    b.wait("vector", "spl", spill_ok)
    b.wait("scalar", "spl", spill_ok)
    b.wait("gpsimd", "spl", spill_ok)

    nsub = [0]

    def maybe_stop():
        nsub[0] += 1
        if (STOP_AFTER == f"ysub:{nsub[0]}"
                or (STOP_AFTER in ("fg0", "fg1") and nsub[0] == 1)):
            b.wait("vector", "act", b.cnt["act"])
            for dc in range(NDC):
                emit(b, "vector", "dve")(
                    lambda eng, dc=dc: eng.tensor_copy(
                        out=xr(dc), in_=yt(dc)[:, HALO:YW]))
            return True
        return False

    phase_barrier("phase")
    for li in range(6):
        attn_sub(li)
        phase_barrier("phase")
        if maybe_stop():
            return
        ffn_sub(li)
        phase_barrier("phase")
        if maybe_stop():
            return

    b.wait("sync", "pe", b.cnt["pe"])
    b.wait("sync", "dve", b.cnt["dve"])
    b.wait("sync", "act", b.cnt["act"])
    for dc in range(NDC):
        emit(b, "sync", "spl", 16)(
            lambda eng, dc=dc: eng.dma_start(
                out=xr(dc), in_=xs_ap[dc * 128:(dc + 1) * 128, :]))
    rst_ok = b.cnt["spl"]
    b.wait("vector", "spl", rst_ok)
    b.wait("vector", "act", b.cnt["act"])
    for dc in range(NDC):
        emit(b, "vector", "dve")(
            lambda eng, dc=dc: eng.tensor_tensor(
                out=xr(dc), in0=xr(dc), in1=yt(dc)[:, HALO:YW], op=ALU.add))





def mamba_block(b, ti, sb, PS, AY, AXa, AW, AS, dram, xr, emit_ln, exchange_x):
    nc = b.nc
    AXb = AXa.bitcast(BF16)
    AYb = AY.bitcast(BF16)
    AWb = AW.bitcast(BF16)
    ASb = AS.bitcast(BF16)
    NCH = 16
    TC = 512

    y0m = lambda dc: AYb[:, dc * T:(dc + 1) * T]
    u_bf = lambda gc: AYb[:, 16384 + gc * TC:16384 + (gc + 1) * TC]
    du_bf = lambda gc: AYb[:, 24576 + gc * TC:24576 + (gc + 1) * TC]
    h_bf = lambda dc: AYb[:, 32768 + dc * TC:32768 + (dc + 1) * TC]

    delta_f = lambda gc: AXa[:, gc * TC:(gc + 1) * TC]

    xin_t = AS[:, 0:516]
    dt_bf = ASb[:, 1152:1664]
    tails = AS[:, 832:880]
    Bc = AS[:, 1024:1536]
    Cc = AS[:, 2048:2560]
    sp_t = AS[:, 2560:3072]
    m_mu = AS[0:1, 3584:4096]
    m_var = AS[0:1, 1536:2048]
    m_sd = AS[0:1, 3072:3584]

    h_halo = ASb[:, 1032:1152]
    xh = AS[:, 880:1024]
    c1_lng = AS[:, 880:888]
    c1_lnb = AS[:, 888:896]
    c1_cw = AS[:, 896:960]
    c1_cb = AS[:, 960:976]
    c1_bdt = AS[:, 976:992]

    wsrc_in = ti["m_win"].ap()
    wsrc_xp = ti["m_wxp"].ap()
    wsrc_dt = ti["m_wdt"].ap()
    wsrc_out = ti["m_wout"].ap()

    w_inx = lambda kc: AWb[:, kc * 2048:(kc + 1) * 2048]
    w_xp = lambda gc: AWb[:, 16384 + gc * 96:16384 + (gc + 1) * 96]
    w_dt = AWb[0:64, 17920:19968]
    w_inz = lambda kc: AWb[:, kc * 2048:(kc + 1) * 2048]
    w_out_t = lambda gc, oc: AXb[:, 16384 + (gc * 8 + oc) * 128:
                                 16384 + (gc * 8 + oc) * 128 + 128]

    xs_ap = dram["x_spill"].ap()
    dl_in, dl_out = dram["a2a_dl_in"].ap(), dram["a2a_dl_out"].ap()
    duio = (dram["a2a_du_in"].ap(), dram["a2a_du_out"].ap())
    bc_in, bc_out = dram["bc_in"].ap(), dram["bc_out"].ap()
    y_in, y_out = dram["a2a_y_in"].ap(), dram["a2a_y_out"].ap()

    hx_ok = exchange_x(xh)

    if MTAP:
        tap(b, xh[:, 0:64])
        tap(b, xh[:, 64:144])

    for st in range(NST):
        def cb(dc, sub, st=st):
            emit(b, "vector", "dve")(
                lambda eng, dc=dc, sub=sub, st=st: eng.tensor_copy(
                    out=y0m(dc)[:, st * ST:(st + 1) * ST], in_=sub))
        emit_ln(xr, st * ST, ST, cb, ("dve", b.cnt["dve"]),
                stats=(m_mu, m_var, m_sd))
    y0m_ok = b.cnt["dve"]

    y0h = ASb[:, 7296:7416]
    b.wait("vector", "dve", hx_ok)

    def xh_t(dc):
        return xh[:, dc * 8:dc * 8 + 64]

    def cbh(dc, sub):
        emit(b, "vector", "dve")(
            lambda eng, dc=dc, sub=sub: eng.tensor_copy(
                out=y0h[:, dc * 8:dc * 8 + 64], in_=sub))
    emit_ln(xh_t, 0, 8, cbh, ("dve", b.cnt["dve"]), stats=(m_mu, m_var, m_sd))


    b.wait("sync", "dve", b.cnt["dve"])
    b.wait("sync", "act", b.cnt["act"])
    b.wait("sync", "pe", b.cnt["pe"])
    for dst, iname in ((c1_lng, "m_lng"), (c1_lnb, "m_lnb"),
                       (c1_cw, "m_convw"), (c1_cb, "m_convb"),
                       (c1_bdt, "m_bdt")):
        emit(b, "sync", "mla", 16)(
            lambda eng, dst=dst, iname=iname: eng.dma_start(
                out=dst, in_=ti[iname].ap()))
    c1_ok = b.cnt["mla"]
    b.wait("vector", "mla", c1_ok)
    b.wait("scalar", "mla", c1_ok)

    if MTAP:
        tap(b, y0h[:, 0:64])

    def y0h_t(dc):
        return y0h[:, dc * 8:dc * 8 + 64]

    def cbh1(dc, sub):
        emit(b, "vector", "dve")(
            lambda eng, dc=dc, sub=sub: eng.tensor_scalar(
                out=h_halo[:, dc * 8:dc * 8 + 64], in0=sub,
                scalar1=c1_lng[:, dc:dc + 1],
                scalar2=c1_lnb[:, dc:dc + 1],
                op0=ALU.mult, op1=ALU.add))
    emit_ln(y0h_t, 0, 8, cbh1, ("dve", b.cnt["dve"]),
            stats=(m_mu, m_var, m_sd), bf_in=True)
    hh_ok = b.cnt["dve"]

    if MTAP:
        tap(b, AS[:, 2560:2568])
        tap(b, AS[:, 3072:3080])
        tap(b, y0h[:, 0:64])
        tap(b, h_halo[:, 0:64])
        tap(b, AS[:, 880:896])

    b.wait("sync", "dve", y0m_ok)
    b.use("sync", "xspill2")
    for dc in range(NDC):
        emit(b, "sync", "spl", 16)(
            lambda eng, dc=dc: eng.dma_start(
                out=xs_ap[dc * 128:(dc + 1) * 128, :], in_=xr(dc)))
    spill_ok = b.cnt["spl"]
    for e in ("tensor", "vector", "scalar", "gpsimd"):
        b.wait(e, "spl", spill_ok)

    b.phase = "m_m1"
    b.use("gpsimd", "aw_regions")
    for kc in range(NDC):
        emit(b, "gpsimd", "mwa", 16)(
            lambda eng, kc=kc: eng.dma_start(
                out=w_inx(kc),
                in_=wsrc_in[kc * 128:(kc + 1) * 128, 0:2048]))
    for gc in range(NCH):
        emit(b, "gpsimd", "mwa", 16)(
            lambda eng, gc=gc: eng.dma_start(
                out=w_xp(gc), in_=wsrc_xp[gc * 128:(gc + 1) * 128, :]))
    emit(b, "gpsimd", "mwa", 16)(
        lambda eng: eng.dma_start(out=w_dt, in_=wsrc_dt))
    mw_ok = b.cnt["mwa"]
    b.wait("tensor", "mwa", mw_ok)

    for c in range(4):
        t0 = c * TC
        def cb1(dc, sub, c=c):
            emit(b, "vector", "dve")(
                lambda eng, dc=dc, sub=sub: eng.tensor_scalar(
                    out=h_bf(dc), in0=sub,
                    scalar1=c1_lng[:, dc:dc + 1],
                    scalar2=c1_lnb[:, dc:dc + 1],
                    op0=ALU.mult, op1=ALU.add))
        b.use("vector", ("mh",))
        emit_ln(y0m, t0, TC, cb1, ("dve", b.cnt["dve"]),
                stats=(m_mu, m_var, m_sd), bf_in=True)
        h_ok = b.cnt["dve"]
        b.wait("tensor", "dve", h_ok)

        b.use("tensor", ("bank", 6))
        for gc in range(NCH):
            bk = PS[gc % 2]
            bh = PS[2 + gc % 2]
            b.use("tensor", ("bank", gc % 2))
            b.use("tensor", ("bank", 2 + gc % 2))
            if c == 0:
                b.wait("tensor", "dve", hh_ok)
                for kc in range(NDC):
                    emit(b, "tensor", "pe")(
                        lambda eng, bh=bh, kc=kc, gc=gc: eng.matmul(
                            bh[:, 0:3],
                            w_inx(kc)[:, gc * 128:(gc + 1) * 128],
                            h_halo[:, kc * 8 + 5:kc * 8 + 8],
                            start=(kc == 0), stop=(kc == NDC - 1)))
            for kc in range(NDC):
                emit(b, "tensor", "pe")(
                    lambda eng, bk=bk, kc=kc, gc=gc, t0=t0: eng.matmul(
                        bk[:, 0:TC],
                        w_inx(kc)[:, gc * 128:(gc + 1) * 128],
                        y0m(kc)[:, t0:t0 + TC] if False else h_bf(kc),
                        start=(kc == 0), stop=(kc == NDC - 1)))
            mm_ok = b.cnt["pe"]
            b.wait("scalar", "pe", mm_ok)
            b.use("scalar", ("xin",))
            if c == 0:
                emit(b, "scalar", "act")(
                    lambda eng, bh=bh: eng.activation(
                        out=xin_t[:, 0:3], in_=bh[:, 0:3],
                        func=AF.Identity, scale=sb["c_hv"][:, 0:1]))
            else:
                b.wait("scalar", "dve", b.cnt["dve"])
                emit(b, "scalar", "act")(
                    lambda eng, gc=gc: eng.copy(out=xin_t[:, 0:3],
                                                in_=tails[:, gc * 3:gc * 3 + 3]))
            xe = emit(b, "scalar", "act")(
                lambda eng, bk=bk: eng.copy(out=xin_t[:, 3:515], in_=bk[:, 0:TC]))
            b.consumed(("bank", gc % 2), "act", xe)
            b.consumed(("bank", 2 + gc % 2), "act", xe)
            b.wait("vector", "act", xe)
            emit(b, "vector", "dve")(
                lambda eng, gc=gc: eng.tensor_copy(
                    out=tails[:, gc * 3:gc * 3 + 3], in_=xin_t[:, 512:515]))
            b.use("vector", ("sp",))
            emit(b, "vector", "dve")(
                lambda eng, gc=gc: eng.tensor_scalar(
                    out=sp_t, in0=xin_t[:, 0:TC],
                    scalar1=c1_cw[:, gc * 4:gc * 4 + 1], scalar2=None,
                    op0=ALU.mult))
            for k in range(1, 4):
                emit(b, "vector", "dve")(
                    lambda eng, gc=gc, k=k: eng.scalar_tensor_tensor(
                        out=sp_t, in0=xin_t[:, k:k + TC],
                        scalar=c1_cw[:, gc * 4 + k:gc * 4 + k + 1],
                        in1=sp_t, op0=ALU.mult, op1=ALU.add))
            cv_ok = b.cnt["dve"]
            b.wait("scalar", "dve", cv_ok)
            b.use("scalar", ("ubf", gc))
            ue = emit(b, "scalar", "act")(
                lambda eng, gc=gc: eng.activation(
                    out=u_bf(gc), in_=sp_t, func=AF.Silu,
                    bias=c1_cb[:, gc:gc + 1]))
            b.consumed(("sp",), "act", ue)
            if MTAP and c == 0 and gc == 0:
                tap(b, xin_t[:, 0:16])
                tap(b, u_bf(0)[:, 0:16])
            b.wait("tensor", "act", ue)
            emit(b, "tensor", "pe")(
                lambda eng, gc=gc: eng.matmul(
                    PS[6][0:96, 0:TC], w_xp(gc), u_bf(gc),
                    start=(gc == 0), stop=(gc == NCH - 1)))
        dbc_ok = b.cnt["pe"]
        b.wait("scalar", "pe", dbc_ok)
        b.use("scalar", ("dtbc",))
        emit(b, "scalar", "act")(
            lambda eng: eng.copy(out=dt_bf[0:64, :], in_=PS[6][0:64, 0:TC]))
        e2 = emit(b, "scalar", "act")(
            lambda eng: eng.copy(out=Bc[0:32, :], in_=PS[6][64:96, 0:TC]))
        b.consumed(("bank", 6), "act", e2)
        sem_m = "msa" if c % 2 == 0 else "msb"
        b.use("sync", ("mst", sem_m))
        b.wait("sync", "act", e2)
        emit(b, "sync", sem_m, 16)(
            lambda eng, c=c: eng.dma_start(
                out=bc_in[0:16, c * TC:(c + 1) * TC], in_=Bc[0:16, :]))
        emit(b, "sync", sem_m, 16)(
            lambda eng, c=c: eng.dma_start(
                out=bc_in[16:32, c * TC:(c + 1) * TC], in_=Bc[16:32, :]))
        b.wait("tensor", "act", e2)
        for gc in range(NCH):
            bk = PS[4 + gc % 2]
            b.use("tensor", ("bank", 4 + gc % 2))
            emit(b, "tensor", "pe")(
                lambda eng, gc=gc: eng.matmul(
                    bk[:, 0:TC] if False else PS[4 + gc % 2][:, 0:TC],
                    w_dt[:, gc * 128:(gc + 1) * 128], dt_bf[0:64, :],
                    start=True, stop=True))
            dmm = b.cnt["pe"]
            b.wait("scalar", "pe", dmm)
            b.use("scalar", ("mdelta", gc))
            emit(b, "scalar", "act")(
                lambda eng, gc=gc: eng.activation(
                    out=delta_f(gc), in_=PS[4 + gc % 2][:, 0:TC],
                    func=AF.Identity, bias=c1_bdt[:, gc:gc + 1]))
            b.consumed(("bank", 4 + gc % 2), "act")
            b.use("scalar", ("sp",))
            emit(b, "scalar", "act")(
                lambda eng, gc=gc: eng.activation(
                    out=sp_t, in_=delta_f(gc), func=AF.Abs))
            emit(b, "scalar", "act")(
                lambda eng: eng.activation(out=sp_t, in_=sp_t, func=AF.Exp,
                                           scale=-1.0))
            sp3 = emit(b, "scalar", "act")(
                lambda eng: eng.activation(out=sp_t, in_=sp_t, func=AF.Ln,
                                           bias=sb["c_cvec"][:, 0:1]))
            b.wait("vector", "act", sp3)
            dl = emit(b, "vector", "dve")(
                lambda eng, gc=gc: eng.scalar_tensor_tensor(
                    out=delta_f(gc), in0=delta_f(gc), scalar=0.0, in1=sp_t,
                    op0=ALU.max, op1=ALU.add))
            b.consumed(("sp",), "dve", dl)
            b.consumed(("mdelta", gc), "dve", dl)
            b.use("vector", ("dubf", gc))
            emit(b, "vector", "dve")(
                lambda eng, gc=gc: eng.tensor_tensor(
                    out=du_bf(gc), in0=delta_f(gc), in1=u_bf(gc), op=ALU.mult))
        dd_ok = b.cnt["dve"]
        b.wait("sync", "dve", dd_ok)
        for gc in range(NCH):
            emit(b, "sync", sem_m, 16)(
                lambda eng, gc=gc, c=c: eng.dma_start(
                    out=dl_in[gc * 128:(gc + 1) * 128, c * TC:(c + 1) * TC],
                    in_=delta_f(gc)))
        sem_u = "mua" if c % 2 == 0 else "mub"
        b.use("gpsimd", ("mstu", sem_u))
        b.wait("gpsimd", "dve", dd_ok)
        for gc in range(NCH):
            emit(b, "gpsimd", sem_u, 16)(
                lambda eng, gc=gc, c=c: eng.dma_start(
                    out=duio[0][gc * 128:(gc + 1) * 128, c * TC:(c + 1) * TC],
                    in_=du_bf(gc)))
        for gc in range(NCH):
            emit(b, "gpsimd", sem_u, 16)(
                lambda eng, gc=gc, c=c: eng.dma_start(
                    out=dram["u_spill"].ap()[gc * 128:(gc + 1) * 128,
                                             c * TC:(c + 1) * TC],
                    in_=u_bf(gc)))
        b.consumed(("mst", sem_m), sem_m, b.cnt[sem_m])
        b.consumed(("mstu", sem_u), sem_u, b.cnt[sem_u])
        b.consumed(("mh",), "pe", b.cnt["pe"])
        for gc in range(NCH):
            b.consumed(("ubf", gc), sem_u, b.cnt[sem_u])
            b.also_consumed(("ubf", gc), "dve", b.cnt["dve"])
            b.also_consumed(("ubf", gc), "pe", b.cnt["pe"])
            b.consumed(("dubf", gc), sem_u, b.cnt[sem_u])
            b.consumed(("mdelta", gc), sem_m, b.cnt[sem_m])
        b.consumed(("xin",), "dve", b.cnt["dve"])
        b.consumed(("dtbc",), sem_m, b.cnt[sem_m])

    b.phase = "m_m2"
    from concourse.aot_env import allow_ici_all_to_all

    def a2a(eng, iin, oout):
        with allow_ici_all_to_all():
            return eng.collective_compute(
                "AllToAll", ALU.bypass, replica_groups=[list(range(NCORES))],
                ins=[iin.opt()], outs=[oout.opt()])

    b.wait("gpsimd", "msa", b.cnt["msa"])
    b.wait("gpsimd", "msb", b.cnt["msb"])
    b.wait("gpsimd", "mua", b.cnt["mua"])
    b.wait("gpsimd", "mub", b.cnt["mub"])
    b.wait("gpsimd", "cc", b.cnt["cc"])
    c1 = emit(b, "gpsimd", "cc")(lambda eng: a2a(eng, dl_in, dl_out))
    b.wait("gpsimd", "cc", c1)
    c2 = emit(b, "gpsimd", "cc")(lambda eng: a2a(eng, duio[0], duio[1]))
    b.wait("gpsimd", "cc", c2)
    c3 = emit(b, "gpsimd", "cc")(
        lambda eng: eng.collective_compute(
            "AllGather", ALU.bypass, replica_groups=[list(range(NCORES))],
            ins=[bc_in.opt()], outs=[bc_out.opt()]))
    b.wait("sync", "cc", c3)

    b.phase = "m_m3"
    m_dl = lambda g, par: AXa[:, par * 2048 + g * TC:
                              par * 2048 + (g + 1) * TC]
    m_du = lambda g, par: AXb[:, 8192 + par * 2048 + g * TC:
                              8192 + par * 2048 + (g + 1) * TC]
    m_bcs = lambda par: AXa[:, 1024 + par * 2048:1536 + par * 2048]
    m_sel = AXb[:, 12288:14336]
    m_y = lambda g: AXa[:, 7168 + g * TC:7168 + (g + 1) * TC]
    m_carry = AXa[:, 1536:1568]
    m_dbu = AS[:, 2048:2560]
    m_s = AS[:, 256:768]
    m_sc = AS[:, 768:1280]
    c_cA = AS[:, 1536:1568]
    dA_t = lambda par: AS[:, 2560 + par * 512:2560 + (par + 1) * 512]

    b.use("sync", ("m3sel",))
    emit(b, "sync", "mlc", 16)(
        lambda eng: eng.dma_start(out=m_sel[0:16, :], in_=ti["m_sel"].ap()))
    b.wait("sync", "dve", b.cnt["dve"])
    b.wait("sync", "act", b.cnt["act"])
    emit(b, "sync", "mlc", 16)(
        lambda eng: eng.dma_start(out=c_cA, in_=ti["m_alog"].ap()))
    ca_ld = b.cnt["mlc"]
    b.wait("scalar", "mlc", ca_ld)
    emit(b, "scalar", "act")(
        lambda eng: eng.activation(out=c_cA, in_=c_cA, func=AF.Exp))
    b.wait("vector", "act", b.cnt["act"])
    ca_c = emit(b, "vector", "dve")(
        lambda eng: eng.tensor_scalar(out=c_cA, in0=c_cA,
                                      scalar1=-1.0, scalar2=None, op0=ALU.mult))
    b.wait("scalar", "dve", ca_c)
    b.wait("vector", "cc", c2)
    emit(b, "vector", "dve")(
        lambda eng: eng.memset(m_carry, 0.0))
    sel_ok = b.cnt["mlc"]
    b.wait("tensor", "mlc", sel_ok)

    for sch in range(32):
        r, lc, par = sch // 4, (sch % 4) * TC, sch % 2
        lsem2 = "mla" if par == 0 else "mlb"
        b.use("sync", ("m3ld", lsem2))
        b.wait("sync", "cc", c2)
        for g in range(2):
            emit(b, "sync", lsem2, 16)(
                lambda eng, r=r, lc=lc, g=g, par=par: eng.dma_start(
                    out=m_dl(g, par),
                    in_=dl_out[r * 256 + g * 128:r * 256 + (g + 1) * 128,
                               lc:lc + TC]))
        for g in range(2):
            emit(b, "sync", lsem2, 16)(
                lambda eng, r=r, lc=lc, g=g, par=par: eng.dma_start(
                    out=m_du(g, par),
                    in_=duio[1][r * 256 + g * 128:r * 256 + (g + 1) * 128,
                                lc:lc + TC]))
        b.wait("sync", "cc", c3)
        emit(b, "sync", lsem2, 16)(
            lambda eng, r=r, lc=lc, par=par: eng.dma_start(
                out=m_bcs(par)[0:16, :],
                in_=bc_out[r * 32:r * 32 + 16, lc:lc + TC]))
        emit(b, "sync", lsem2, 16)(
            lambda eng, r=r, lc=lc, par=par: eng.dma_start(
                out=m_bcs(par)[32:48, :],
                in_=bc_out[r * 32 + 16:r * 32 + 32, lc:lc + TC]))
        ld_ok = (lsem2, b.cnt[lsem2])
        b.wait("tensor", *ld_ok)
        b.wait("vector", *ld_ok)
        b.wait("scalar", *ld_ok)
        b.use("vector", ("m3y",))
        for g in range(2):
            emit(b, "vector", "dve")(
                lambda eng, g=g: eng.memset(m_y(g), 0.0))
        for n in range(16):
            bcbB = ASb[0:16, 2560:3072]
            bcbC = ASb[0:16, 3136:3648]
            if n == 0:
                b.use("vector", ("m3bc",))
                cv = emit(b, "vector", "dve")(
                    lambda eng, bcbB=bcbB, par=par: eng.tensor_copy(
                        out=bcbB[:, 0:TC], in_=m_bcs(par)[0:16, :]))
                cv = emit(b, "vector", "dve")(
                    lambda eng, bcbC=bcbC, par=par: eng.tensor_copy(
                        out=bcbC[:, 0:TC], in_=m_bcs(par)[32:48, :]))
                b.wait("tensor", "dve", cv)
            b.use("tensor", ("bank", 6))
            b.use("tensor", ("bank", 7))
            emit(b, "tensor", "pe")(
                lambda eng, n=n, bcbB=bcbB: eng.matmul(
                    PS[6][:, 0:TC], m_sel[0:16, n * 128:(n + 1) * 128],
                    bcbB[:, 0:TC], start=True, stop=True))
            rep_ok = emit(b, "tensor", "pe")(
                lambda eng, n=n, bcbC=bcbC: eng.matmul(
                    PS[7][:, 0:TC], m_sel[0:16, n * 128:(n + 1) * 128],
                    bcbC[:, 0:TC], start=True, stop=True))
            for g in range(2):
                gn = g * 16 + n
                abk = dA_t(n % 2)
                b.use("scalar", ("m3da", n % 2))
                da = emit(b, "scalar", "act")(
                    lambda eng, g=g, gn=gn, abk=abk, par=par: eng.activation(
                        out=abk[:, 0:TC], in_=m_dl(g, par), func=AF.Exp,
                        scale=c_cA[:, gn:gn + 1]))
                b.wait("vector", "pe", rep_ok)
                b.use("vector", ("m3t",))
                emit(b, "vector", "dve")(
                    lambda eng, g=g, par=par: eng.tensor_tensor(
                        out=m_dbu, in0=m_du(g, par), in1=PS[6][:, 0:TC],
                        op=ALU.mult))
                b.wait("vector", "act", da)
                sc_i = emit(b, "vector", "dve")(
                    lambda eng, abk=abk, gn=gn: eng.tensor_tensor_scan(
                        out=m_s, data0=abk[:, 0:TC], data1=m_dbu,
                        initial=m_carry[:, gn:gn + 1],
                        op0=ALU.mult, op1=ALU.add))
                b.consumed(("m3da", n % 2), "dve", sc_i)
                b.wait("scalar", "dve", sc_i)
                cw_i = emit(b, "scalar", "act")(
                    lambda eng, gn=gn: eng.copy(
                        out=m_carry[:, gn:gn + 1], in_=m_s[:, TC - 1:TC]))
                b.wait("vector", "act", cw_i)
                emit(b, "vector", "dve")(
                    lambda eng: eng.tensor_tensor(
                        out=m_sc, in0=m_s, in1=PS[7][:, 0:TC], op=ALU.mult))
                ya = emit(b, "vector", "dve")(
                    lambda eng, g=g: eng.tensor_tensor(
                        out=m_y(g), in0=m_y(g), in1=m_sc, op=ALU.add))
            b.consumed(("bank", 6), "dve", b.cnt["dve"])
            b.consumed(("bank", 7), "dve", b.cnt["dve"])
        ysem = "msa" if par == 0 else "msb"
        dma_y = b.cnt["dve"]
        b.wait("sync", "dve", dma_y)
        b.use("sync", ("m3ys", ysem))
        for g in range(2):
            emit(b, "sync", ysem, 16)(
                lambda eng, r=r, lc=lc, g=g: eng.dma_start(
                    out=y_in[r * 256 + g * 128:r * 256 + (g + 1) * 128,
                             lc:lc + TC],
                    in_=m_y(g)))
        b.consumed(("m3ys", ysem), ysem, b.cnt[ysem])
        b.consumed(("m3ld", lsem2), "dve", b.cnt["dve"])
        b.also_consumed(("m3ld", lsem2), "act", b.cnt["act"])
        b.consumed(("m3y",), ysem, b.cnt[ysem])
        b.consumed(("m3bc",), "pe", b.cnt["pe"])
        b.consumed(("m3t",), "dve", b.cnt["dve"])
    b.consumed(("m3sel",), "pe", b.cnt["pe"])

    b.wait("gpsimd", "mla", b.cnt["mla"])
    b.wait("gpsimd", "mlb", b.cnt["mlb"])
    b.wait("gpsimd", "msa", b.cnt["msa"])
    b.wait("gpsimd", "msb", b.cnt["msb"])
    b.wait("gpsimd", "cc", b.cnt["cc"])
    c4 = emit(b, "gpsimd", "cc")(lambda eng: a2a(eng, y_in, y_out))

    b.phase = "m_m5"
    b.use("gpsimd", "aw_regions")
    b.wait("gpsimd", "pe", b.cnt["pe"])
    for kc in range(NDC):
        emit(b, "gpsimd", "mwb", 16)(
            lambda eng, kc=kc: eng.dma_start(
                out=w_inz(kc),
                in_=wsrc_in[kc * 128:(kc + 1) * 128, 2048:4096]))
    for gc in range(NCH):
        for oc in range(NDC):
            emit(b, "gpsimd", "mwb", 16)(
                lambda eng, gc=gc, oc=oc: eng.dma_start(
                    out=w_out_t(gc, oc),
                    in_=wsrc_out[gc * 128:(gc + 1) * 128,
                                 oc * 128:(oc + 1) * 128]))
    mw2_ok = b.cnt["mwb"]
    b.wait("tensor", "mwb", mw2_ok)

    b.wait("sync", "pe", b.cnt["pe"])
    b.wait("sync", "dve", b.cnt["dve"])
    b.wait("sync", "act", b.cnt["act"])
    for s in ("msa", "msb", "mla", "mlb", "mwa", "mua", "mub"):
        b.wait("sync", s, b.cnt[s])
    for dc in range(4):
        emit(b, "sync", "spl", 16)(
            lambda eng, dc=dc: eng.dma_start(
                out=xr(dc), in_=xs_ap[dc * 128:(dc + 1) * 128, :]))
    rst_ok = b.cnt["spl"]
    b.wait("vector", "spl", rst_ok)
    b.wait("tensor", "spl", rst_ok)
    b.wait("scalar", "spl", rst_ok)

    for e in ("tensor", "vector", "scalar", "sync", "gpsimd"):
        b.wait(e, "dve", b.cnt["dve"])
        b.wait(e, "pe", b.cnt["pe"])
    park = lambda dc: AYb[:, 16384 + (dc - 4) * T:16384 + (dc - 3) * T]
    y3b = lambda gc: AYb[:, 24576 + gc * TC:24576 + (gc + 1) * TC]
    y2l = AS[:, 0:512]
    y2l2 = AS[:, 512:1024]
    zs_t = AS[:, 1280:1792]
    ub5a = ASb[:, 2048:2560]
    ub5b = ASb[:, 4096:4608]
    c5_dp = AS[:, 2304:2320]
    c5_lng = AS[:, 2320:2328]
    c5_lnb = AS[:, 2328:2336]

    def ub5(gc, c):
        return ub5a if gc % 2 == 0 else ub5b

    for dst, iname in ((c5_dp, "m_dparam"), (c5_lng, "m_lng"),
                       (c5_lnb, "m_lnb")):
        emit(b, "sync", "mlc", 16)(
            lambda eng, dst=dst, iname=iname: eng.dma_start(
                out=dst, in_=ti[iname].ap()))
    c5_ok = b.cnt["mlc"]
    b.wait("vector", "mlc", c5_ok)
    b.wait("scalar", "mlc", c5_ok)

    for c in range(4):
        t0 = c * TC
        def cb6(dc, sub, c=c):
            emit(b, "vector", "dve")(
                lambda eng, dc=dc, sub=sub: eng.tensor_scalar(
                    out=h_bf(dc), in0=sub,
                    scalar1=c5_lng[:, dc:dc + 1],
                    scalar2=c5_lnb[:, dc:dc + 1],
                    op0=ALU.mult, op1=ALU.add))
        b.use("vector", ("mh",))
        emit_ln(y0m, t0, TC, cb6, ("dve", b.cnt["dve"]),
                stats=(m_mu, m_var, m_sd), bf_in=True)
        h5_ok = b.cnt["dve"]
        b.wait("tensor", "dve", h5_ok)
        b.wait("sync", "cc", c4)
        for gc in range(NCH):
            bk = PS[gc % 2]
            b.use("tensor", ("bank", gc % 2))
            for kc in range(NDC):
                emit(b, "tensor", "pe")(
                    lambda eng, bk=bk, kc=kc, gc=gc: eng.matmul(
                        bk[:, 0:TC],
                        w_inz(kc)[:, gc * 128:(gc + 1) * 128], h_bf(kc),
                        start=(kc == 0), stop=(kc == NDC - 1)))
            zmm = b.cnt["pe"]
            ybuf = y2l if gc % 2 == 0 else y2l2
            sem5 = "msa" if gc % 2 == 0 else "msb"
            b.use("sync", ("y2l", gc % 2))
            emit(b, "sync", sem5, 16)(
                lambda eng, gc=gc, c=c, ybuf=ybuf: eng.dma_start(
                    out=ybuf, in_=y_out[gc * 128:(gc + 1) * 128,
                                        c * TC:(c + 1) * TC]))
            emit(b, "sync", sem5, 16)(
                lambda eng, gc=gc, c=c: eng.dma_start(
                    out=ub5(gc, c),
                    in_=dram["u_spill"].ap()[gc * 128:(gc + 1) * 128,
                                             c * TC:(c + 1) * TC]))
            yld = (sem5, b.cnt[sem5])
            b.wait("scalar", "pe", zmm)
            b.use("scalar", ("zst",))
            zs = emit(b, "scalar", "act")(
                lambda eng, bk=bk: eng.activation(out=zs_t, in_=bk[:, 0:TC],
                                                  func=AF.Silu))
            b.consumed(("bank", gc % 2), "act", zs)
            b.wait("vector", "act", zs)
            b.wait("vector", *yld)
            b.use("vector", ("y3", gc))
            emit(b, "vector", "dve")(
                lambda eng, gc=gc, ybuf=ybuf, c=c: eng.scalar_tensor_tensor(
                    out=ybuf, in0=ub5(gc, c), scalar=c5_dp[:, gc:gc + 1],
                    in1=ybuf, op0=ALU.mult, op1=ALU.add))
            y3c = emit(b, "vector", "dve")(
                lambda eng, gc=gc, ybuf=ybuf: eng.tensor_tensor(
                    out=y3b(gc), in0=ybuf, in1=zs_t, op=ALU.mult))
            b.consumed(("y2l", gc % 2), "dve", y3c)
            b.consumed(("zst",), "dve", y3c)
        y3_ok = b.cnt["dve"]
        b.wait("tensor", "dve", y3_ok)
        for oc in range(NDC):
            bk = PS[2 + oc % 2]
            b.use("tensor", ("bank", 2 + oc % 2))
            for gc in range(NCH):
                emit(b, "tensor", "pe")(
                    lambda eng, bk=bk, gc=gc, oc=oc: eng.matmul(
                        bk[:, 0:TC], w_out_t(gc, oc), y3b(gc),
                        start=(gc == 0), stop=(gc == NCH - 1)))
            omm = b.cnt["pe"]
            b.wait("vector", "pe", omm)
            if oc < 4:
                emit(b, "vector", "dve")(
                    lambda eng, bk=bk, oc=oc, t0=t0: eng.tensor_tensor(
                        out=xr(oc)[:, t0:t0 + TC], in0=xr(oc)[:, t0:t0 + TC],
                        in1=bk[:, 0:TC], op=ALU.add))
                xad = emit(b, "vector", "dve")(
                    lambda eng, oc=oc, t0=t0: eng.tensor_tensor(
                        out=xr(oc)[:, t0:t0 + TC], in0=xr(oc)[:, t0:t0 + TC],
                        in1=y0m(oc)[:, t0:t0 + TC], op=ALU.add))
            else:
                xad = emit(b, "vector", "dve")(
                    lambda eng, bk=bk, oc=oc, t0=t0: eng.tensor_tensor(
                        out=park(oc)[:, t0:t0 + TC], in0=bk[:, 0:TC],
                        in1=y0m(oc)[:, t0:t0 + TC], op=ALU.add))
            b.consumed(("bank", 2 + oc % 2), "dve", xad)
        for gc in range(NCH):
            b.consumed(("y3", gc), "pe", b.cnt["pe"])
        b.consumed(("mh",), "pe", b.cnt["pe"])
    b.consumed("aw_regions", "pe", b.cnt["pe"])

    b.wait("sync", "pe", b.cnt["pe"])
    b.wait("sync", "dve", b.cnt["dve"])
    for dc in range(4, NDC):
        emit(b, "sync", "spl", 16)(
            lambda eng, dc=dc: eng.dma_start(
                out=xr(dc), in_=xs_ap[dc * 128:(dc + 1) * 128, :]))
    rst2 = b.cnt["spl"]
    b.wait("vector", "spl", rst2)
    for dc in range(4, NDC):
        emit(b, "vector", "dve")(
            lambda eng, dc=dc: eng.tensor_tensor(
                out=xr(dc), in0=xr(dc), in1=park(dc), op=ALU.add))
    b.wait("tensor", "spl", rst2)
    b.wait("scalar", "spl", rst2)



def _ffn_host_prep(inputs):
    bf = ml_dtypes.bfloat16
    W1 = np.asarray(inputs["L0_ffn_W1"])
    W2 = np.asarray(inputs["L0_ffn_W2"])
    b1 = np.asarray(inputs["L0_ffn_b1"])
    b2 = np.asarray(inputs["L0_ffn_b2"])
    lng = np.asarray(inputs["L0_ffn_ln_g"])
    lnb = np.asarray(inputs["L0_ffn_ln_b"])
    qs = np.asarray(inputs["L0_ffn_q_scale"])
    ksc = np.asarray(inputs["L0_ffn_k_scale"])

    a = np.zeros((6, 1024, 2816), np.float32)
    g = np.zeros((6, 1024, 2816), np.float32)
    a[:, :, :2730] = W1[:, :, :2730]
    g[:, :, :2730] = W1[:, :, 2730:]
    f_w1p = np.concatenate([a, g], axis=2).reshape(6 * 1024, 5632)

    w2p = np.zeros((6, 2816, 1024), np.float32)
    w2p[:, :2730] = W2
    f_w2p = w2p.reshape(6 * 2816, 1024)

    b1a = np.zeros((6, 2816), np.float32)
    b1g = np.zeros((6, 2816), np.float32)
    b1a[:, :2730] = b1[:, :2730]
    b1g[:, :2730] = b1[:, 2730:]
    b1c = np.zeros((128, 6 * 44), np.float32)
    for li in range(6):
        for j in range(22):
            b1c[:, li * 44 + 2 * j] = b1a[li, j * 128:(j + 1) * 128]
            b1c[:, li * 44 + 2 * j + 1] = b1g[li, j * 128:(j + 1) * 128]

    def cols8(v):
        out = np.zeros((128, 48), np.float32)
        for li in range(6):
            for dc in range(8):
                out[:, li * 8 + dc] = v[li, dc * 128:(dc + 1) * 128]
        return out

    b2c, lngc, lnbc = cols8(b2), cols8(lng), cols8(lnb)

    qs2 = np.zeros((2, 6 * 128), np.float32)
    ks2 = np.zeros((2, 6 * 128), np.float32)
    for li in range(6):
        qs2[0, li * 128:li * 128 + 64] = qs[li]
        qs2[1, li * 128 + 64:li * 128 + 128] = qs[li]
        ks2[0, li * 128:li * 128 + 64] = ksc[li]
        ks2[1, li * 128 + 64:li * 128 + 128] = ksc[li]

    r64 = np.zeros((64, 64), np.float32)
    for q in range(32):
        r64[q + 32, q] = -1.0
        r64[q, q + 32] = 1.0
    rsign = np.zeros((128, 128), np.float32)
    rsign[0:64, 0:64] = r64
    rsign[64:128, 64:128] = r64

    NEGM = -3.0e36
    masks = np.zeros((4, 128, 512), np.float32)
    for v in range(2):
        for chunk in range(2):
            for il in range(128):
                i_loc = chunk * 128 + il
                jl = np.arange(512)
                keep = (jl >= i_loc) & (jl <= i_loc + 256)
                if v == 0:
                    keep &= (jl >= 256)
                masks[2 * v + chunk, il] = np.where(keep, 0.0, NEGM)
    masks = masks.reshape(4 * 128, 512).astype(bf)

    sel = np.zeros((16, 2048), np.float32)
    for n in range(16):
        sel[n, n * 128:(n + 1) * 128] = 1.0

    cw = np.asarray(inputs["L0_attn_b_conv_w"])[:, 0, :]
    m_convw = np.zeros((128, 64), np.float32)
    for gc in range(16):
        for k in range(4):
            m_convw[:, gc * 4 + k] = cw[gc * 128:(gc + 1) * 128, k]

    def cols16(v):
        out = np.zeros((128, 16), np.float32)
        for gc in range(16):
            out[:, gc] = v[gc * 128:(gc + 1) * 128]
        return out

    def cols8v(v):
        out = np.zeros((128, 8), np.float32)
        for dc in range(8):
            out[:, dc] = v[dc * 128:(dc + 1) * 128]
        return out

    mamba = {
        "m_win": np.ascontiguousarray(np.asarray(inputs["L0_attn_b_W_in"])),
        "m_wxp": np.ascontiguousarray(np.asarray(inputs["L0_attn_b_W_xproj"])),
        "m_wdt": np.ascontiguousarray(np.asarray(inputs["L0_attn_b_W_dt"])),
        "m_wout": np.ascontiguousarray(np.asarray(inputs["L0_attn_b_W_out"])),
        "m_dparam": cols16(np.asarray(inputs["L0_attn_b_D_param"])),
        "m_lng": cols8v(np.asarray(inputs["L0_attn_b_ln_g"])),
        "m_lnb": cols8v(np.asarray(inputs["L0_attn_b_ln_b"])),
        "m_convw": m_convw,
        "m_convb": cols16(np.asarray(inputs["L0_attn_b_conv_b"])),
        "m_bdt": cols16(np.asarray(inputs["L0_attn_b_b_dt"])),
        "m_sel": sel.astype(bf),
        "ones_col_bf": np.ones((128, 1)).astype(bf),
    }

    for k in ("m_win", "m_wxp", "m_wdt", "m_wout"):
        mamba[k] = mamba[k].astype(bf) if mamba[k].dtype != bf else mamba[k]
    return {
        **mamba,
        "f_wqkv": np.ascontiguousarray(
            np.asarray(inputs["L0_ffn_Wqkv"]).reshape(6 * 1024, 3072)
            .astype(bf)),
        "f_wo": np.ascontiguousarray(
            np.asarray(inputs["L0_ffn_Wo"]).reshape(6 * 1024, 1024)
            .astype(bf)),
        "f_w1t": np.ascontiguousarray(
            f_w1p.astype(bf).reshape(6, 8, 128, 2, 22, 128)
            .transpose(2, 0, 4, 1, 3, 5).reshape(128, 6 * 22 * 2048)),
        "f_w2t": np.ascontiguousarray(
            f_w2p.astype(bf).reshape(6, 22, 128, 8, 128)
            .transpose(2, 0, 3, 1, 4).reshape(128, 6 * 8 * 2816)),
        "b1c": b1c, "b2c": b2c, "lngc": lngc, "lnbc": lnbc,
        "qs2": qs2, "ks2": ks2,
        "rsign": rsign.astype(bf),
        "masks": masks,
    }


def _rope_tables(g):
    bf = ml_dtypes.bfloat16
    n = 16384
    half = np.arange(0, 64, 2, dtype=np.float64)
    inv_freq = 1.0 / 10000.0 ** (half / 64.0)
    base_scale = (half + 0.4 * 64.0) / (1.4 * 64.0)
    cols = np.arange(YW, dtype=np.float64)
    t = g * T + cols - HALO
    freqs = t[:, None] * inv_freq[None, :]
    freqs = np.concatenate([freqs, freqs], axis=1)
    power = (t - n // 2) / (256 // 2)
    scale = base_scale[None, :] ** power[:, None]
    scale = np.concatenate([scale, scale], axis=1)
    cs = np.cos(freqs) * scale * 8.0
    ss = np.sin(freqs) * scale * 8.0
    ci = np.cos(freqs) / scale
    si = np.sin(freqs) / scale
    out = np.zeros((4, 128, YW), np.float64)
    for ti, tab in enumerate((cs, ss, ci, si)):
        out[ti, 0:64] = tab.T
        out[ti, 64:128] = tab.T
    return np.ascontiguousarray(out.reshape(4 * 128, YW).astype(bf))


def _alog_slice(inputs, g):
    al = np.asarray(inputs["L0_attn_b_A_log"])
    out = np.zeros((128, 32), np.float32)
    for gcl in range(2):
        ch0 = g * 256 + gcl * 128
        out[:, gcl * 16:(gcl + 1) * 16] = al[ch0:ch0 + 128, :]
    return out


_PROG_CACHE = {}


def build_in_maps(inputs):
    x = np.asarray(inputs["L0_attn_a_x"])
    Wqkv_a = np.asarray(inputs["L0_attn_a_W_qkv"])
    ident_f = np.eye(128, dtype=np.float32)
    ident_b = np.eye(128).astype(ml_dtypes.bfloat16)
    ones_col = np.ones((128, 1), np.float32)
    ones_row = np.ones((1, 128), np.float32)
    blk2 = np.zeros((128, 2), np.float32)
    blk2[0:64, 0] = 1.0
    blk2[64:128, 1] = 1.0
    blk2T = np.ascontiguousarray(blk2.T)
    cvec = np.zeros((128, 16), np.float32)
    cvec[:, 0] = 1.0
    cvec[:, 1] = 1e-5
    aw_qkvT = np.ascontiguousarray(Wqkv_a.T.astype(ml_dtypes.bfloat16))

    shared = _ffn_host_prep(inputs)
    xs = x[0]
    in_maps = []
    for g in range(NCORES):
        m = {
            "x_own": np.ascontiguousarray(xs[g * T:(g + 1) * T]),
            "aw_qkvT": aw_qkvT,
            "ident_f": ident_f,
            "ident_b": ident_b,
            "ones_col": ones_col,
            "ones_row": ones_row,
            "blk2": blk2,
            "blk2T": blk2T,
            "cvec": cvec,
            "rope": _rope_tables(g),
            "hvalid": np.full((128, 1), 0.0 if g == 0 else 1.0, np.float32),
            "m_alog": _alog_slice(inputs, g),
        }
        m.update(shared)
        if g > 0:
            mk = shared["masks"].copy()
            mk[0:256] = mk[256:512]
            m["masks"] = mk
        in_maps.append(m)
    return in_maps


def kernel(**inputs):
    key = ("v1", STOP_AFTER)
    if key not in _PROG_CACHE:
        _PROG_CACHE[key] = build_program()
    nc = _PROG_CACHE[key]
    in_maps = build_in_maps(inputs)
    res = run_bass_kernel_spmd(nc, in_maps, core_ids=list(range(NCORES)))
    out = np.concatenate([res.results[g]["out"] for g in range(NCORES)],
                         axis=0)
    global LAST_DBG
    LAST_DBG = res.results[0].get("dbg")
    return out.reshape(1, 16384, 1024)


# version 10
# speedup vs baseline: 1.0470x; 1.0470x over previous
import numpy as np
import ml_dtypes
from contextlib import ExitStack

import concourse.bass as bass
import concourse.mybir as mybir
from concourse.bass_utils import run_bass_kernel_spmd

F32 = mybir.dt.float32
BF16 = mybir.dt.bfloat16
AF = mybir.ActivationFunctionType
ALU = mybir.AluOpType
AX = mybir.AxisListType

NCORES = 8
T = 2048
D = 1024
NDC = 8
HEADS = 16
E = 64
ST = 512
NST = T // ST

AY_W = 18496
AX_W = 16384
AW_W = 12288
AS_W = 4096

STOP_AFTER = None
MTAP = False
LAST_DBG = None
LAST_BUILD = None

ENGINES = ("sync", "gpsimd", "tensor", "vector", "scalar")


class Build:
    def __init__(self, nc):
        self.nc = nc
        self.steps = []
        self.phase = "init"
        self.sem = {}
        self.cnt = {}
        self.observed = {e: {} for e in ENGINES}
        self.owner = {}

    def add_sem(self, name, handle):
        self.sem[name] = handle
        self.cnt[name] = 0

    def inc(self, name, n=1):
        self.cnt[name] += n
        return self.cnt[name]

    def on(self, engine, fn, nreal=1):
        self.steps.append((engine, fn, nreal, self.phase))

    def wait(self, engine, sem_name, value=None):
        if value is None:
            value = self.cnt[sem_name]
        if value <= 0:
            return
        ob = self.observed[engine]
        if ob.get(sem_name, 0) >= value:
            return
        ob[sem_name] = value
        h = self.sem[sem_name]
        self.on(engine, lambda eng, h=h, value=value: eng.wait_ge(h, value))

    def use(self, engine, key):
        for s, c in self.owner.get(key, ()):
            self.wait(engine, s, c)

    def consumed(self, key, sem_name, cnt=None):
        if cnt is None:
            cnt = self.cnt[sem_name]
        self.owner[key] = [(sem_name, cnt)]

    def also_consumed(self, key, sem_name, cnt=None):
        if cnt is None:
            cnt = self.cnt[sem_name]
        self.owner.setdefault(key, []).append((sem_name, cnt))


def tap(b, ap):
    i = b.dbg_i
    b.dbg_i += 1
    eng_q0 = "sync" if str(ap.dtype) == "dt.float32" else "gpsimd"
    for s in ("pe", "dve", "act"):
        b.wait(eng_q0, s, b.cnt[s])
    eng_q = "sync" if str(ap.dtype) == "dt.float32" else "gpsimd"
    c = emit(b, eng_q, "dtap", 16)(
        lambda eng, ap=ap, i=i: eng.dma_start(
            out=b.dbg_t.ap()[i * 128:i * 128 + ap.shape[0], 0:ap.shape[1]],
            in_=ap))
    for e in ("tensor", "vector", "scalar"):
        b.wait(e, "dtap", c)


def emit(b, engine, sem_name=None, n=1):
    def deco(fn):
        if sem_name is None:
            b.on(engine, fn)
            return None
        h = b.sem[sem_name]
        val = b.inc(sem_name, n)

        def wrapped(eng, fn=fn, h=h, n=n):
            fn(eng).then_inc(h, n)
        b.on(engine, wrapped)
        return val
    return deco



def build_program():
    nc = bass.Bass("TRN2", target_bir_lowering=False, enable_partition_id=True)

    ti = {}

    def ein(name, shape, dtype=F32):
        ti[name] = nc.dram_tensor(name, shape, dtype, kind="ExternalInput")

    ein("x_own", [T, D])
    ein("aw_qkvT", [D, 3 * D], BF16)
    ein("ident_f", [128, 128])
    ein("ident_b", [128, 128], BF16)
    ein("ones_col", [128, 1])
    ein("ones_row", [1, 128])
    ein("blk2", [128, 2])
    ein("blk2T", [2, 128])
    ein("cvec", [128, 16])
    ein("f_wqkv", [6 * 1024, 3 * 1024], BF16)
    ein("f_wo", [6 * 1024, 1024], BF16)
    ein("f_w1t", [128, 6 * 22 * 2048], BF16)
    ein("f_w2t", [128, 6 * 8 * 2816], BF16)
    ein("rope", [4 * 128, 2312], BF16)
    ein("masks", [4 * 128, 512], BF16)
    ein("rsign", [128, 128], BF16)
    ein("qs2", [2, 6 * 128])
    ein("ks2", [2, 6 * 128])
    ein("b1c", [128, 6 * 44])
    ein("b2c", [128, 6 * 8])
    ein("lngc", [128, 6 * 8])
    ein("lnbc", [128, 6 * 8])
    ein("hvalid", [128, 1])
    ein("ones_col_bf", [128, 1], BF16)
    ein("m_win", [1024, 4096], BF16)
    ein("m_wxp", [2048, 96], BF16)
    ein("m_wdt", [64, 2048], BF16)
    ein("m_wout", [2048, 1024], BF16)
    ein("m_alog", [128, 32])
    ein("m_dparam", [128, 16])
    ein("m_lng", [128, 8])
    ein("m_lnb", [128, 8])
    ein("m_convw", [128, 64])
    ein("m_convb", [128, 16])
    ein("m_bdt", [128, 16])
    ein("m_sel", [16, 2048], BF16)

    out_t = nc.dram_tensor("out", [T, D], F32, kind="ExternalOutput")
    dbg_t = nc.dram_tensor("dbg", [16 * 128, 512], F32, kind="ExternalOutput")

    dram = {
        "ar_in": nc.dram_tensor("ar_in", [128, 520], F32),
        "ar_out": nc.dram_tensor("ar_out", [128, 520], F32),
        "hx_in": nc.dram_tensor("hx_in", [1024, 257], F32),
        "hx_out": nc.dram_tensor("hx_out", [8192, 257], F32),
        "hxs_in": nc.dram_tensor("hxs_in", [1024, 8], F32),
        "hxs_out": nc.dram_tensor("hxs_out", [8192, 8], F32),
        "x_spill": nc.dram_tensor("x_spill", [1024, 2048], F32),
        "a2a_dl_in": nc.dram_tensor("a2a_dl_in", [2048, 2048], F32),
        "a2a_dl_out": nc.dram_tensor("a2a_dl_out", [2048, 2048], F32),
        "a2a_du_in": nc.dram_tensor("a2a_du_in", [2048, 2048], BF16),
        "a2a_du_out": nc.dram_tensor("a2a_du_out", [2048, 2048], BF16),
        "bc_in": nc.dram_tensor("bc_in", [32, 2048], F32),
        "bc_out": nc.dram_tensor("bc_out", [256, 2048], F32),
        "a2a_y_in": nc.dram_tensor("a2a_y_in", [2048, 2048], F32),
        "u_spill": nc.dram_tensor("u_spill", [2048, 2048], BF16),
        "a2a_y_out": nc.dram_tensor("a2a_y_out", [2048, 2048], F32),
    }

    global LAST_BUILD
    b = Build(nc)
    LAST_BUILD = b
    with ExitStack() as ctx:
        sb = {}
        for name, shape, dt in (
            ("AY", [128, AY_W], F32), ("AX", [128, AX_W], F32),
            ("AW", [128, AW_W], F32), ("AS", [128, AS_W], F32),
            ("c_identf", [128, 128], F32), ("c_identb", [128, 128], BF16),
            ("c_ones_col", [128, 1], F32), ("c_ones_row", [1, 128], F32),
            ("c_blk2", [128, 2], F32), ("c_blk2T", [2, 128], F32),
            ("c_cvec", [128, 16], F32),
            ("c_rsign", [128, 128], BF16),
            ("c_qs2", [2, 128], F32), ("c_ks2", [2, 128], F32),
            ("c_b1", [128, 44], F32), ("c_b2", [128, 8], F32),
            ("c_lng", [128, 8], F32), ("c_lnb", [128, 8], F32),
            ("c_masks", [128, 2048], BF16),
            ("c_hv", [128, 1], F32),
            ("c_ones_col_bf", [128, 1], BF16),
        ):
            sb[name] = ctx.enter_context(nc.sbuf_tensor(name, shape, dt)).ap()
        PS = [ctx.enter_context(nc.psum_tensor(f"ps{i}", [128, 512], F32)).ap()
              for i in range(8)]
        for s in ("dc0", "dxa", "dxb", "dwa", "dwb", "dsa", "dsb",
                  "arx", "ary", "dtap", "hxa", "hxb", "rta", "rtb",
                  "w1a", "w1b", "w2a", "w2b", "dwc", "dwd", "spl",
                  "lca", "lcb", "lcc", "lcd", "msa", "msb", "mla", "mlb",
                  "mwa", "mwb", "mlc", "mua", "mub",
                  "pe", "dve", "act", "cc", "pool"):
            b.add_sem(s, ctx.enter_context(nc.semaphore(s)))

        b.dbg_t = dbg_t
        b.dbg_i = 0
        build_steps(b, ti, out_t, dram, sb, PS)

        with nc.Block() as block:
            for ename in ENGINES:
                def mk(ename):
                    def body(eng):
                        for e, fn, _n, ph in b.steps:
                            if e == ename:
                                with nc.named_scope(ph):
                                    fn(eng)
                    return body
                getattr(block, ename)(mk(ename))
    return nc


PHASE_MARKS = []


def build_steps(b, ti, out_t, dram, sb, PS):
    nc = b.nc
    PHASE_MARKS.clear()
    def mark(name):
        PHASE_MARKS.append((name, dict(b.cnt)))
        b.phase = name
    AY, AXa, AW, AS = sb["AY"], sb["AX"], sb["AW"], sb["AS"]
    AYb, AWb, ASb = AY.bitcast(BF16), AW.bitcast(BF16), AS.bitcast(BF16)

    def xr(dc):
        return AXa[:, dc * T:(dc + 1) * T]

    for cname, iname in (("c_identf", "ident_f"), ("c_identb", "ident_b"),
                         ("c_ones_col", "ones_col"), ("c_ones_row", "ones_row"),
                         ("c_blk2", "blk2"), ("c_blk2T", "blk2T"),
                         ("c_cvec", "cvec"), ("c_rsign", "rsign"),
                         ("c_hv", "hvalid"),
                         ("c_ones_col_bf", "ones_col_bf")):
        emit(b, "sync", "dc0", 16)(
            lambda eng, cname=cname, iname=iname: eng.dma_start(
                out=sb[cname], in_=ti[iname].ap()))
    for t in range(4):
        emit(b, "sync", "dc0", 16)(
            lambda eng, t=t: eng.dma_start(
                out=sb["c_masks"][:, t * 512:(t + 1) * 512],
                in_=ti["masks"].ap()[t * 128:(t + 1) * 128, :]))
    consts_loaded = b.cnt["dc0"]
    b.wait("tensor", "dc0", consts_loaded)
    b.wait("vector", "dc0", consts_loaded)
    b.wait("scalar", "dc0", consts_loaded)

    regs = {}

    def _(eng):
        pid = eng.partition_id()
        prev = (pid + 7) % 8
        regs["r257"] = prev * (1024 * 257)
        regs["r8"] = prev * (1024 * 8)
    b.on("sync", _, nreal=0)
    b.regs = regs

    xin = ti["x_own"].ap()
    for tc in range(T // 128):
        buf = AS[:, (tc % 2) * 1024:(tc % 2) * 1024 + 1024]
        sem = "dxa" if tc % 2 == 0 else "dxb"
        b.use("sync", ("xin", tc % 2))
        ld = emit(b, "sync", sem, 16)(
            lambda eng, buf=buf, tc=tc: eng.dma_start(
                out=buf, in_=xin[tc * 128:(tc + 1) * 128, :]))
        b.wait("tensor", sem, ld)
        for grp in range(2):
            bk = grp
            b.use("tensor", ("bank", bk))
            for j in range(4):
                dc = grp * 4 + j
                emit(b, "tensor", "pe")(
                    lambda eng, bk=bk, j=j, buf=buf, dc=dc: eng.matmul(
                        PS[bk][:, j * 128:(j + 1) * 128],
                        buf[:, dc * 128:(dc + 1) * 128], sb["c_identf"],
                        start=(j == 0), stop=(j == 3), is_transpose=True))
            pc = b.cnt["pe"]
            b.wait("scalar", "pe", pc)
            for j in range(4):
                dc = grp * 4 + j
                emit(b, "scalar", "act")(
                    lambda eng, bk=bk, j=j, dc=dc, tc=tc: eng.copy(
                        out=xr(dc)[:, tc * 128:(tc + 1) * 128],
                        in_=PS[bk][:, j * 128:(j + 1) * 128]))
            b.consumed(("bank", bk), "act")
        b.consumed(("xin", tc % 2), "pe")
    x_dep = ("act", b.cnt["act"])

    pipeA = AS[:, 1536:2048]
    pipeB = AS[:, 2048:2560]
    sqA = AS[:, 2560:3072]
    sqB = AS[:, 3072:3584]
    mu_t = AS[0:1, 3584:4096]
    var_t = AS[0:1, 1536:2048]
    sd_t = sqA[0:1, :]

    def emit_ln(x_tiles, t0, n, out_cb, x_dep, stats=None, bf_in=False):
        if stats is None:
            l_mu, l_var, l_sd = mu_t, var_t, sd_t
        else:
            l_mu, l_var, l_sd = stats
        ones_l = sb["c_ones_col_bf"] if bf_in else sb["c_ones_col"]
        ps_sum = PS[2]
        ps_sq = PS[3]
        b.use("tensor", ("bank", 2))
        b.use("tensor", ("bank", 3))
        b.wait("tensor", *x_dep)
        sq_mm_at = {}
        for dc in range(NDC):
            xap = x_tiles(dc)[:, t0:t0 + n]
            emit(b, "tensor", "pe")(
                lambda eng, xap=xap, dc=dc: eng.matmul(
                    ps_sum[0:1, 0:n], ones_l, xap,
                    start=(dc == 0), stop=(dc == NDC - 1)))
            sq = sqA if dc % 2 == 0 else sqB
            if dc >= 2:
                b.wait("scalar", "pe", sq_mm_at[dc - 2])
            b.use("scalar", ("sq", dc % 2))
            b.wait("scalar", *x_dep)
            ac = emit(b, "scalar", "act")(
                lambda eng, xap=xap, sq=sq: eng.activation(
                    out=sq[:, 0:n], in_=xap, func=AF.Square))
            b.wait("tensor", "act", ac)
            emit(b, "tensor", "pe")(
                lambda eng, sq=sq, dc=dc: eng.matmul(
                    ps_sq[0:1, 0:n], sb["c_ones_col"], sq[:, 0:n],
                    start=(dc == 0), stop=(dc == NDC - 1)))
            sq_mm_at[dc] = b.cnt["pe"]
        if n < 128:
            emit(b, "tensor", "pe")(
                lambda eng: eng.matmul(PS[4][0:1, 0:512], sb["c_ones_col"],
                                       sqA[:, 0:512], start=True, stop=True))
        mm_done = b.cnt["pe"]
        b.wait("vector", "pe", mm_done)
        b.use("vector", "ln_stats")
        emit(b, "vector", "dve")(
            lambda eng: eng.tensor_scalar(
                out=l_mu[:, 0:n], in0=ps_sum[0:1, 0:n], scalar1=1.0 / D,
                scalar2=None, op0=ALU.mult))
        dve_mu = b.cnt["dve"]
        b.wait("scalar", "dve", dve_mu)
        b.use("scalar", "pipeA")
        ac = emit(b, "scalar", "act")(
            lambda eng: eng.activation(out=l_var[:, 0:n], in_=l_mu[:, 0:n],
                                       func=AF.Square))
        b.wait("vector", "act", ac)
        emit(b, "vector", "dve")(
            lambda eng: eng.tensor_scalar(
                out=l_var[:, 0:n], in0=l_var[:, 0:n], scalar1=-1.0,
                scalar2=None, op0=ALU.mult))
        emit(b, "vector", "dve")(
            lambda eng: eng.scalar_tensor_tensor(
                out=var_t[:, 0:n], in0=ps_sq[0:1, 0:n], scalar=1.0 / D,
                in1=l_var[:, 0:n], op0=ALU.mult, op1=ALU.add))
        b.consumed(("bank", 2), "dve")
        b.consumed(("bank", 3), "dve")
        b.wait("scalar", "dve", b.cnt["dve"])
        ac = emit(b, "scalar", "act")(
            lambda eng: eng.activation(out=l_sd[:, 0:n], in_=l_var[:, 0:n],
                                       func=AF.Sqrt, bias=sb["c_cvec"][0:1, 1:2]))
        b.wait("vector", "act", ac)
        dve_r = emit(b, "vector", "dve")(
            lambda eng: eng.reciprocal(out=l_var[:, 0:n], in_=l_sd[:, 0:n]))
        b.use("tensor", ("bank", 4))
        b.use("tensor", ("bank", 5))
        b.wait("tensor", "dve", dve_r)
        nw = max(n, 512)
        emit(b, "tensor", "pe")(
            lambda eng: eng.matmul(PS[4][:, 0:nw], sb["c_ones_row"],
                                   l_mu[:, 0:nw], start=True, stop=True))
        rep_done = emit(b, "tensor", "pe")(
            lambda eng: eng.matmul(PS[5][:, 0:nw], sb["c_ones_row"],
                                   l_var[:, 0:nw], start=True, stop=True))
        b.consumed("ln_stats", "pe", rep_done)
        b.consumed("pipeA", "pe", rep_done)
        npd = n if n >= 128 else 64
        if n < 128:
            mu_rep = AS[:, 3712:3712 + npd]
            sd_rep = AS[:, 3776:3776 + npd]
            b.wait("scalar", "pe", rep_done)
            emit(b, "scalar", "act")(
                lambda eng: eng.copy(out=mu_rep, in_=PS[4][:, 0:npd]))
            ev_c = emit(b, "scalar", "act")(
                lambda eng: eng.copy(out=sd_rep, in_=PS[5][:, 0:npd]))
            b.wait("vector", "act", ev_c)
            rep4, rep5 = mu_rep, sd_rep
        else:
            rep4, rep5 = PS[4][:, 0:n], PS[5][:, 0:n]
        b.wait("vector", "pe", rep_done)
        b.wait("vector", *x_dep)
        for dc in range(NDC):
            xap = x_tiles(dc)[:, t0:t0 + npd]
            sub = sqA if dc % 2 == 0 else sqB
            b.wait("vector", "pe", mm_done)
            emit(b, "vector", "dve")(
                lambda eng, xap=xap, sub=sub: eng.tensor_tensor(
                    out=sub[:, 0:npd], in0=xap, in1=rep4,
                    op=ALU.subtract))
            emit(b, "vector", "dve")(
                lambda eng, sub=sub: eng.tensor_tensor(
                    out=sub[:, 0:npd], in0=sub[:, 0:npd], in1=rep5,
                    op=ALU.mult))
            out_cb(dc, sub[:, 0:npd])
            b.consumed(("sq", dc % 2), "dve")
        b.consumed(("bank", 4), "dve")
        b.consumed(("bank", 5), "dve")

    def y0bf(dc):
        return AYb[:, dc * T:(dc + 1) * T]

    if STOP_AFTER == "x0":
        store_x(b, sb, xr, out_t, AS, PS)
        return

    y0_done = []
    for st in range(NST):
        def cb(dc, sub, st=st):
            emit(b, "vector", "dve")(
                lambda eng, dc=dc, sub=sub: eng.tensor_copy(
                    out=y0bf(dc)[:, st * ST:(st + 1) * ST], in_=sub))
        emit_ln(xr, st * ST, ST, cb, x_dep)
        y0_done.append(b.cnt["dve"])

    if STOP_AFTER == "y0":
        for dc in range(NDC):
            emit(b, "vector", "dve")(
                lambda eng, dc=dc: eng.tensor_copy(out=xr(dc), in_=y0bf(dc)))
        store_x(b, sb, xr, out_t, AS, PS)
        return

    mark("p0_ln")
    attn_a(b, ti, sb, PS, AYb, AS, ASb, AWb, y0bf, xr, dram, y0_done, out_t)
    mark("attn_a")
    if STOP_AFTER == "kv":
        return

    if STOP_AFTER == "x1":
        store_x(b, sb, xr, out_t, AS, PS)
        return

    ffn6_call(b, ti, sb, PS, AY, AXa, AW, AS, dram, xr, emit_ln, out_t)
    mark("ffn0")
    if STOP_AFTER is not None and (STOP_AFTER in ("x2", "fg0", "fg1")
                                   or STOP_AFTER.startswith("ysub")):
        store_x(b, sb, xr, out_t, AS, PS)
        return

    def exchange_x(xh_ap):
        hxs_in, hxs_out = dram["hxs_in"].ap(), dram["hxs_out"].ap()
        b.use("sync", ("hxm",))
        b.wait("sync", "dve", b.cnt["dve"])
        for dc in range(NDC):
            emit(b, "sync", "hxa", 16)(
                lambda eng, dc=dc: eng.dma_start(
                    out=hxs_in[dc * 128:(dc + 1) * 128, :],
                    in_=xr(dc)[:, T - 8:T]))
        b.wait("gpsimd", "hxa", b.cnt["hxa"])
        b.wait("gpsimd", "cc", b.cnt["cc"])
        cc_c = emit(b, "gpsimd", "cc")(
            lambda eng: eng.collective_compute(
                "AllGather", ALU.bypass, replica_groups=[list(range(NCORES))],
                ins=[hxs_in.opt()], outs=[hxs_out.opt()]))
        b.wait("sync", "cc", cc_c)
        for dc in range(NDC):
            def f(eng, dc=dc):
                src_ap = bass.AP(hxs_out.tensor, b.regs["r8"] + dc * 128 * 8,
                                 [[8, 128], [1, 8]])
                return eng.dma_start(out=xh_ap[:, dc * 8:(dc + 1) * 8],
                                     in_=src_ap)
            emit(b, "sync", "hxa", 16)(f)
        rb = b.cnt["hxa"]
        b.wait("vector", "hxa", rb)
        for dc in range(NDC):
            emit(b, "vector", "dve")(
                lambda eng, dc=dc: eng.tensor_scalar(
                    out=xh_ap[:, dc * 8:(dc + 1) * 8],
                    in0=xh_ap[:, dc * 8:(dc + 1) * 8],
                    scalar1=sb["c_hv"][:, 0:1], scalar2=None, op0=ALU.mult))
        b.consumed(("hxm",), "dve", b.cnt["dve"])
        return b.cnt["dve"]

    mamba_block(b, ti, sb, PS, AY, AXa, AW, AS, dram, xr, emit_ln, exchange_x)
    mark("mamba")
    if STOP_AFTER == "x3":
        store_x(b, sb, xr, out_t, AS, PS)
        return

    ffn6_call(b, ti, sb, PS, AY, AXa, AW, AS, dram, xr, emit_ln, out_t)
    mark("ffn1")

    store_x(b, sb, xr, out_t, AS, PS)
    mark("store")


def store_x(b, sb, xr, out_t, AS, PS):
    oap = out_t.ap()
    xw = ("dve", b.cnt["dve"])
    for tc in range(T // 128):
        buf = AS[:, (tc % 2) * 1024:(tc % 2) * 1024 + 1024]
        b.use("scalar", ("xout", tc % 2))
        b.wait("tensor", *xw)
        for grp in range(2):
            bk = 6 + grp
            b.use("tensor", ("bank", bk))
            for j in range(4):
                dc = grp * 4 + j
                emit(b, "tensor", "pe")(
                    lambda eng, bk=bk, j=j, dc=dc, tc=tc: eng.matmul(
                        PS[bk][:, j * 128:(j + 1) * 128],
                        xr(dc)[:, tc * 128:(tc + 1) * 128], sb["c_identf"],
                        start=(j == 0), stop=(j == 3), is_transpose=True))
            pc = b.cnt["pe"]
            b.wait("scalar", "pe", pc)
            emit(b, "scalar", "act")(
                lambda eng, bk=bk, buf=buf, grp=grp: eng.copy(
                    out=buf[:, grp * 512:(grp + 1) * 512], in_=PS[bk][:, 0:512]))
            b.consumed(("bank", bk), "act")
        b.wait("sync", "act", b.cnt["act"])
        sem = "dsa" if tc % 2 == 0 else "dsb"
        sc = emit(b, "sync", sem, 16)(
            lambda eng, buf=buf, tc=tc: eng.dma_start(
                out=oap[tc * 128:(tc + 1) * 128, :], in_=buf))
        b.consumed(("xout", tc % 2), sem, sc)
    b.wait("sync", "dsa", b.cnt["dsa"])
    b.wait("sync", "dsb", b.cnt["dsb"])


def attn_a(b, ti, sb, PS, AYb, AS, ASb, AWb, y0bf, xr, dram, y0_done,
           *out_t_global):
    nc = b.nc
    wsrc = ti["aw_qkvT"].ap()

    def wslot(slot, kc):
        return AWb[:, slot * 8192 + kc * 1024: slot * 8192 + (kc + 1) * 1024]

    def load_third(slot, which):
        sem = "dwa" if slot == 0 else "dwb"
        b.use("gpsimd", ("wslot", slot))
        for kc in range(NDC):
            emit(b, "gpsimd", sem, 16)(
                lambda eng, slot=slot, kc=kc, which=which: eng.dma_start(
                    out=wslot(slot, kc),
                    in_=wsrc[kc * 128:(kc + 1) * 128,
                             which * 1024:(which + 1) * 1024]))
        return sem, b.cnt[sem]

    k_loaded = load_third(0, 1)
    v_loaded = load_third(1, 2)

    kst = lambda oc: AYb[:, 16384 + oc * 512: 16384 + (oc + 1) * 512]
    vst = lambda oc: AYb[:, 20480 + oc * 512: 20480 + (oc + 1) * 512]
    ktm = lambda tcl: AYb[:, 24576 + tcl * 1024: 24576 + (tcl + 1) * 1024]
    vtm = lambda tcl: AYb[:, 28672 + tcl * 1024: 28672 + (tcl + 1) * 1024]

    kvbd = lambda pr: ASb[:, pr * 128:(pr + 1) * 128]
    q2s = lambda i: ASb[:, 1024 + i * 512: 1024 + (i + 1) * 512]
    ks2_bf = ASb[:, 2048:2064]
    pipeA = AS[:, 1536:2048]
    pipeB = AS[:, 2048:2560]
    kv_acc = AS[:, 1024:1536]
    ksum_st = AS[:, 3584:3616]
    ksum_acc = AS[:, 3616:3624]

    ks2_bf = ASb[:, 7248:7280]

    pipe_calls = [0]

    def qk_pipeline(ps_in, n, out_bf):
        dbgme = (STOP_AFTER == "tm" and pipe_calls[0] == 0)
        pipe_calls[0] += 1
        b.use("scalar", "pipeA")
        b.use("scalar", "pipeB")
        a1 = emit(b, "scalar", "act")(
            lambda eng: eng.activation(out=pipeA[:, 0:n], in_=ps_in,
                                       func=AF.Square))
        a2 = emit(b, "scalar", "act")(
            lambda eng: eng.copy(out=pipeB[:, 0:n], in_=ps_in))
        if dbgme:
            tap(b, pipeB[:, 0:n])
            tap(b, pipeA[:, 0:n])
        b.wait("tensor", "act", a1)
        b.use("tensor", ("bank", 2))
        p1 = emit(b, "tensor", "pe")(
            lambda eng: eng.matmul(PS[2][0:2, 0:n], sb["c_blk2"],
                                   pipeA[:, 0:n], start=True, stop=True))
        b.wait("scalar", "pe", p1)
        a3 = emit(b, "scalar", "act")(
            lambda eng: eng.activation(out=pipeA[0:2, 0:n], in_=PS[2][0:2, 0:n],
                                       func=AF.Sqrt))
        b.consumed(("bank", 2), "act", a3)
        b.wait("vector", "act", a3)
        emit(b, "vector", "dve")(
            lambda eng: eng.tensor_scalar(
                out=pipeA[0:2, 0:n], in0=pipeA[0:2, 0:n], scalar1=1e-12,
                scalar2=None, op0=ALU.max))
        d2 = emit(b, "vector", "dve")(
            lambda eng: eng.reciprocal(out=pipeA[0:2, 0:n],
                                       in_=pipeA[0:2, 0:n]))
        if dbgme:
            tap(b, pipeA[0:2, 0:n])
        b.wait("tensor", "dve", d2)
        b.use("tensor", ("bank", 3))
        p2 = emit(b, "tensor", "pe")(
            lambda eng: eng.matmul(PS[3][:, 0:n], sb["c_blk2T"],
                                   pipeA[0:2, 0:n], start=True, stop=True))
        b.wait("vector", "pe", p2)
        d3 = emit(b, "vector", "dve")(
            lambda eng: eng.tensor_tensor(out=pipeB[:, 0:n], in0=pipeB[:, 0:n],
                                          in1=PS[3][:, 0:n], op=ALU.mult))
        b.consumed(("bank", 3), "dve", d3)
        if dbgme:
            tap(b, pipeB[:, 0:n])
        b.wait("scalar", "dve", d3)
        a4 = emit(b, "scalar", "act")(
            lambda eng: eng.activation(out=pipeA[:, 0:n], in_=pipeB[:, 0:n],
                                       func=AF.Abs))
        emit(b, "scalar", "act")(
            lambda eng: eng.activation(out=pipeA[:, 0:n], in_=pipeA[:, 0:n],
                                       func=AF.Exp, scale=-1.0))
        a6 = emit(b, "scalar", "act")(
            lambda eng: eng.activation(out=pipeA[:, 0:n], in_=pipeA[:, 0:n],
                                       func=AF.Ln, bias=sb["c_cvec"][:, 0:1]))
        b.wait("vector", "act", a6)
        d5 = emit(b, "vector", "dve")(
            lambda eng: eng.scalar_tensor_tensor(
                out=out_bf, in0=pipeB[:, 0:n], scalar=0.0, in1=pipeA[:, 0:n],
                op0=ALU.max, op1=ALU.add))
        b.consumed("pipeA", "dve", d5)
        b.consumed("pipeB", "dve", d5)
        return d5

    b.wait("tensor", *k_loaded)
    emit(b, "vector", "dve")(
        lambda eng: eng.memset(kv_acc, 0.0))

    k_mms_done = 0
    for st in range(NST):
        b.wait("tensor", "dve", y0_done[st])
        for half in (0, 1):
            if half == 1:
                b.wait("tensor", *v_loaded)
            for oc in range(NDC):
                bank = PS[oc % 2]
                b.use("tensor", ("bank", oc % 2))
                for kc in range(NDC):
                    emit(b, "tensor", "pe")(
                        lambda eng, bank=bank, half=half, kc=kc, oc=oc, st=st:
                        eng.matmul(
                            bank[:, 0:ST],
                            wslot(half, kc)[:, oc * 128:(oc + 1) * 128],
                            y0bf(kc)[:, st * ST:(st + 1) * ST],
                            start=(kc == 0), stop=(kc == NDC - 1)))
                mm_done = b.cnt["pe"]
                if half == 0:
                    k_mms_done = mm_done
                b.wait("scalar", "pe", mm_done)
                if half == 0:
                    b.use("vector", ("kst", oc))
                    qk_pipeline(bank[:, 0:ST], ST, kst(oc))
                    b.consumed(("bank", oc % 2), "act")
                    b.also_consumed(("bank", oc % 2), "dve")
                    emit(b, "vector", "dve")(
                        lambda eng, oc=oc, st=st: eng.reduce_sum(
                            out=ksum_st[:, oc * 4 + st:oc * 4 + st + 1],
                            in_=kst(oc), axis=AX.X))
                else:
                    b.use("scalar", ("vst", oc))
                    emit(b, "scalar", "act")(
                        lambda eng, bank=bank, oc=oc: eng.copy(
                            out=vst(oc), in_=bank[:, 0:ST]))
                    b.consumed(("bank", oc % 2), "act")
        stage_done_d = b.cnt["dve"]
        stage_done_a = b.cnt["act"]
        b.wait("tensor", "dve", stage_done_d)
        b.wait("tensor", "act", stage_done_a)
        for tcl in range(ST // 128):
            for src, dstf in ((kst, ktm), (vst, vtm)):
                for grp in range(2):
                    bk = 4 + grp
                    trv = PS[bk].bitcast(BF16)
                    b.use("tensor", ("bank", bk))
                    for j in range(4):
                        oc = grp * 4 + j
                        emit(b, "tensor", "pe")(
                            lambda eng, trv=trv, j=j, src=src, oc=oc, tcl=tcl:
                            eng.matmul(
                                trv[:, j * 128:(j + 1) * 128],
                                src(oc)[:, tcl * 128:(tcl + 1) * 128],
                                sb["c_identb"],
                                start=(j == 0), stop=(j == 3),
                                is_transpose=True))
                    pc = b.cnt["pe"]
                    b.wait("scalar", "pe", pc)
                    b.use("scalar", ("tm", dstf is vtm, tcl))
                    emit(b, "scalar", "act")(
                        lambda eng, trv=trv, dstf=dstf, tcl=tcl, grp=grp:
                        eng.copy(
                            out=dstf(tcl)[:, grp * 512:(grp + 1) * 512],
                            in_=trv[:, 0:512]))
                    b.consumed(("bank", bk), "act")
        tm_done = b.cnt["act"]
        if STOP_AFTER == "tm" and st == 0:
            oap = out_t_global[0].ap()
            b.wait("sync", "act", tm_done)
            b.wait("sync", "dve", b.cnt["dve"])
            for oc in range(4):
                emit(b, "vector", "dve")(
                    lambda eng, oc=oc: eng.tensor_copy(
                        out=AS[:, 1536 + oc * 512:1536 + (oc + 1) * 512][:, 0:512],
                        in_=kst(oc)))
            b.wait("sync", "dve", b.cnt["dve"])
            emit(b, "sync", "dsa", 16)(
                lambda eng: eng.dma_start(out=oap[0:128, 0:1024],
                                          in_=AS[:, 1536:2560]))
            for i, tmf in ((0, ktm), (1, vtm)):
                emit(b, "vector", "dve")(
                    lambda eng, tmf=tmf: eng.tensor_copy(
                        out=AS[:, 1536:2560], in_=tmf(0)))
                b.wait("sync", "dve", b.cnt["dve"])
                emit(b, "sync", "dsa", 16)(
                    lambda eng, i=i: eng.dma_start(
                        out=oap[128 * (i + 1):128 * (i + 2), 0:1024],
                        in_=AS[:, 1536:2560]))
            b.wait("sync", "dsa", b.cnt["dsa"])
            return
        b.wait("tensor", "act", tm_done)
        for h in range(HEADS):
            pr, off = h // 2, (h % 2) * 64
            bk = 6 + (h % 2)
            b.use("tensor", ("bank", bk))
            for tcl in range(ST // 128):
                emit(b, "tensor", "pe")(
                    lambda eng, h=h, tcl=tcl, bk=bk: eng.matmul(
                        PS[bk][0:64, 0:64],
                        ktm(tcl)[:, h * 64:(h + 1) * 64],
                        vtm(tcl)[:, h * 64:(h + 1) * 64],
                        start=(tcl == 0), stop=(tcl == 3)))
            hp = b.cnt["pe"]
            b.wait("vector", "pe", hp)
            emit(b, "vector", "dve")(
                lambda eng, pr=pr, off=off, bk=bk, st=st: eng.tensor_tensor(
                    out=kv_acc[off:off + 64, pr * 64:(pr + 1) * 64],
                    in0=kv_acc[off:off + 64, pr * 64:(pr + 1) * 64],
                    in1=PS[bk][0:64, 0:64], op=ALU.add))
            b.consumed(("bank", bk), "dve")
        for oc in range(NDC):
            b.consumed(("kst", oc), "pe", b.cnt["pe"])
            b.consumed(("vst", oc), "pe", b.cnt["pe"])
        for tcl in range(ST // 128):
            b.consumed(("tm", False, tcl), "pe", b.cnt["pe"])
            b.consumed(("tm", True, tcl), "pe", b.cnt["pe"])

    if STOP_AFTER == "kv":
        tap(b, ksum_st)

    for oc in range(NDC):
        emit(b, "vector", "dve")(
            lambda eng, oc=oc: eng.tensor_tensor(
                out=ksum_st[:, oc * 4:oc * 4 + 1],
                in0=ksum_st[:, oc * 4:oc * 4 + 1],
                in1=ksum_st[:, oc * 4 + 1:oc * 4 + 2], op=ALU.add))
        emit(b, "vector", "dve")(
            lambda eng, oc=oc: eng.tensor_tensor(
                out=ksum_st[:, oc * 4 + 2:oc * 4 + 3],
                in0=ksum_st[:, oc * 4 + 2:oc * 4 + 3],
                in1=ksum_st[:, oc * 4 + 3:oc * 4 + 4], op=ALU.add))
        emit(b, "vector", "dve")(
            lambda eng, oc=oc: eng.tensor_tensor(
                out=ksum_acc[:, oc:oc + 1],
                in0=ksum_st[:, oc * 4:oc * 4 + 1],
                in1=ksum_st[:, oc * 4 + 2:oc * 4 + 3], op=ALU.add))
    ks_done = b.cnt["dve"]

    if STOP_AFTER == "kv":
        tap(b, kv_acc)
        tap(b, ksum_acc)

    ar_in, ar_out = dram["ar_in"], dram["ar_out"]
    b.wait("sync", "dve", ks_done)
    emit(b, "sync", "arx", 16)(
        lambda eng: eng.dma_start(out=ar_in.ap()[:, 0:512], in_=kv_acc))
    d2 = emit(b, "sync", "arx", 16)(
        lambda eng: eng.dma_start(out=ar_in.ap()[:, 512:520], in_=ksum_acc))
    b.wait("gpsimd", "arx", d2)
    cc_c = emit(b, "gpsimd", "cc")(
        lambda eng: eng.collective_compute(
            "AllReduce", ALU.add, replica_groups=[list(range(NCORES))],
            ins=[ar_in.ap().opt()], outs=[ar_out.ap().opt()]))
    b.wait("sync", "cc", cc_c)
    d3 = emit(b, "sync", "ary", 16)(
        lambda eng: eng.dma_start(out=kv_acc, in_=ar_out.ap()[:, 0:512]))
    d4 = emit(b, "sync", "ary", 16)(
        lambda eng: eng.dma_start(out=ksum_acc, in_=ar_out.ap()[:, 512:520]))
    b.wait("vector", "ary", d4)
    emit(b, "vector", "dve")(
        lambda eng: eng.tensor_scalar(
            out=ksum_acc, in0=ksum_acc, scalar1=1e-6, scalar2=None,
            op0=ALU.add))
    emit(b, "vector", "dve")(
        lambda eng: eng.tensor_scalar(
            out=kv_acc, in0=kv_acc, scalar1=0.125, scalar2=None,
            op0=ALU.mult))
    emit(b, "vector", "dve")(
        lambda eng: eng.memset(ASb[:, 0:1024], 0.0))
    for pr in range(8):
        emit(b, "vector", "dve")(
            lambda eng, pr=pr: eng.tensor_copy(
                out=kvbd(pr)[0:64, 0:64],
                in_=kv_acc[0:64, pr * 64:(pr + 1) * 64]))
        emit(b, "vector", "dve")(
            lambda eng, pr=pr: eng.tensor_copy(
                out=kvbd(pr)[64:128, 64:128],
                in_=kv_acc[64:128, pr * 64:(pr + 1) * 64]))
    kv_ready = b.cnt["dve"]

    if STOP_AFTER == "kv":
        oap = out_t_global[0]
        b.wait("sync", "dve", kv_ready)
        emit(b, "sync", "dsa", 16)(
            lambda eng: eng.dma_start(out=oap.ap()[0:128, 0:512], in_=kv_acc))
        emit(b, "sync", "dsa", 16)(
            lambda eng: eng.dma_start(out=oap.ap()[0:128, 512:520], in_=ksum_acc))
        b.wait("sync", "dsa", b.cnt["dsa"])
        return

    b.consumed(("wslot", 0), "pe", k_mms_done)
    b.consumed(("wslot", 1), "pe", b.cnt["pe"])
    q_loaded = load_third(0, 0)
    b.wait("tensor", *q_loaded)
    b.wait("tensor", "dve", kv_ready)

    for st in range(NST):
        for oc in range(NDC):
            bank = PS[oc % 2]
            b.use("tensor", ("bank", oc % 2))
            for kc in range(NDC):
                emit(b, "tensor", "pe")(
                    lambda eng, bank=bank, kc=kc, oc=oc, st=st: eng.matmul(
                        bank[:, 0:ST],
                        wslot(0, kc)[:, oc * 128:(oc + 1) * 128],
                        y0bf(kc)[:, st * ST:(st + 1) * ST],
                        start=(kc == 0), stop=(kc == NDC - 1)))
            mm_done = b.cnt["pe"]
            b.wait("scalar", "pe", mm_done)
            q2 = q2s(oc % 2)
            b.use("vector", ("q2", oc % 2))
            d5 = qk_pipeline(bank[:, 0:ST], ST, q2)
            b.consumed(("bank", oc % 2), "act")
            b.also_consumed(("bank", oc % 2), "dve")
            b.use("vector", ("ks2", oc))
            emit(b, "vector", "dve")(
                lambda eng, oc=oc: eng.tensor_tensor(
                    out=ks2_bf[:, 2 * oc:2 * oc + 1],
                    in0=ksum_acc[:, oc:oc + 1], in1=sb["c_blk2"][:, 0:1],
                    op=ALU.mult))
            ks_c = emit(b, "vector", "dve")(
                lambda eng, oc=oc: eng.tensor_tensor(
                    out=ks2_bf[:, 2 * oc + 1:2 * oc + 2],
                    in0=ksum_acc[:, oc:oc + 1], in1=sb["c_blk2"][:, 1:2],
                    op=ALU.mult))
            b.use("tensor", ("bank", 2))
            b.wait("tensor", "dve", ks_c)
            p1 = emit(b, "tensor", "pe")(
                lambda eng, oc=oc, q2=q2: eng.matmul(
                    PS[2][0:2, 0:ST], ks2_bf[:, 2 * oc:2 * oc + 2], q2,
                    start=True, stop=True))
            b.consumed(("ks2", oc), "pe", p1)
            b.wait("vector", "pe", p1)
            b.use("vector", "pipeA")
            d6 = emit(b, "vector", "dve")(
                lambda eng: eng.reciprocal(out=pipeA[0:2, 0:ST],
                                           in_=PS[2][0:2, 0:ST]))
            b.consumed(("bank", 2), "dve", d6)
            b.wait("tensor", "dve", d6)
            b.use("tensor", ("bank", 3))
            p2 = emit(b, "tensor", "pe")(
                lambda eng: eng.matmul(PS[3][:, 0:ST], sb["c_blk2T"],
                                       pipeA[0:2, 0:ST], start=True, stop=True))
            b.consumed("pipeA", "pe", p2)
            b.wait("vector", "pe", p2)
            d7 = emit(b, "vector", "dve")(
                lambda eng, q2=q2: eng.tensor_tensor(
                    out=q2, in0=q2, in1=PS[3][:, 0:ST], op=ALU.mult))
            b.consumed(("bank", 3), "dve", d7)
            b.wait("tensor", "dve", d7)
            b.use("tensor", ("bank", 5))
            po = emit(b, "tensor", "pe")(
                lambda eng, oc=oc, q2=q2: eng.matmul(
                    PS[5][:, 0:ST], kvbd(oc), q2, start=True, stop=True))
            b.consumed(("q2", oc % 2), "pe", po)
            b.wait("vector", "pe", po)
            d8 = emit(b, "vector", "dve")(
                lambda eng, oc=oc, st=st: eng.tensor_tensor(
                    out=xr(oc)[:, st * ST:(st + 1) * ST],
                    in0=xr(oc)[:, st * ST:(st + 1) * ST],
                    in1=PS[5][:, 0:ST], op=ALU.add))
            b.consumed(("bank", 5), "dve", d8)





HALO = 264
YW = HALO + T


def ffn6_call(b, ti, sb, PS, AY, AXa, AW, AS, dram, xr, emit_ln, out_t):
    nc = b.nc
    AXb = AXa.bitcast(BF16)
    AWb = AW.bitcast(BF16)
    ASb = AS.bitcast(BF16)

    def yt(dc):
        return AY[:, dc * YW:(dc + 1) * YW]

    def ys(dc, c0, n):
        off = -1 if dc >= 4 else 0
        return yt(dc)[:, c0 + off:c0 + off + n]

    wo_bf = lambda kc: AXb[:, kc * 1024:(kc + 1) * 1024]
    kroll = lambda oc: AXb[:, 8192 + oc * 768: 8192 + (oc + 1) * 768]
    vchunk = lambda r: AXb[:, 14336 + r * 1024: 14336 + (r + 1) * 1024]
    tb_cs = AXb[:, 20480:20992]
    tb_ss = AXb[:, 20992:21504]
    tb_ci = AXb[:, 21504:22272]
    tb_si = AXb[:, 22272:23040]
    qhg = lambda oc: AXb[:, 23040 + oc * 512: 23040 + (oc + 1) * 512]
    xreg = lambda i: AXb[:, 27136 + i * 512: 27136 + (i + 1) * 512]

    xn_bf = lambda kc: AXb[:, kc * 1024:(kc + 1) * 1024]
    h_bf = lambda j: AXb[:, 8192 + j * 1024: 8192 + (j + 1) * 1024]

    wreg = lambda r, kc: AWb[:, r * 8192 + kc * 1024: r * 8192 + (kc + 1) * 1024]
    w1buf = lambda par, kc: AWb[:, par * 2048 + kc * 256:
                                par * 2048 + (kc + 1) * 256]
    w2buf = lambda par, j: AWb[:, 4096 + par * 2816 + j * 128:
                               4096 + par * 2816 + (j + 1) * 128]

    pstage = lambda i: ASb[:, i * 512:(i + 1) * 512]
    pf32 = lambda i: AS[:, i * 512:(i + 1) * 512]
    pbf = lambda i: ASb[:, 2048 + i * 512:2048 + (i + 1) * 512]
    ptbf = lambda i: ASb[:, 3072 + i * 512:3072 + (i + 1) * 512]
    pipeA = AS[:, 2048:2560]
    pipeAb = ASb[:, 4096:4608]
    pipeB = AS[:, 2560:3072]
    stat = lambda i: AS[:, 3584 + i:3584 + i + 1]

    mask_t = lambda v, chunk: sb["c_masks"][:, (2 * v + chunk) * 512:
                                            (2 * v + chunk + 1) * 512]

    hx_in, hx_out = dram["hx_in"].ap(), dram["hx_out"].ap()
    hxs_in, hxs_out = dram["hxs_in"].ap(), dram["hxs_out"].ap()
    rope = ti["rope"].ap()

    exch_i = [0]

    def exchange(wide):
        sem = "hxa" if exch_i[0] % 2 == 0 else "hxb"
        exch_i[0] += 1
        if wide == 257:
            bin_, bout, roff = hx_in, hx_out, "r257"
        else:
            bin_, bout, roff = hxs_in, hxs_out, "r8"
        b.use("sync", ("hx", sem))
        b.wait("sync", "dve", b.cnt["dve"])
        b.wait("sync", "act", b.cnt["act"])
        b.wait("sync", "pe", b.cnt["pe"])
        for dc in range(NDC):
            emit(b, "sync", sem, 16)(
                lambda eng, dc=dc: eng.dma_start(
                    out=bin_[dc * 128:(dc + 1) * 128, :],
                    in_=yt(dc)[:, YW - wide:YW]))
        b.wait("gpsimd", sem, b.cnt[sem])
        cc_c = emit(b, "gpsimd", "cc")(
            lambda eng: eng.collective_compute(
                "AllGather", ALU.bypass, replica_groups=[list(range(NCORES))],
                ins=[bin_.opt()], outs=[bout.opt()]))
        b.wait("sync", "cc", cc_c)
        for dc in range(NDC):
            def f(eng, dc=dc, wide=wide, bout=bout, roff=roff):
                src = bass.AP(bout.tensor, b.regs[roff] + dc * 128 * wide,
                              [[wide, 128], [1, wide]])
                return eng.dma_start(out=yt(dc)[:, HALO - wide:HALO], in_=src)
            emit(b, "sync", sem, 16)(f)
        rb = b.cnt[sem]
        b.wait("vector", sem, rb)
        for dc in range(NDC):
            emit(b, "vector", "dve")(
                lambda eng, dc=dc, wide=wide: eng.tensor_scalar(
                    out=yt(dc)[:, HALO - wide:HALO],
                    in0=yt(dc)[:, HALO - wide:HALO],
                    scalar1=sb["c_hv"][:, 0:1], scalar2=None, op0=ALU.mult))
        b.consumed(("hx", sem), "dve")
        return b.cnt["dve"]

    def phase_barrier(tag):
        for s in ("pe", "dve", "act"):
            b.consumed((tag, s), s, b.cnt[s])

    def phase_wait(tag, engines=("vector", "scalar", "gpsimd", "tensor")):
        for e in engines:
            for s in ("pe", "dve", "act"):
                b.use(e, (tag, s))

    ci = ffn6_call.ci = getattr(ffn6_call, "ci", -1) + 1

    def attn_sub(li):
        b.phase = f"f{ci}L{li}a"
        halo_ready = exchange(257)
        if STOP_AFTER == "fg0" and li == 0:
            for dcq in range(2):
                tap(b, yt(dcq)[:, 0:512])
        phase_wait("phase")
        lsem = "lca" if li % 2 == 0 else "lcb"
        b.use("sync", ("lc_attn", lsem))
        for cn, iname in (("c_qs2", "qs2"), ("c_ks2", "ks2")):
            emit(b, "sync", lsem, 16)(
                lambda eng, cn=cn, iname=iname, li=li: eng.dma_start(
                    out=sb[cn], in_=ti[iname].ap()[:, li * 128:(li + 1) * 128]))
        lc_ok = (lsem, b.cnt[lsem])
        b.wait("tensor", *lc_ok)
        b.use("gpsimd", "aw_regions")
        b.use("sync", "aw_regions_s")
        wq = ti["f_wqkv"].ap()
        for r in range(2):
            for kc in range(NDC):
                emit(b, "gpsimd", "dwa", 16)(
                    lambda eng, r=r, kc=kc: eng.dma_start(
                        out=wreg(r, kc),
                        in_=wq[li * 1024 + kc * 128: li * 1024 + (kc + 1) * 128,
                               r * 1024:(r + 1) * 1024]))
        for kc in range(NDC):
            emit(b, "sync", "dwc", 16)(
                lambda eng, kc=kc: eng.dma_start(
                    out=wreg(2, kc),
                    in_=wq[li * 1024 + kc * 128: li * 1024 + (kc + 1) * 128,
                           2 * 1024:3 * 1024]))
        for kc in range(NDC):
            emit(b, "sync", "dwc", 16)(
                lambda eng, kc=kc: eng.dma_start(
                    out=wo_bf(kc),
                    in_=ti["f_wo"].ap()[li * 1024 + kc * 128:
                                        li * 1024 + (kc + 1) * 128, :]))
        w_ok = b.cnt["dwa"]
        b.wait("tensor", "dwa", w_ok)
        b.wait("tensor", "dwc", b.cnt["dwc"])

        shst = lambda dc, g: AS[:, 3600 + (dc - 4) * 3 + (g - 1):
                                3601 + (dc - 4) * 3 + (g - 1)]
        b.use("vector", ("shst",))
        b.wait("vector", "dve", halo_ready)
        for dc in range(4, 8):
            for gg in range(1, 4):
                emit(b, "vector", "dve")(
                    lambda eng, dc=dc, gg=gg: eng.tensor_copy(
                        out=shst(dc, gg),
                        in_=yt(dc)[:, HALO + gg * 512 - 1:HALO + gg * 512]))
        stash_ok = b.cnt["dve"]

        def qk_pipe(bank, n, out_bf, li, kind, tcol0):
            cs_t, ss_t, sc2 = ((tb_cs, tb_ss, "c_qs2") if kind == "q"
                               else (tb_ci, tb_si, "c_ks2"))
            b.use("scalar", "pipeA")
            b.use("scalar", "pipeB")
            a1 = emit(b, "scalar", "act")(
                lambda eng: eng.activation(out=pipeA[:, 0:n], in_=bank[:, 0:n],
                                           func=AF.Square))
            emit(b, "scalar", "act")(
                lambda eng: eng.copy(out=pipeB[:, 0:n], in_=bank[:, 0:n]))
            b.wait("tensor", "act", a1)
            b.use("tensor", ("bank", 2))
            p1 = emit(b, "tensor", "pe")(
                lambda eng: eng.matmul(PS[2][0:2, 0:n], sb["c_blk2"],
                                       pipeA[:, 0:n], start=True, stop=True))
            b.wait("scalar", "pe", p1)
            a3 = emit(b, "scalar", "act")(
                lambda eng: eng.activation(out=pipeA[0:2, 0:n],
                                           in_=PS[2][0:2, 0:n], func=AF.Sqrt))
            b.consumed(("bank", 2), "act", a3)
            b.wait("vector", "act", a3)
            emit(b, "vector", "dve")(
                lambda eng: eng.tensor_scalar(
                    out=pipeA[0:2, 0:n], in0=pipeA[0:2, 0:n], scalar1=1e-12,
                    scalar2=None, op0=ALU.max))
            d2 = emit(b, "vector", "dve")(
                lambda eng: eng.reciprocal(out=pipeA[0:2, 0:n],
                                           in_=pipeA[0:2, 0:n]))
            b.wait("tensor", "dve", d2)
            b.use("tensor", ("bank", 3))
            p2 = emit(b, "tensor", "pe")(
                lambda eng: eng.matmul(
                    PS[3][:, 0:n], sb[sc2],
                    pipeA[0:2, 0:n], start=True, stop=True))
            b.wait("vector", "pe", p2)
            emit(b, "vector", "dve")(
                lambda eng: eng.tensor_tensor(out=pipeB[:, 0:n],
                                              in0=pipeB[:, 0:n],
                                              in1=PS[3][:, 0:n], op=ALU.mult))
            d4 = emit(b, "vector", "dve")(
                lambda eng: eng.tensor_copy(out=pipeAb[:, 0:n],
                                            in_=pipeB[:, 0:n]))
            b.consumed(("bank", 3), "dve", d4)
            b.wait("tensor", "dve", d4)
            b.use("tensor", ("bank", 4))
            p3 = emit(b, "tensor", "pe")(
                lambda eng: eng.matmul(PS[4][:, 0:n], sb["c_rsign"],
                                       pipeAb[:, 0:n], start=True, stop=True))
            emit(b, "vector", "dve")(
                lambda eng: eng.tensor_tensor(
                    out=pipeB[:, 0:n], in0=pipeAb[:, 0:n],
                    in1=cs_t[:, tcol0:tcol0 + n], op=ALU.mult))
            b.wait("vector", "pe", p3)
            emit(b, "vector", "dve")(
                lambda eng: eng.tensor_tensor(
                    out=pipeA[:, 0:n], in0=PS[4][:, 0:n],
                    in1=ss_t[:, tcol0:tcol0 + n], op=ALU.mult))
            d7 = emit(b, "vector", "dve")(
                lambda eng: eng.tensor_tensor(
                    out=out_bf, in0=pipeB[:, 0:n], in1=pipeA[:, 0:n],
                    op=ALU.add))
            b.consumed(("bank", 4), "dve", d7)
            b.consumed("pipeA", "dve", d7)
            b.consumed("pipeB", "dve", d7)
            return d7

        for g in range(4):
            b.phase = f"f{ci}L{li}set"
            c0 = HALO + g * 512
            tsem = "rta" if g % 2 == 0 else "rtb"
            b.use("sync", ("tbl",))
            for t, dst, tc0, tw in ((0, tb_cs, c0, 512), (1, tb_ss, c0, 512),
                                    (2, tb_ci, 8 + g * 512, 768),
                                    (3, tb_si, 8 + g * 512, 768)):
                emit(b, "sync", tsem, 16)(
                    lambda eng, t=t, dst=dst, tc0=tc0, tw=tw: eng.dma_start(
                        out=dst[:, 0:tw],
                        in_=rope[t * 128:(t + 1) * 128, tc0:tc0 + tw]))
            tbl_ok = b.cnt[tsem]

            if g > 0:
                b.wait("vector", "pe", b.cnt["pe"])
                b.wait("vector", "act", b.cnt["act"])
                for oc in range(NDC):
                    emit(b, "vector", "dve")(
                        lambda eng, oc=oc: eng.tensor_copy(
                            out=kroll(oc)[:, 0:256], in_=kroll(oc)[:, 512:768]))
                for r in range(2):
                    emit(b, "vector", "dve")(
                        lambda eng, r=r: eng.tensor_copy(
                            out=vchunk(r), in_=vchunk(4 + r)))
            roll_ok = b.cnt["dve"]

            passes = []
            if g == 0:
                passes.append(("halo", 8, 256, 0))
            passes.append(("own", c0, 512, 256))

            for pname, ycol0, n, koff in passes:
                b.use("vector", ("xreg",))
                b.wait("vector", "dve", halo_ready)
                for kc in range(NDC):
                    if kc >= 4 and pname == "own" and g > 0:
                        emit(b, "vector", "dve")(
                            lambda eng, kc=kc, g=g: eng.tensor_copy(
                                out=xreg(kc)[:, 0:1], in_=shst(kc, g)))
                        emit(b, "vector", "dve")(
                            lambda eng, kc=kc, ycol0=ycol0, n=n: eng.tensor_copy(
                                out=xreg(kc)[:, 1:n],
                                in_=yt(kc)[:, ycol0:ycol0 + n - 1]))
                    else:
                        emit(b, "vector", "dve")(
                            lambda eng, kc=kc, ycol0=ycol0, n=n: eng.tensor_copy(
                                out=xreg(kc)[:, 0:n], in_=ys(kc, ycol0, n)))
                cast_ok = b.cnt["dve"]
                b.wait("tensor", "dve", cast_ok)
                b.wait("tensor", tsem, tbl_ok)
                b.wait("tensor", "dve", roll_ok)
                kinds = [("k", 1), ("v", 2)] if pname == "halo" else \
                        [("q", 0), ("k", 1), ("v", 2)]
                for kind, r in kinds:
                    for oc in range(NDC):
                        bank = PS[oc % 2]
                        b.use("tensor", ("bank", oc % 2))
                        for kc in range(NDC):
                            emit(b, "tensor", "pe")(
                                lambda eng, bank=bank, r=r, kc=kc, oc=oc, n=n:
                                eng.matmul(
                                    bank[:, 0:n],
                                    wreg(r, kc)[:, oc * 128:(oc + 1) * 128],
                                    xreg(kc)[:, 0:n],
                                    start=(kc == 0), stop=(kc == NDC - 1)))
                        mm_done = b.cnt["pe"]
                        b.wait("scalar", "pe", mm_done)
                        if kind == "q":
                            b.use("vector", ("qhg", oc))
                            qk_pipe(bank, n, qhg(oc), li, "q", 0)
                            b.consumed(("bank", oc % 2), "act")
                            b.also_consumed(("bank", oc % 2), "dve")
                        elif kind == "k":
                            b.use("vector", ("kroll", oc))
                            tcol0 = 0 if pname == "halo" else 256
                            qk_pipe(bank, n,
                                    kroll(oc)[:, koff:koff + n], li, "k", tcol0)
                            b.consumed(("bank", oc % 2), "act")
                            b.also_consumed(("bank", oc % 2), "dve")
                        else:
                            b.use("scalar", "pipeA")
                            a_c = emit(b, "scalar", "act")(
                                lambda eng, bank=bank, n=n: eng.copy(
                                    out=pipeAb[:, 0:n], in_=bank[:, 0:n]))
                            b.consumed(("bank", oc % 2), "act", a_c)
                            b.wait("tensor", "act", a_c)
                            ntc = n // 128
                            b.use("tensor", ("bank", 5))
                            trv = PS[5].bitcast(BF16)
                            for tcl in range(ntc):
                                emit(b, "tensor", "pe")(
                                    lambda eng, trv=trv, tcl=tcl, ntc=ntc:
                                    eng.matmul(
                                        trv[:, tcl * 128:(tcl + 1) * 128],
                                        pipeAb[:, tcl * 128:(tcl + 1) * 128],
                                        sb["c_identb"],
                                        start=(tcl == 0), stop=(tcl == ntc - 1),
                                        is_transpose=True))
                            pc = b.cnt["pe"]
                            b.wait("scalar", "pe", pc)
                            b.use("scalar", ("vchunks",))
                            for tcl in range(ntc):
                                r0 = koff // 128 + tcl
                                emit(b, "scalar", "act")(
                                    lambda eng, trv=trv, tcl=tcl, r0=r0, oc=oc:
                                    eng.copy(
                                        out=vchunk(r0)[:, oc * 128:(oc + 1) * 128],
                                        in_=trv[:, tcl * 128:(tcl + 1) * 128]))
                            b.consumed(("bank", 5), "act")
                            b.consumed("pipeA", "pe", pc)
                qv_done_d, qv_done_a = b.cnt["dve"], b.cnt["act"]
                b.consumed(("xreg",), "pe", b.cnt["pe"])
            b.consumed(("xreg",), "pe", b.cnt["pe"])
            b.consumed(("tbl",), "pe", b.cnt["pe"])
            b.also_consumed(("tbl",), "dve")

            b.phase = f"f{ci}L{li}att"
            b.wait("tensor", "dve", qv_done_d)
            b.wait("tensor", "act", qv_done_a)
            its = [(h, qc) for h in range(HEADS) for qc in range(4)]
            S = {}

            def p_sc(i):
                h, qc = its[i]
                oc, hrow = h // 2, (h % 2) * 64
                w = qc // 2
                first_win = (g == 0 and w == 0)
                sbank = PS[i % 3]
                b.use("tensor", ("bank", i % 3))
                mt = mask_t(0 if first_win else 1, qc % 2)
                emit(b, "tensor", "pe")(
                    lambda eng, sbank=sbank, mt=mt: eng.matmul(
                        sbank[:, 0:512], sb["c_identb"], mt,
                        start=True, stop=False))
                S[i] = {"sbank": sbank}
                S[i]["sc"] = emit(b, "tensor", "pe")(
                    lambda eng, sbank=sbank, h=h, oc=oc, hrow=hrow,
                    qc=qc, w=w: eng.matmul(
                        sbank[:, 0:512],
                        qhg(oc)[hrow:hrow + 64, qc * 128:(qc + 1) * 128],
                        kroll(oc)[hrow:hrow + 64, w * 256:w * 256 + 512],
                        start=False, stop=True))

            def p_mx(i):
                sA = 4 * (i % 3)
                b.wait("vector", "pe", S[i]["sc"])
                b.use("vector", ("stat", i % 3))
                S[i]["mx"] = emit(b, "vector", "dve")(
                    lambda eng, sbank=S[i]["sbank"], sA=sA: eng.reduce_max(
                        out=stat(sA), in_=sbank[:, 0:512], axis=AX.X,
                        negate=True))

            def p_ex(i):
                sA = 4 * (i % 3)
                pfs = pf32(i % 2)
                S[i]["pf"] = pfs
                b.wait("scalar", "dve", S[i]["mx"])
                b.use("scalar", ("pf", i % 2))
                S[i]["ex"] = emit(b, "scalar", "act")(
                    lambda eng, sbank=S[i]["sbank"], pfs=pfs, sA=sA:
                    eng.activation(
                        out=pfs, in_=sbank[:, 0:512], func=AF.Exp,
                        bias=stat(sA), scale=1.0, accum_out=stat(sA + 2)))
                b.consumed(("bank", i % 3), "act", S[i]["ex"])

            def p_rn(i):
                sA = 4 * (i % 3)
                pslot = pbf(i % 2)
                S[i]["pb"] = pslot
                b.wait("vector", "act", S[i]["ex"])
                drc = emit(b, "vector", "dve")(
                    lambda eng, sA=sA: eng.reciprocal(out=stat(sA + 3),
                                                      in_=stat(sA + 2)))
                b.wait("scalar", "dve", drc)
                b.use("scalar", ("pbf", i % 2))
                S[i]["rn"] = emit(b, "scalar", "act")(
                    lambda eng, pslot=pslot, pfs=S[i]["pf"], sA=sA:
                    eng.activation(
                        out=pslot, in_=pfs, func=AF.Copy,
                        scale=stat(sA + 3)))
                b.consumed(("pf", i % 2), "act", S[i]["rn"])
                b.consumed(("stat", i % 3), "act", S[i]["rn"])

            def p_pt(i):
                b.wait("tensor", "act", S[i]["rn"])
                b.use("tensor", ("bank", 4 + i % 2))
                trv = PS[4 + i % 2].bitcast(BF16)
                S[i]["trv"] = trv
                pslot = S[i]["pb"]
                for kcB in range(4):
                    emit(b, "tensor", "pe")(
                        lambda eng, trv=trv, kcB=kcB, pslot=pslot:
                        eng.matmul(
                            trv[:, kcB * 128:(kcB + 1) * 128],
                            pslot[:, kcB * 128:(kcB + 1) * 128],
                            sb["c_identb"],
                            start=(kcB == 0), stop=(kcB == 3),
                            is_transpose=True))
                ptc = b.cnt["pe"]
                b.wait("scalar", "pe", ptc)
                ptslot = ptbf(i % 2)
                S[i]["pt"] = ptslot
                b.use("scalar", ("pt", i % 2))
                S[i]["apt"] = emit(b, "scalar", "act")(
                    lambda eng, trv=trv, ptslot=ptslot: eng.copy(
                        out=ptslot, in_=trv[:, 0:512]))
                b.consumed(("bank", 4 + i % 2), "act", S[i]["apt"])
                b.consumed(("pbf", i % 2), "pe", ptc)

            def p_av(i):
                h, qc = its[i]
                oc, hrow = h // 2, (h % 2) * 64
                w = qc // 2
                avbk = 6 + (h % 2)
                ptslot = S[i]["pt"]
                b.use("tensor", ("bank", avbk))
                b.wait("tensor", "act", S[i]["apt"])
                for kcB in range(4):
                    r = 2 * w + kcB
                    tp = (0, hrow)
                    emit(b, "tensor", "pe")(
                        lambda eng, avbk=avbk, hrow=hrow, r=r, h=h,
                        kcB=kcB, ptslot=ptslot, tp=tp: eng.matmul(
                            PS[avbk][hrow:hrow + 64, 0:128],
                            vchunk(r)[:, h * 64:(h + 1) * 64],
                            ptslot[:, kcB * 128:(kcB + 1) * 128],
                            start=(kcB == 0), stop=(kcB == 3),
                            tile_position=tp))
                avc = b.cnt["pe"]
                b.wait("scalar", "pe", avc)
                b.use("scalar", ("xreg",))
                aav = emit(b, "scalar", "act")(
                    lambda eng, avbk=avbk, hrow=hrow, oc=oc, qc=qc:
                    eng.copy(
                        out=xreg(oc)[hrow:hrow + 64,
                                     qc * 128:(qc + 1) * 128],
                        in_=PS[avbk][hrow:hrow + 64, 0:128]))
                b.consumed(("bank", avbk), "act", aav)
                b.consumed(("pt", i % 2), "pe", avc)
                del S[i]

            stages = (p_sc, p_mx, p_ex, p_rn, p_pt, p_av)
            for r in range(len(its) + len(stages) - 1):
                for lag, fn in enumerate(stages):
                    j = r - lag
                    if 0 <= j < len(its):
                        fn(j)
            if STOP_AFTER == "fg1" and li == 0 and g == 0:
                for hh2 in range(2):
                    tap(b, xreg(hh2 // 2)[:, 0:512])
            if STOP_AFTER == "fg0" and li == 0 and g == 0:
                for ap in (qhg(0), qhg(4), kroll(0)[:, 0:512],
                           kroll(0)[:, 256:768], vchunk(2)[:, 0:512],
                           vchunk(2)[:, 512:1024], vchunk(3)[:, 0:512],
                           vchunk(3)[:, 512:1024]):
                    tap(b, ap)
            av_ok = b.cnt["act"]
            b.phase = f"f{ci}L{li}wo"
            b.wait("tensor", "act", av_ok)
            for oco in range(NDC):
                obk = PS[oco % 2]
                b.use("tensor", ("bank", oco % 2))
                for pair in range(NDC):
                    emit(b, "tensor", "pe")(
                        lambda eng, obk=obk, pair=pair, oco=oco: eng.matmul(
                            obk[:, 0:512],
                            wo_bf(pair)[:, oco * 128:(oco + 1) * 128],
                            xreg(pair)[:, 0:512],
                            start=(pair == 0), stop=(pair == NDC - 1)))
                oc_mm = b.cnt["pe"]
                b.wait("vector", "pe", oc_mm)
                dres = emit(b, "vector", "dve")(
                    lambda eng, obk=obk, oco=oco, c0=c0: eng.tensor_tensor(
                        out=yt(oco)[:, c0:c0 + 512],
                        in0=yt(oco)[:, c0:c0 + 512],
                        in1=obk[:, 0:512], op=ALU.add))
                b.consumed(("bank", oco % 2), "dve", dres)
            b.consumed(("xreg",), "pe", b.cnt["pe"])
        b.consumed("aw_regions", "pe", b.cnt["pe"])
        b.consumed(("lc_attn", lsem), "pe", b.cnt["pe"])
        b.consumed(("shst",), "dve", b.cnt["dve"])

    def ffn_sub(li):
        b.phase = f"f{ci}L{li}f"
        exchange(8)
        phase_wait("phase")
        lsem = "lcc" if li % 2 == 0 else "lcd"
        b.use("sync", ("lc_ffn", lsem))
        for cn, iname, w in (("c_b1", "b1c", 44), ("c_b2", "b2c", 8),
                             ("c_lng", "lngc", 8), ("c_lnb", "lnbc", 8)):
            emit(b, "sync", lsem, 16)(
                lambda eng, cn=cn, iname=iname, w=w, li=li: eng.dma_start(
                    out=sb[cn], in_=ti[iname].ap()[:, li * w:(li + 1) * w]))
        lcf = (lsem, b.cnt[lsem])
        b.wait("vector", *lcf)
        b.wait("scalar", *lcf)
        w1 = ti["f_w1t"].ap()
        w2 = ti["f_w2t"].ap()
        for half in (1, 0):
            hc0 = HALO + half * 1024
            for sst in range(2):
                t0 = half * 1024 + sst * 512

                def cb(dc, sub, sst=sst):
                    emit(b, "vector", "dve")(
                        lambda eng, dc=dc, sub=sub, sst=sst: eng.tensor_scalar(
                            out=xn_bf(dc)[:, sst * 512:(sst + 1) * 512],
                            in0=sub,
                            scalar1=sb["c_lng"][:, dc:dc + 1],
                            scalar2=sb["c_lnb"][:, dc:dc + 1],
                            op0=ALU.mult, op1=ALU.add))
                b.use("vector", ("xn",))
                emit_ln(lambda dc: ys(dc, HALO, T), t0, 512, cb,
                        ("dve", b.cnt["dve"]))
            xn_ok = b.cnt["dve"]
            for j in range(22):
                par = j % 2
                w1q = "gpsimd" if par == 0 else "sync"
                b.use(w1q, ("w1buf", par))
                w1sem = "w1a" if par == 0 else "w1b"
                emit(b, w1q, w1sem, 16)(
                    lambda eng, par=par, j=j, li=li: eng.dma_start(
                        out=AWb[:, par * 2048:(par + 1) * 2048],
                        in_=w1[:, (li * 22 + j) * 2048:
                               (li * 22 + j + 1) * 2048]))
                w1_ok = b.cnt[w1sem]
                b.wait("tensor", w1sem, w1_ok)
                b.wait("tensor", "dve", xn_ok)
                for sst in range(2):
                    bka = PS[2 + (sst % 2) * 2]
                    bkg = PS[3 + (sst % 2) * 2]
                    b.use("tensor", ("bank", 2 + (sst % 2) * 2))
                    b.use("tensor", ("bank", 3 + (sst % 2) * 2))
                    for which, bk in ((0, bka), (1, bkg)):
                        for kc in range(NDC):
                            emit(b, "tensor", "pe")(
                                lambda eng, bk=bk, par=par, kc=kc, which=which,
                                sst=sst: eng.matmul(
                                    bk[:, 0:512],
                                    AWb[:, par * 2048 + kc * 256 + which * 128:
                                        par * 2048 + kc * 256 + which * 128 + 128],
                                    xn_bf(kc)[:, sst * 512:(sst + 1) * 512],
                                    start=(kc == 0), stop=(kc == NDC - 1)))
                    hg_mm = b.cnt["pe"]
                    b.wait("scalar", "pe", hg_mm)
                    b.use("scalar", "pipeA")
                    ag = emit(b, "scalar", "act")(
                        lambda eng, bkg=bkg, li=li, j=j: eng.activation(
                            out=pipeAb[:, 0:512], in_=bkg[:, 0:512],
                            func=AF.Gelu,
                            bias=sb["c_b1"][:, 2 * j + 1: 2 * j + 2]))
                    b.consumed(("bank", 3 + (sst % 2) * 2), "act", ag)
                    b.wait("vector", "act", ag)
                    b.use("vector", ("h", j))
                    dh = emit(b, "vector", "dve")(
                        lambda eng, bka=bka, li=li, j=j, sst=sst:
                        eng.scalar_tensor_tensor(
                            out=h_bf(j)[:, sst * 512:(sst + 1) * 512],
                            in0=bka[:, 0:512],
                            scalar=sb["c_b1"][:, 2 * j: 2 * j + 1],
                            in1=pipeAb[:, 0:512], op0=ALU.add, op1=ALU.mult))
                    b.consumed(("bank", 2 + (sst % 2) * 2), "dve", dh)
                    b.consumed("pipeA", "dve", dh)
                b.consumed(("w1buf", par), "pe", b.cnt["pe"])
            h_ok_d = b.cnt["dve"]
            b.phase = f"f{ci}L{li}w2"
            for oco in range(NDC):
                par = oco % 2
                w2sem = "w2a" if par == 0 else "w2b"
                w2q = "sync" if par == 0 else "gpsimd"
                b.use(w2q, ("w2buf", par))
                emit(b, w2q, w2sem, 16)(
                    lambda eng, par=par, oco=oco, li=li: eng.dma_start(
                        out=AWb[:, 4096 + par * 2816:4096 + par * 2816 + 2816],
                        in_=w2[:, (li * 8 + oco) * 2816:
                               (li * 8 + oco + 1) * 2816]))
                w2_ok = b.cnt[w2sem]
                b.wait("tensor", w2sem, w2_ok)
                b.wait("tensor", "dve", h_ok_d)
                for sst in range(2):
                    obk = PS[6 + (sst % 2)]
                    b.use("tensor", ("bank", 6 + (sst % 2)))
                    for j in range(22):
                        emit(b, "tensor", "pe")(
                            lambda eng, obk=obk, par=par, j=j, sst=sst:
                            eng.matmul(
                                obk[:, 0:512], w2buf(par, j),
                                h_bf(j)[:, sst * 512:(sst + 1) * 512],
                                start=(j == 0), stop=(j == 21)))
                    o_mm = b.cnt["pe"]
                    b.wait("vector", "pe", o_mm)
                    dres = emit(b, "vector", "dve")(
                        lambda eng, obk=obk, oco=oco, li=li, hc0=hc0, sst=sst:
                        eng.scalar_tensor_tensor(
                            out=yt(oco)[:, hc0 + sst * 512:hc0 + (sst + 1) * 512],
                            in0=obk[:, 0:512],
                            scalar=sb["c_b2"][:, oco:oco + 1],
                            in1=yt(oco)[:, hc0 + sst * 512:hc0 + (sst + 1) * 512],
                            op0=ALU.add, op1=ALU.add))
                    b.consumed(("bank", 6 + (sst % 2)), "dve", dres)
                b.consumed(("w2buf", par), "pe", b.cnt["pe"])
            b.consumed(("xn",), "pe", b.cnt["pe"])
            for j in range(22):
                b.consumed(("h", j), "pe", b.cnt["pe"])
        b.consumed(("lc_ffn", lsem), "pe", b.cnt["pe"])
        b.also_consumed(("lc_ffn", lsem), "dve")
        b.also_consumed(("lc_ffn", lsem), "act")

    xs_ap = dram["x_spill"].ap()
    x_rdy = b.cnt["dve"]
    b.wait("sync", "dve", x_rdy)
    for dc in range(NDC):
        emit(b, "sync", "spl", 16)(
            lambda eng, dc=dc: eng.dma_start(
                out=xs_ap[dc * 128:(dc + 1) * 128, :], in_=xr(dc)))
    spill_ok = b.cnt["spl"]

    for dc in range(NDC):
        emit(b, "vector", "dve")(
            lambda eng, dc=dc: eng.memset(yt(dc)[:, 0:HALO], 0.0))
    for st in range(NST):
        def cb(dc, sub, st=st):
            emit(b, "vector", "dve")(
                lambda eng, dc=dc, sub=sub, st=st: eng.tensor_copy(
                    out=yt(dc)[:, HALO + st * ST:HALO + (st + 1) * ST],
                    in_=sub))
        emit_ln(xr, st * ST, ST, cb, ("dve", x_rdy))
    b.wait("tensor", "spl", spill_ok)
    b.wait("vector", "spl", spill_ok)
    b.wait("scalar", "spl", spill_ok)
    b.wait("gpsimd", "spl", spill_ok)

    nsub = [0]

    def maybe_stop():
        nsub[0] += 1
        if (STOP_AFTER == f"ysub:{nsub[0]}"
                or (STOP_AFTER in ("fg0", "fg1") and nsub[0] == 1)):
            b.wait("vector", "act", b.cnt["act"])
            for dc in range(NDC):
                emit(b, "vector", "dve")(
                    lambda eng, dc=dc: eng.tensor_copy(
                        out=xr(dc), in_=yt(dc)[:, HALO:YW]))
            return True
        return False

    phase_barrier("phase")
    for li in range(6):
        attn_sub(li)
        phase_barrier("phase")
        if maybe_stop():
            return
        ffn_sub(li)
        phase_barrier("phase")
        if maybe_stop():
            return

    b.wait("sync", "pe", b.cnt["pe"])
    b.wait("sync", "dve", b.cnt["dve"])
    b.wait("sync", "act", b.cnt["act"])
    for dc in range(NDC):
        emit(b, "sync", "spl", 16)(
            lambda eng, dc=dc: eng.dma_start(
                out=xr(dc), in_=xs_ap[dc * 128:(dc + 1) * 128, :]))
    rst_ok = b.cnt["spl"]
    b.wait("vector", "spl", rst_ok)
    b.wait("vector", "act", b.cnt["act"])
    for dc in range(NDC):
        emit(b, "vector", "dve")(
            lambda eng, dc=dc: eng.tensor_tensor(
                out=xr(dc), in0=xr(dc), in1=yt(dc)[:, HALO:YW], op=ALU.add))





def mamba_block(b, ti, sb, PS, AY, AXa, AW, AS, dram, xr, emit_ln, exchange_x):
    nc = b.nc
    AXb = AXa.bitcast(BF16)
    AYb = AY.bitcast(BF16)
    AWb = AW.bitcast(BF16)
    ASb = AS.bitcast(BF16)
    NCH = 16
    TC = 512

    y0m = lambda dc: AYb[:, dc * T:(dc + 1) * T]
    u_bf = lambda gc: AYb[:, 16384 + gc * TC:16384 + (gc + 1) * TC]
    du_bf = lambda gc: AYb[:, 24576 + gc * TC:24576 + (gc + 1) * TC]
    h_bf = lambda dc: AYb[:, 32768 + dc * TC:32768 + (dc + 1) * TC]

    delta_f = lambda gc: AXa[:, gc * TC:(gc + 1) * TC]

    xin_t = AS[:, 0:516]
    dt_bf = ASb[:, 1152:1664]
    tails = AS[:, 832:880]
    Bc = AS[:, 1024:1536]
    Cc = AS[:, 2048:2560]
    sp_t = AS[:, 2560:3072]
    m_mu = AS[0:1, 3584:4096]
    m_var = AS[0:1, 1536:2048]
    m_sd = AS[0:1, 3072:3584]

    h_halo = ASb[:, 1032:1152]
    xh = AS[:, 880:1024]
    c1_lng = AS[:, 880:888]
    c1_lnb = AS[:, 888:896]
    c1_cw = AS[:, 896:960]
    c1_cb = AS[:, 960:976]
    c1_bdt = AS[:, 976:992]

    wsrc_in = ti["m_win"].ap()
    wsrc_xp = ti["m_wxp"].ap()
    wsrc_dt = ti["m_wdt"].ap()
    wsrc_out = ti["m_wout"].ap()

    w_inx = lambda kc: AWb[:, kc * 2048:(kc + 1) * 2048]
    w_xp = lambda gc: AWb[:, 16384 + gc * 96:16384 + (gc + 1) * 96]
    w_dt = AWb[0:64, 17920:19968]
    w_inz = lambda kc: AWb[:, kc * 2048:(kc + 1) * 2048]
    w_out_t = lambda gc, oc: AXb[:, 16384 + (gc * 8 + oc) * 128:
                                 16384 + (gc * 8 + oc) * 128 + 128]

    xs_ap = dram["x_spill"].ap()
    dl_in, dl_out = dram["a2a_dl_in"].ap(), dram["a2a_dl_out"].ap()
    duio = (dram["a2a_du_in"].ap(), dram["a2a_du_out"].ap())
    bc_in, bc_out = dram["bc_in"].ap(), dram["bc_out"].ap()
    y_in, y_out = dram["a2a_y_in"].ap(), dram["a2a_y_out"].ap()

    hx_ok = exchange_x(xh)

    if MTAP:
        tap(b, xh[:, 0:64])
        tap(b, xh[:, 64:144])

    for st in range(NST):
        def cb(dc, sub, st=st):
            emit(b, "vector", "dve")(
                lambda eng, dc=dc, sub=sub, st=st: eng.tensor_copy(
                    out=y0m(dc)[:, st * ST:(st + 1) * ST], in_=sub))
        emit_ln(xr, st * ST, ST, cb, ("dve", b.cnt["dve"]),
                stats=(m_mu, m_var, m_sd))
    y0m_ok = b.cnt["dve"]

    y0h = ASb[:, 7296:7416]
    b.wait("vector", "dve", hx_ok)

    def xh_t(dc):
        return xh[:, dc * 8:dc * 8 + 64]

    def cbh(dc, sub):
        emit(b, "vector", "dve")(
            lambda eng, dc=dc, sub=sub: eng.tensor_copy(
                out=y0h[:, dc * 8:dc * 8 + 64], in_=sub))
    emit_ln(xh_t, 0, 8, cbh, ("dve", b.cnt["dve"]), stats=(m_mu, m_var, m_sd))


    b.wait("sync", "dve", b.cnt["dve"])
    b.wait("sync", "act", b.cnt["act"])
    b.wait("sync", "pe", b.cnt["pe"])
    for dst, iname in ((c1_lng, "m_lng"), (c1_lnb, "m_lnb"),
                       (c1_cw, "m_convw"), (c1_cb, "m_convb"),
                       (c1_bdt, "m_bdt")):
        emit(b, "sync", "mla", 16)(
            lambda eng, dst=dst, iname=iname: eng.dma_start(
                out=dst, in_=ti[iname].ap()))
    c1_ok = b.cnt["mla"]
    b.wait("vector", "mla", c1_ok)
    b.wait("scalar", "mla", c1_ok)

    if MTAP:
        tap(b, y0h[:, 0:64])

    def y0h_t(dc):
        return y0h[:, dc * 8:dc * 8 + 64]

    def cbh1(dc, sub):
        emit(b, "vector", "dve")(
            lambda eng, dc=dc, sub=sub: eng.tensor_scalar(
                out=h_halo[:, dc * 8:dc * 8 + 64], in0=sub,
                scalar1=c1_lng[:, dc:dc + 1],
                scalar2=c1_lnb[:, dc:dc + 1],
                op0=ALU.mult, op1=ALU.add))
    emit_ln(y0h_t, 0, 8, cbh1, ("dve", b.cnt["dve"]),
            stats=(m_mu, m_var, m_sd), bf_in=True)
    hh_ok = b.cnt["dve"]

    if MTAP:
        tap(b, AS[:, 2560:2568])
        tap(b, AS[:, 3072:3080])
        tap(b, y0h[:, 0:64])
        tap(b, h_halo[:, 0:64])
        tap(b, AS[:, 880:896])

    b.wait("sync", "dve", y0m_ok)
    b.use("sync", "xspill2")
    for dc in range(NDC):
        emit(b, "sync", "spl", 16)(
            lambda eng, dc=dc: eng.dma_start(
                out=xs_ap[dc * 128:(dc + 1) * 128, :], in_=xr(dc)))
    spill_ok = b.cnt["spl"]
    for e in ("tensor", "vector", "scalar", "gpsimd"):
        b.wait(e, "spl", spill_ok)

    b.phase = "m_m1"
    b.use("gpsimd", "aw_regions")
    for kc in range(NDC):
        emit(b, "gpsimd", "mwa", 16)(
            lambda eng, kc=kc: eng.dma_start(
                out=w_inx(kc),
                in_=wsrc_in[kc * 128:(kc + 1) * 128, 0:2048]))
    for gc in range(NCH):
        emit(b, "gpsimd", "mwa", 16)(
            lambda eng, gc=gc: eng.dma_start(
                out=w_xp(gc), in_=wsrc_xp[gc * 128:(gc + 1) * 128, :]))
    emit(b, "gpsimd", "mwa", 16)(
        lambda eng: eng.dma_start(out=w_dt, in_=wsrc_dt))
    mw_ok = b.cnt["mwa"]
    b.wait("tensor", "mwa", mw_ok)

    for c in range(4):
        t0 = c * TC
        def cb1(dc, sub, c=c):
            emit(b, "vector", "dve")(
                lambda eng, dc=dc, sub=sub: eng.tensor_scalar(
                    out=h_bf(dc), in0=sub,
                    scalar1=c1_lng[:, dc:dc + 1],
                    scalar2=c1_lnb[:, dc:dc + 1],
                    op0=ALU.mult, op1=ALU.add))
        b.use("vector", ("mh",))
        emit_ln(y0m, t0, TC, cb1, ("dve", b.cnt["dve"]),
                stats=(m_mu, m_var, m_sd), bf_in=True)
        h_ok = b.cnt["dve"]
        b.wait("tensor", "dve", h_ok)

        b.use("tensor", ("bank", 6))
        for gc in range(NCH):
            bk = PS[gc % 2]
            bh = PS[2 + gc % 2]
            b.use("tensor", ("bank", gc % 2))
            b.use("tensor", ("bank", 2 + gc % 2))
            if c == 0:
                b.wait("tensor", "dve", hh_ok)
                for kc in range(NDC):
                    emit(b, "tensor", "pe")(
                        lambda eng, bh=bh, kc=kc, gc=gc: eng.matmul(
                            bh[:, 0:3],
                            w_inx(kc)[:, gc * 128:(gc + 1) * 128],
                            h_halo[:, kc * 8 + 5:kc * 8 + 8],
                            start=(kc == 0), stop=(kc == NDC - 1)))
            for kc in range(NDC):
                emit(b, "tensor", "pe")(
                    lambda eng, bk=bk, kc=kc, gc=gc, t0=t0: eng.matmul(
                        bk[:, 0:TC],
                        w_inx(kc)[:, gc * 128:(gc + 1) * 128],
                        y0m(kc)[:, t0:t0 + TC] if False else h_bf(kc),
                        start=(kc == 0), stop=(kc == NDC - 1)))
            mm_ok = b.cnt["pe"]
            b.wait("scalar", "pe", mm_ok)
            b.use("scalar", ("xin",))
            if c == 0:
                emit(b, "scalar", "act")(
                    lambda eng, bh=bh: eng.activation(
                        out=xin_t[:, 0:3], in_=bh[:, 0:3],
                        func=AF.Identity, scale=sb["c_hv"][:, 0:1]))
            else:
                b.wait("scalar", "dve", b.cnt["dve"])
                emit(b, "scalar", "act")(
                    lambda eng, gc=gc: eng.copy(out=xin_t[:, 0:3],
                                                in_=tails[:, gc * 3:gc * 3 + 3]))
            xe = emit(b, "scalar", "act")(
                lambda eng, bk=bk: eng.copy(out=xin_t[:, 3:515], in_=bk[:, 0:TC]))
            b.consumed(("bank", gc % 2), "act", xe)
            b.consumed(("bank", 2 + gc % 2), "act", xe)
            b.wait("vector", "act", xe)
            emit(b, "vector", "dve")(
                lambda eng, gc=gc: eng.tensor_copy(
                    out=tails[:, gc * 3:gc * 3 + 3], in_=xin_t[:, 512:515]))
            b.use("vector", ("sp",))
            emit(b, "vector", "dve")(
                lambda eng, gc=gc: eng.tensor_scalar(
                    out=sp_t, in0=xin_t[:, 0:TC],
                    scalar1=c1_cw[:, gc * 4:gc * 4 + 1], scalar2=None,
                    op0=ALU.mult))
            for k in range(1, 4):
                emit(b, "vector", "dve")(
                    lambda eng, gc=gc, k=k: eng.scalar_tensor_tensor(
                        out=sp_t, in0=xin_t[:, k:k + TC],
                        scalar=c1_cw[:, gc * 4 + k:gc * 4 + k + 1],
                        in1=sp_t, op0=ALU.mult, op1=ALU.add))
            cv_ok = b.cnt["dve"]
            b.wait("scalar", "dve", cv_ok)
            b.use("scalar", ("ubf", gc))
            ue = emit(b, "scalar", "act")(
                lambda eng, gc=gc: eng.activation(
                    out=u_bf(gc), in_=sp_t, func=AF.Silu,
                    bias=c1_cb[:, gc:gc + 1]))
            b.consumed(("sp",), "act", ue)
            if MTAP and c == 0 and gc == 0:
                tap(b, xin_t[:, 0:16])
                tap(b, u_bf(0)[:, 0:16])
            b.wait("tensor", "act", ue)
            emit(b, "tensor", "pe")(
                lambda eng, gc=gc: eng.matmul(
                    PS[6][0:96, 0:TC], w_xp(gc), u_bf(gc),
                    start=(gc == 0), stop=(gc == NCH - 1)))
        dbc_ok = b.cnt["pe"]
        b.wait("scalar", "pe", dbc_ok)
        b.use("scalar", ("dtbc",))
        emit(b, "scalar", "act")(
            lambda eng: eng.copy(out=dt_bf[0:64, :], in_=PS[6][0:64, 0:TC]))
        e2 = emit(b, "scalar", "act")(
            lambda eng: eng.copy(out=Bc[0:32, :], in_=PS[6][64:96, 0:TC]))
        b.consumed(("bank", 6), "act", e2)
        sem_m = "msa" if c % 2 == 0 else "msb"
        b.use("sync", ("mst", sem_m))
        b.wait("sync", "act", e2)
        emit(b, "sync", sem_m, 16)(
            lambda eng, c=c: eng.dma_start(
                out=bc_in[0:16, c * TC:(c + 1) * TC], in_=Bc[0:16, :]))
        emit(b, "sync", sem_m, 16)(
            lambda eng, c=c: eng.dma_start(
                out=bc_in[16:32, c * TC:(c + 1) * TC], in_=Bc[16:32, :]))
        b.wait("tensor", "act", e2)
        for gc in range(NCH):
            bk = PS[4 + gc % 2]
            b.use("tensor", ("bank", 4 + gc % 2))
            emit(b, "tensor", "pe")(
                lambda eng, gc=gc: eng.matmul(
                    bk[:, 0:TC] if False else PS[4 + gc % 2][:, 0:TC],
                    w_dt[:, gc * 128:(gc + 1) * 128], dt_bf[0:64, :],
                    start=True, stop=True))
            dmm = b.cnt["pe"]
            b.wait("scalar", "pe", dmm)
            b.use("scalar", ("mdelta", gc))
            emit(b, "scalar", "act")(
                lambda eng, gc=gc: eng.activation(
                    out=delta_f(gc), in_=PS[4 + gc % 2][:, 0:TC],
                    func=AF.Identity, bias=c1_bdt[:, gc:gc + 1]))
            b.consumed(("bank", 4 + gc % 2), "act")
            b.use("scalar", ("sp",))
            emit(b, "scalar", "act")(
                lambda eng, gc=gc: eng.activation(
                    out=sp_t, in_=delta_f(gc), func=AF.Abs))
            emit(b, "scalar", "act")(
                lambda eng: eng.activation(out=sp_t, in_=sp_t, func=AF.Exp,
                                           scale=-1.0))
            sp3 = emit(b, "scalar", "act")(
                lambda eng: eng.activation(out=sp_t, in_=sp_t, func=AF.Ln,
                                           bias=sb["c_cvec"][:, 0:1]))
            b.wait("vector", "act", sp3)
            dl = emit(b, "vector", "dve")(
                lambda eng, gc=gc: eng.scalar_tensor_tensor(
                    out=delta_f(gc), in0=delta_f(gc), scalar=0.0, in1=sp_t,
                    op0=ALU.max, op1=ALU.add))
            b.consumed(("sp",), "dve", dl)
            b.consumed(("mdelta", gc), "dve", dl)
            b.use("vector", ("dubf", gc))
            emit(b, "vector", "dve")(
                lambda eng, gc=gc: eng.tensor_tensor(
                    out=du_bf(gc), in0=delta_f(gc), in1=u_bf(gc), op=ALU.mult))
        dd_ok = b.cnt["dve"]
        b.wait("sync", "dve", dd_ok)
        for gc in range(NCH):
            emit(b, "sync", sem_m, 16)(
                lambda eng, gc=gc, c=c: eng.dma_start(
                    out=dl_in[gc * 128:(gc + 1) * 128, c * TC:(c + 1) * TC],
                    in_=delta_f(gc)))
        sem_u = "mua" if c % 2 == 0 else "mub"
        b.use("gpsimd", ("mstu", sem_u))
        b.wait("gpsimd", "dve", dd_ok)
        for gc in range(NCH):
            emit(b, "gpsimd", sem_u, 16)(
                lambda eng, gc=gc, c=c: eng.dma_start(
                    out=duio[0][gc * 128:(gc + 1) * 128, c * TC:(c + 1) * TC],
                    in_=du_bf(gc)))
        for gc in range(NCH):
            emit(b, "gpsimd", sem_u, 16)(
                lambda eng, gc=gc, c=c: eng.dma_start(
                    out=dram["u_spill"].ap()[gc * 128:(gc + 1) * 128,
                                             c * TC:(c + 1) * TC],
                    in_=u_bf(gc)))
        b.consumed(("mst", sem_m), sem_m, b.cnt[sem_m])
        b.consumed(("mstu", sem_u), sem_u, b.cnt[sem_u])
        b.consumed(("mh",), "pe", b.cnt["pe"])
        for gc in range(NCH):
            b.consumed(("ubf", gc), sem_u, b.cnt[sem_u])
            b.also_consumed(("ubf", gc), "dve", b.cnt["dve"])
            b.also_consumed(("ubf", gc), "pe", b.cnt["pe"])
            b.consumed(("dubf", gc), sem_u, b.cnt[sem_u])
            b.consumed(("mdelta", gc), sem_m, b.cnt[sem_m])
        b.consumed(("xin",), "dve", b.cnt["dve"])
        b.consumed(("dtbc",), sem_m, b.cnt[sem_m])

    b.phase = "m_m2"
    from concourse.aot_env import allow_ici_all_to_all

    def a2a(eng, iin, oout):
        with allow_ici_all_to_all():
            return eng.collective_compute(
                "AllToAll", ALU.bypass, replica_groups=[list(range(NCORES))],
                ins=[iin.opt()], outs=[oout.opt()])

    b.wait("gpsimd", "msa", b.cnt["msa"])
    b.wait("gpsimd", "msb", b.cnt["msb"])
    b.wait("gpsimd", "mua", b.cnt["mua"])
    b.wait("gpsimd", "mub", b.cnt["mub"])
    b.wait("gpsimd", "cc", b.cnt["cc"])
    c1 = emit(b, "gpsimd", "cc")(lambda eng: a2a(eng, dl_in, dl_out))
    b.wait("gpsimd", "cc", c1)
    c2 = emit(b, "gpsimd", "cc")(lambda eng: a2a(eng, duio[0], duio[1]))
    b.wait("gpsimd", "cc", c2)
    c3 = emit(b, "gpsimd", "cc")(
        lambda eng: eng.collective_compute(
            "AllGather", ALU.bypass, replica_groups=[list(range(NCORES))],
            ins=[bc_in.opt()], outs=[bc_out.opt()]))
    b.wait("sync", "cc", c3)

    b.phase = "m_m3"
    m_dl = lambda g, par: AXa[:, par * 2048 + g * TC:
                              par * 2048 + (g + 1) * TC]
    m_du = lambda g, par: AXb[:, 8192 + par * 2048 + g * TC:
                              8192 + par * 2048 + (g + 1) * TC]
    m_bcs = lambda par: AXa[:, 1024 + par * 2048:1536 + par * 2048]
    m_sel = AXb[:, 12288:14336]
    m_y = lambda g: AXa[:, 7168 + g * TC:7168 + (g + 1) * TC]
    m_carry = AXa[:, 1536:1568]
    m_dbu = AS[:, 2048:2560]
    m_s = AS[:, 256:768]
    m_sc = AS[:, 768:1280]
    c_cA = AS[:, 1536:1568]
    _dabufs = (AS[:, 2560:3072], AS[:, 3072:3584],
               AXa[:, 3584:4096], AXa[:, 4608:5120])
    dA_t = lambda k: _dabufs[k]

    b.use("sync", ("m3sel",))
    emit(b, "sync", "mlc", 16)(
        lambda eng: eng.dma_start(out=m_sel[0:16, :], in_=ti["m_sel"].ap()))
    b.wait("sync", "dve", b.cnt["dve"])
    b.wait("sync", "act", b.cnt["act"])
    emit(b, "sync", "mlc", 16)(
        lambda eng: eng.dma_start(out=c_cA, in_=ti["m_alog"].ap()))
    ca_ld = b.cnt["mlc"]
    b.wait("scalar", "mlc", ca_ld)
    emit(b, "scalar", "act")(
        lambda eng: eng.activation(out=c_cA, in_=c_cA, func=AF.Exp))
    b.wait("vector", "act", b.cnt["act"])
    ca_c = emit(b, "vector", "dve")(
        lambda eng: eng.tensor_scalar(out=c_cA, in0=c_cA,
                                      scalar1=-1.0, scalar2=None, op0=ALU.mult))
    b.wait("scalar", "dve", ca_c)
    b.wait("vector", "cc", c2)
    emit(b, "vector", "dve")(
        lambda eng: eng.memset(m_carry, 0.0))
    sel_ok = b.cnt["mlc"]
    b.wait("tensor", "mlc", sel_ok)

    for sch in range(32):
        r, lc, par = sch // 4, (sch % 4) * TC, sch % 2
        lsem2 = "mla" if par == 0 else "mlb"
        b.use("sync", ("m3ld", lsem2))
        b.wait("sync", "cc", c2)
        for g in range(2):
            emit(b, "sync", lsem2, 16)(
                lambda eng, r=r, lc=lc, g=g, par=par: eng.dma_start(
                    out=m_dl(g, par),
                    in_=dl_out[r * 256 + g * 128:r * 256 + (g + 1) * 128,
                               lc:lc + TC]))
        for g in range(2):
            emit(b, "sync", lsem2, 16)(
                lambda eng, r=r, lc=lc, g=g, par=par: eng.dma_start(
                    out=m_du(g, par),
                    in_=duio[1][r * 256 + g * 128:r * 256 + (g + 1) * 128,
                                lc:lc + TC]))
        b.wait("sync", "cc", c3)
        emit(b, "sync", lsem2, 16)(
            lambda eng, r=r, lc=lc, par=par: eng.dma_start(
                out=m_bcs(par)[0:16, :],
                in_=bc_out[r * 32:r * 32 + 16, lc:lc + TC]))
        emit(b, "sync", lsem2, 16)(
            lambda eng, r=r, lc=lc, par=par: eng.dma_start(
                out=m_bcs(par)[32:48, :],
                in_=bc_out[r * 32 + 16:r * 32 + 32, lc:lc + TC]))
        ld_ok = (lsem2, b.cnt[lsem2])
        b.wait("tensor", *ld_ok)
        b.wait("vector", *ld_ok)
        b.wait("scalar", *ld_ok)
        b.use("vector", ("m3y",))
        for g in range(2):
            emit(b, "vector", "dve")(
                lambda eng, g=g: eng.memset(m_y(g), 0.0))
        repB = lambda n: PS[6] if n % 2 == 0 else PS[0]
        repC = lambda n: PS[7] if n % 2 == 0 else PS[1]
        bcbB = ASb[0:16, 2560:3072]
        bcbC = ASb[0:16, 3136:3648]
        rep_at = {}
        da_at = {}

        def head(n):
            if n == 0:
                b.use("vector", ("m3bc",))
                cv = emit(b, "vector", "dve")(
                    lambda eng, par=par: eng.tensor_copy(
                        out=bcbB[:, 0:TC], in_=m_bcs(par)[0:16, :]))
                cv = emit(b, "vector", "dve")(
                    lambda eng, par=par: eng.tensor_copy(
                        out=bcbC[:, 0:TC], in_=m_bcs(par)[32:48, :]))
                b.wait("tensor", "dve", cv)
            bB, bC = repB(n), repC(n)
            b.use("tensor", ("bank", 6 if n % 2 == 0 else 0))
            b.use("tensor", ("bank", 7 if n % 2 == 0 else 1))
            emit(b, "tensor", "pe")(
                lambda eng, n=n, bB=bB: eng.matmul(
                    bB[:, 0:TC], m_sel[0:16, n * 128:(n + 1) * 128],
                    bcbB[:, 0:TC], start=True, stop=True))
            rep_at[n] = emit(b, "tensor", "pe")(
                lambda eng, n=n, bC=bC: eng.matmul(
                    bC[:, 0:TC], m_sel[0:16, n * 128:(n + 1) * 128],
                    bcbC[:, 0:TC], start=True, stop=True))
            for g in range(2):
                gn = g * 16 + n
                slot = (n % 2) * 2 + g
                abk = dA_t(slot)
                b.use("scalar", ("m3da", slot))
                da_at[(n, g)] = emit(b, "scalar", "act")(
                    lambda eng, g=g, gn=gn, abk=abk, par=par: eng.activation(
                        out=abk[:, 0:TC], in_=m_dl(g, par), func=AF.Exp,
                        scale=c_cA[:, gn:gn + 1]))

        def tail(n):
            bB, bC = repB(n), repC(n)
            for g in range(2):
                gn = g * 16 + n
                slot = (n % 2) * 2 + g
                abk = dA_t(slot)
                b.wait("vector", "pe", rep_at[n])
                b.use("vector", ("m3t",))
                emit(b, "vector", "dve")(
                    lambda eng, g=g, par=par, bB=bB: eng.tensor_tensor(
                        out=m_dbu, in0=m_du(g, par), in1=bB[:, 0:TC],
                        op=ALU.mult))
                b.wait("vector", "act", da_at[(n, g)])
                sc_i = emit(b, "vector", "dve")(
                    lambda eng, abk=abk, gn=gn: eng.tensor_tensor_scan(
                        out=m_s, data0=abk[:, 0:TC], data1=m_dbu,
                        initial=m_carry[:, gn:gn + 1],
                        op0=ALU.mult, op1=ALU.add))
                b.consumed(("m3da", slot), "dve", sc_i)
                b.wait("scalar", "dve", sc_i)
                cw_i = emit(b, "scalar", "act")(
                    lambda eng, gn=gn: eng.copy(
                        out=m_carry[:, gn:gn + 1], in_=m_s[:, TC - 1:TC]))
                b.wait("vector", "act", cw_i)
                emit(b, "vector", "dve")(
                    lambda eng, bC=bC: eng.tensor_tensor(
                        out=m_sc, in0=m_s, in1=bC[:, 0:TC], op=ALU.mult))
                ya = emit(b, "vector", "dve")(
                    lambda eng, g=g: eng.tensor_tensor(
                        out=m_y(g), in0=m_y(g), in1=m_sc, op=ALU.add))
            b.consumed(("bank", 6 if n % 2 == 0 else 0), "dve", b.cnt["dve"])
            b.consumed(("bank", 7 if n % 2 == 0 else 1), "dve", b.cnt["dve"])

        for n in range(16):
            head(n)
            if n >= 1:
                tail(n - 1)
        tail(15)
        ysem = "msa" if par == 0 else "msb"
        dma_y = b.cnt["dve"]
        b.wait("sync", "dve", dma_y)
        b.use("sync", ("m3ys", ysem))
        for g in range(2):
            emit(b, "sync", ysem, 16)(
                lambda eng, r=r, lc=lc, g=g: eng.dma_start(
                    out=y_in[r * 256 + g * 128:r * 256 + (g + 1) * 128,
                             lc:lc + TC],
                    in_=m_y(g)))
        b.consumed(("m3ys", ysem), ysem, b.cnt[ysem])
        b.consumed(("m3ld", lsem2), "dve", b.cnt["dve"])
        b.also_consumed(("m3ld", lsem2), "act", b.cnt["act"])
        b.consumed(("m3y",), ysem, b.cnt[ysem])
        b.consumed(("m3bc",), "pe", b.cnt["pe"])
        b.consumed(("m3t",), "dve", b.cnt["dve"])
    b.consumed(("m3sel",), "pe", b.cnt["pe"])

    b.wait("gpsimd", "mla", b.cnt["mla"])
    b.wait("gpsimd", "mlb", b.cnt["mlb"])
    b.wait("gpsimd", "msa", b.cnt["msa"])
    b.wait("gpsimd", "msb", b.cnt["msb"])
    b.wait("gpsimd", "cc", b.cnt["cc"])
    c4 = emit(b, "gpsimd", "cc")(lambda eng: a2a(eng, y_in, y_out))

    b.phase = "m_m5"
    b.use("gpsimd", "aw_regions")
    b.wait("gpsimd", "pe", b.cnt["pe"])
    for kc in range(NDC):
        emit(b, "gpsimd", "mwb", 16)(
            lambda eng, kc=kc: eng.dma_start(
                out=w_inz(kc),
                in_=wsrc_in[kc * 128:(kc + 1) * 128, 2048:4096]))
    for gc in range(NCH):
        for oc in range(NDC):
            emit(b, "gpsimd", "mwb", 16)(
                lambda eng, gc=gc, oc=oc: eng.dma_start(
                    out=w_out_t(gc, oc),
                    in_=wsrc_out[gc * 128:(gc + 1) * 128,
                                 oc * 128:(oc + 1) * 128]))
    mw2_ok = b.cnt["mwb"]
    b.wait("tensor", "mwb", mw2_ok)

    b.wait("sync", "pe", b.cnt["pe"])
    b.wait("sync", "dve", b.cnt["dve"])
    b.wait("sync", "act", b.cnt["act"])
    for s in ("msa", "msb", "mla", "mlb", "mwa", "mua", "mub"):
        b.wait("sync", s, b.cnt[s])
    for dc in range(4):
        emit(b, "sync", "spl", 16)(
            lambda eng, dc=dc: eng.dma_start(
                out=xr(dc), in_=xs_ap[dc * 128:(dc + 1) * 128, :]))
    rst_ok = b.cnt["spl"]
    b.wait("vector", "spl", rst_ok)
    b.wait("tensor", "spl", rst_ok)
    b.wait("scalar", "spl", rst_ok)

    for e in ("tensor", "vector", "scalar", "sync", "gpsimd"):
        b.wait(e, "dve", b.cnt["dve"])
        b.wait(e, "pe", b.cnt["pe"])
    park = lambda dc: AYb[:, 16384 + (dc - 4) * T:16384 + (dc - 3) * T]
    y3b = lambda gc: AYb[:, 24576 + gc * TC:24576 + (gc + 1) * TC]
    y2l = AS[:, 0:512]
    y2l2 = AS[:, 512:1024]
    zs_t = AS[:, 1280:1792]
    ub5a = ASb[:, 2048:2560]
    ub5b = ASb[:, 4096:4608]
    c5_dp = AS[:, 2304:2320]
    c5_lng = AS[:, 2320:2328]
    c5_lnb = AS[:, 2328:2336]

    def ub5(gc, c):
        return ub5a if gc % 2 == 0 else ub5b

    for dst, iname in ((c5_dp, "m_dparam"), (c5_lng, "m_lng"),
                       (c5_lnb, "m_lnb")):
        emit(b, "sync", "mlc", 16)(
            lambda eng, dst=dst, iname=iname: eng.dma_start(
                out=dst, in_=ti[iname].ap()))
    c5_ok = b.cnt["mlc"]
    b.wait("vector", "mlc", c5_ok)
    b.wait("scalar", "mlc", c5_ok)

    for c in range(4):
        t0 = c * TC
        def cb6(dc, sub, c=c):
            emit(b, "vector", "dve")(
                lambda eng, dc=dc, sub=sub: eng.tensor_scalar(
                    out=h_bf(dc), in0=sub,
                    scalar1=c5_lng[:, dc:dc + 1],
                    scalar2=c5_lnb[:, dc:dc + 1],
                    op0=ALU.mult, op1=ALU.add))
        b.use("vector", ("mh",))
        emit_ln(y0m, t0, TC, cb6, ("dve", b.cnt["dve"]),
                stats=(m_mu, m_var, m_sd), bf_in=True)
        h5_ok = b.cnt["dve"]
        b.wait("tensor", "dve", h5_ok)
        b.wait("sync", "cc", c4)
        for gc in range(NCH):
            bk = PS[gc % 2]
            b.use("tensor", ("bank", gc % 2))
            for kc in range(NDC):
                emit(b, "tensor", "pe")(
                    lambda eng, bk=bk, kc=kc, gc=gc: eng.matmul(
                        bk[:, 0:TC],
                        w_inz(kc)[:, gc * 128:(gc + 1) * 128], h_bf(kc),
                        start=(kc == 0), stop=(kc == NDC - 1)))
            zmm = b.cnt["pe"]
            ybuf = y2l if gc % 2 == 0 else y2l2
            sem5 = "msa" if gc % 2 == 0 else "msb"
            b.use("sync", ("y2l", gc % 2))
            emit(b, "sync", sem5, 16)(
                lambda eng, gc=gc, c=c, ybuf=ybuf: eng.dma_start(
                    out=ybuf, in_=y_out[gc * 128:(gc + 1) * 128,
                                        c * TC:(c + 1) * TC]))
            emit(b, "sync", sem5, 16)(
                lambda eng, gc=gc, c=c: eng.dma_start(
                    out=ub5(gc, c),
                    in_=dram["u_spill"].ap()[gc * 128:(gc + 1) * 128,
                                             c * TC:(c + 1) * TC]))
            yld = (sem5, b.cnt[sem5])
            b.wait("scalar", "pe", zmm)
            b.use("scalar", ("zst",))
            zs = emit(b, "scalar", "act")(
                lambda eng, bk=bk: eng.activation(out=zs_t, in_=bk[:, 0:TC],
                                                  func=AF.Silu))
            b.consumed(("bank", gc % 2), "act", zs)
            b.wait("vector", "act", zs)
            b.wait("vector", *yld)
            b.use("vector", ("y3", gc))
            emit(b, "vector", "dve")(
                lambda eng, gc=gc, ybuf=ybuf, c=c: eng.scalar_tensor_tensor(
                    out=ybuf, in0=ub5(gc, c), scalar=c5_dp[:, gc:gc + 1],
                    in1=ybuf, op0=ALU.mult, op1=ALU.add))
            y3c = emit(b, "vector", "dve")(
                lambda eng, gc=gc, ybuf=ybuf: eng.tensor_tensor(
                    out=y3b(gc), in0=ybuf, in1=zs_t, op=ALU.mult))
            b.consumed(("y2l", gc % 2), "dve", y3c)
            b.consumed(("zst",), "dve", y3c)
        y3_ok = b.cnt["dve"]
        b.wait("tensor", "dve", y3_ok)
        for oc in range(NDC):
            bk = PS[2 + oc % 2]
            b.use("tensor", ("bank", 2 + oc % 2))
            for gc in range(NCH):
                emit(b, "tensor", "pe")(
                    lambda eng, bk=bk, gc=gc, oc=oc: eng.matmul(
                        bk[:, 0:TC], w_out_t(gc, oc), y3b(gc),
                        start=(gc == 0), stop=(gc == NCH - 1)))
            omm = b.cnt["pe"]
            b.wait("vector", "pe", omm)
            if oc < 4:
                emit(b, "vector", "dve")(
                    lambda eng, bk=bk, oc=oc, t0=t0: eng.tensor_tensor(
                        out=xr(oc)[:, t0:t0 + TC], in0=xr(oc)[:, t0:t0 + TC],
                        in1=bk[:, 0:TC], op=ALU.add))
                xad = emit(b, "vector", "dve")(
                    lambda eng, oc=oc, t0=t0: eng.tensor_tensor(
                        out=xr(oc)[:, t0:t0 + TC], in0=xr(oc)[:, t0:t0 + TC],
                        in1=y0m(oc)[:, t0:t0 + TC], op=ALU.add))
            else:
                xad = emit(b, "vector", "dve")(
                    lambda eng, bk=bk, oc=oc, t0=t0: eng.tensor_tensor(
                        out=park(oc)[:, t0:t0 + TC], in0=bk[:, 0:TC],
                        in1=y0m(oc)[:, t0:t0 + TC], op=ALU.add))
            b.consumed(("bank", 2 + oc % 2), "dve", xad)
        for gc in range(NCH):
            b.consumed(("y3", gc), "pe", b.cnt["pe"])
        b.consumed(("mh",), "pe", b.cnt["pe"])
    b.consumed("aw_regions", "pe", b.cnt["pe"])

    b.wait("sync", "pe", b.cnt["pe"])
    b.wait("sync", "dve", b.cnt["dve"])
    for dc in range(4, NDC):
        emit(b, "sync", "spl", 16)(
            lambda eng, dc=dc: eng.dma_start(
                out=xr(dc), in_=xs_ap[dc * 128:(dc + 1) * 128, :]))
    rst2 = b.cnt["spl"]
    b.wait("vector", "spl", rst2)
    for dc in range(4, NDC):
        emit(b, "vector", "dve")(
            lambda eng, dc=dc: eng.tensor_tensor(
                out=xr(dc), in0=xr(dc), in1=park(dc), op=ALU.add))
    b.wait("tensor", "spl", rst2)
    b.wait("scalar", "spl", rst2)



def _ffn_host_prep(inputs):
    bf = ml_dtypes.bfloat16
    W1 = np.asarray(inputs["L0_ffn_W1"])
    W2 = np.asarray(inputs["L0_ffn_W2"])
    b1 = np.asarray(inputs["L0_ffn_b1"])
    b2 = np.asarray(inputs["L0_ffn_b2"])
    lng = np.asarray(inputs["L0_ffn_ln_g"])
    lnb = np.asarray(inputs["L0_ffn_ln_b"])
    qs = np.asarray(inputs["L0_ffn_q_scale"])
    ksc = np.asarray(inputs["L0_ffn_k_scale"])

    a = np.zeros((6, 1024, 2816), np.float32)
    g = np.zeros((6, 1024, 2816), np.float32)
    a[:, :, :2730] = W1[:, :, :2730]
    g[:, :, :2730] = W1[:, :, 2730:]
    f_w1p = np.concatenate([a, g], axis=2).reshape(6 * 1024, 5632)

    w2p = np.zeros((6, 2816, 1024), np.float32)
    w2p[:, :2730] = W2
    f_w2p = w2p.reshape(6 * 2816, 1024)

    b1a = np.zeros((6, 2816), np.float32)
    b1g = np.zeros((6, 2816), np.float32)
    b1a[:, :2730] = b1[:, :2730]
    b1g[:, :2730] = b1[:, 2730:]
    b1c = np.zeros((128, 6 * 44), np.float32)
    for li in range(6):
        for j in range(22):
            b1c[:, li * 44 + 2 * j] = b1a[li, j * 128:(j + 1) * 128]
            b1c[:, li * 44 + 2 * j + 1] = b1g[li, j * 128:(j + 1) * 128]

    def cols8(v):
        out = np.zeros((128, 48), np.float32)
        for li in range(6):
            for dc in range(8):
                out[:, li * 8 + dc] = v[li, dc * 128:(dc + 1) * 128]
        return out

    b2c, lngc, lnbc = cols8(b2), cols8(lng), cols8(lnb)

    qs2 = np.zeros((2, 6 * 128), np.float32)
    ks2 = np.zeros((2, 6 * 128), np.float32)
    for li in range(6):
        qs2[0, li * 128:li * 128 + 64] = qs[li]
        qs2[1, li * 128 + 64:li * 128 + 128] = qs[li]
        ks2[0, li * 128:li * 128 + 64] = ksc[li]
        ks2[1, li * 128 + 64:li * 128 + 128] = ksc[li]

    r64 = np.zeros((64, 64), np.float32)
    for q in range(32):
        r64[q + 32, q] = -1.0
        r64[q, q + 32] = 1.0
    rsign = np.zeros((128, 128), np.float32)
    rsign[0:64, 0:64] = r64
    rsign[64:128, 64:128] = r64

    NEGM = -3.0e36
    masks = np.zeros((4, 128, 512), np.float32)
    for v in range(2):
        for chunk in range(2):
            for il in range(128):
                i_loc = chunk * 128 + il
                jl = np.arange(512)
                keep = (jl >= i_loc) & (jl <= i_loc + 256)
                if v == 0:
                    keep &= (jl >= 256)
                masks[2 * v + chunk, il] = np.where(keep, 0.0, NEGM)
    masks = masks.reshape(4 * 128, 512).astype(bf)

    sel = np.zeros((16, 2048), np.float32)
    for n in range(16):
        sel[n, n * 128:(n + 1) * 128] = 1.0

    cw = np.asarray(inputs["L0_attn_b_conv_w"])[:, 0, :]
    m_convw = np.zeros((128, 64), np.float32)
    for gc in range(16):
        for k in range(4):
            m_convw[:, gc * 4 + k] = cw[gc * 128:(gc + 1) * 128, k]

    def cols16(v):
        out = np.zeros((128, 16), np.float32)
        for gc in range(16):
            out[:, gc] = v[gc * 128:(gc + 1) * 128]
        return out

    def cols8v(v):
        out = np.zeros((128, 8), np.float32)
        for dc in range(8):
            out[:, dc] = v[dc * 128:(dc + 1) * 128]
        return out

    mamba = {
        "m_win": np.ascontiguousarray(np.asarray(inputs["L0_attn_b_W_in"])),
        "m_wxp": np.ascontiguousarray(np.asarray(inputs["L0_attn_b_W_xproj"])),
        "m_wdt": np.ascontiguousarray(np.asarray(inputs["L0_attn_b_W_dt"])),
        "m_wout": np.ascontiguousarray(np.asarray(inputs["L0_attn_b_W_out"])),
        "m_dparam": cols16(np.asarray(inputs["L0_attn_b_D_param"])),
        "m_lng": cols8v(np.asarray(inputs["L0_attn_b_ln_g"])),
        "m_lnb": cols8v(np.asarray(inputs["L0_attn_b_ln_b"])),
        "m_convw": m_convw,
        "m_convb": cols16(np.asarray(inputs["L0_attn_b_conv_b"])),
        "m_bdt": cols16(np.asarray(inputs["L0_attn_b_b_dt"])),
        "m_sel": sel.astype(bf),
        "ones_col_bf": np.ones((128, 1)).astype(bf),
    }

    for k in ("m_win", "m_wxp", "m_wdt", "m_wout"):
        mamba[k] = mamba[k].astype(bf) if mamba[k].dtype != bf else mamba[k]
    return {
        **mamba,
        "f_wqkv": np.ascontiguousarray(
            np.asarray(inputs["L0_ffn_Wqkv"]).reshape(6 * 1024, 3072)
            .astype(bf)),
        "f_wo": np.ascontiguousarray(
            np.asarray(inputs["L0_ffn_Wo"]).reshape(6 * 1024, 1024)
            .astype(bf)),
        "f_w1t": np.ascontiguousarray(
            f_w1p.astype(bf).reshape(6, 8, 128, 2, 22, 128)
            .transpose(2, 0, 4, 1, 3, 5).reshape(128, 6 * 22 * 2048)),
        "f_w2t": np.ascontiguousarray(
            f_w2p.astype(bf).reshape(6, 22, 128, 8, 128)
            .transpose(2, 0, 3, 1, 4).reshape(128, 6 * 8 * 2816)),
        "b1c": b1c, "b2c": b2c, "lngc": lngc, "lnbc": lnbc,
        "qs2": qs2, "ks2": ks2,
        "rsign": rsign.astype(bf),
        "masks": masks,
    }


def _rope_tables(g):
    bf = ml_dtypes.bfloat16
    n = 16384
    half = np.arange(0, 64, 2, dtype=np.float64)
    inv_freq = 1.0 / 10000.0 ** (half / 64.0)
    base_scale = (half + 0.4 * 64.0) / (1.4 * 64.0)
    cols = np.arange(YW, dtype=np.float64)
    t = g * T + cols - HALO
    freqs = t[:, None] * inv_freq[None, :]
    freqs = np.concatenate([freqs, freqs], axis=1)
    power = (t - n // 2) / (256 // 2)
    scale = base_scale[None, :] ** power[:, None]
    scale = np.concatenate([scale, scale], axis=1)
    cs = np.cos(freqs) * scale * 8.0
    ss = np.sin(freqs) * scale * 8.0
    ci = np.cos(freqs) / scale
    si = np.sin(freqs) / scale
    out = np.zeros((4, 128, YW), np.float64)
    for ti, tab in enumerate((cs, ss, ci, si)):
        out[ti, 0:64] = tab.T
        out[ti, 64:128] = tab.T
    return np.ascontiguousarray(out.reshape(4 * 128, YW).astype(bf))


def _alog_slice(inputs, g):
    al = np.asarray(inputs["L0_attn_b_A_log"])
    out = np.zeros((128, 32), np.float32)
    for gcl in range(2):
        ch0 = g * 256 + gcl * 128
        out[:, gcl * 16:(gcl + 1) * 16] = al[ch0:ch0 + 128, :]
    return out


_PROG_CACHE = {}


def build_in_maps(inputs):
    x = np.asarray(inputs["L0_attn_a_x"])
    Wqkv_a = np.asarray(inputs["L0_attn_a_W_qkv"])
    ident_f = np.eye(128, dtype=np.float32)
    ident_b = np.eye(128).astype(ml_dtypes.bfloat16)
    ones_col = np.ones((128, 1), np.float32)
    ones_row = np.ones((1, 128), np.float32)
    blk2 = np.zeros((128, 2), np.float32)
    blk2[0:64, 0] = 1.0
    blk2[64:128, 1] = 1.0
    blk2T = np.ascontiguousarray(blk2.T)
    cvec = np.zeros((128, 16), np.float32)
    cvec[:, 0] = 1.0
    cvec[:, 1] = 1e-5
    aw_qkvT = np.ascontiguousarray(Wqkv_a.T.astype(ml_dtypes.bfloat16))

    shared = _ffn_host_prep(inputs)
    xs = x[0]
    in_maps = []
    for g in range(NCORES):
        m = {
            "x_own": np.ascontiguousarray(xs[g * T:(g + 1) * T]),
            "aw_qkvT": aw_qkvT,
            "ident_f": ident_f,
            "ident_b": ident_b,
            "ones_col": ones_col,
            "ones_row": ones_row,
            "blk2": blk2,
            "blk2T": blk2T,
            "cvec": cvec,
            "rope": _rope_tables(g),
            "hvalid": np.full((128, 1), 0.0 if g == 0 else 1.0, np.float32),
            "m_alog": _alog_slice(inputs, g),
        }
        m.update(shared)
        if g > 0:
            mk = shared["masks"].copy()
            mk[0:256] = mk[256:512]
            m["masks"] = mk
        in_maps.append(m)
    return in_maps


def kernel(**inputs):
    key = ("v1", STOP_AFTER)
    if key not in _PROG_CACHE:
        _PROG_CACHE[key] = build_program()
    nc = _PROG_CACHE[key]
    in_maps = build_in_maps(inputs)
    res = run_bass_kernel_spmd(nc, in_maps, core_ids=list(range(NCORES)))
    out = np.concatenate([res.results[g]["out"] for g in range(NCORES)],
                         axis=0)
    global LAST_DBG
    LAST_DBG = res.results[0].get("dbg")
    return out.reshape(1, 16384, 1024)
